# version 37
# baseline (speedup 1.0000x reference)
import math
import numpy as np
import concourse.bass as bass
import concourse.mybir as mybir
from concourse.bass_utils import run_bass_kernel_spmd

F32 = mybir.dt.float32
BF16 = mybir.dt.bfloat16
AF = mybir.ActivationFunctionType
ALU = mybir.AluOpType
AX = mybir.AxisListType

NCORES = 8
BL = 2
T = 2048
D = 1024
EC = 8
H = 16
DH = 64
HFF = 4096
LN_EPS = 1e-5


class Buf:
    __slots__ = ("space", "start", "end", "name", "overlaps")

    def __init__(self, space, start, end, name):
        self.space, self.start, self.end, self.name = space, start, end, name
        self.overlaps = []


class _Op:
    __slots__ = ("eng", "emit", "reads", "writes", "seq", "kind", "queue",
                 "inc_needed", "inc_val", "dma_val", "waits", "tag", "skey")

    def __init__(self, eng, emit, reads, writes, seq, kind, queue, tag):
        self.eng, self.emit = eng, emit
        self.reads, self.writes = reads, writes
        self.seq, self.kind, self.queue = seq, kind, queue
        self.inc_needed = False
        self.inc_val = None
        self.dma_val = None
        self.waits = []
        self.tag = tag


ENGINES = ("sync", "vector", "scalar", "tensor", "gpsimd")
DRAIN_ENGINES = ("vector", "scalar", "gpsimd")


class Prog:
    def __init__(self):
        self.ops = []
        self.state = {}
        self._bufs = []

    def buf(self, space, start, end, name):
        b = Buf(space, start, end, name)
        for o in self._bufs:
            if o.space == space and not (end <= o.start or start >= o.end):
                o.overlaps.append(b)
                b.overlaps.append(o)
        self._bufs.append(b)
        return b

    def _touch(self, op):
        def dep(producer):
            if producer is None or producer is op:
                return
            op.waits.append(producer)
        for r in op.reads:
            for b in (r, *r.overlaps):
                st = self.state.get(b)
                if st:
                    dep(st[0])
        for w in op.writes:
            for b in (w, *w.overlaps):
                st = self.state.get(b)
                if st:
                    dep(st[0])
                    for rd in st[1]:
                        dep(rd)
        for r in op.reads:
            st = self.state.get(r)
            if st is None:
                self.state[r] = (None, [op])
            else:
                st[1].append(op)
        for w in op.writes:
            self.state[w] = (op, [])

    def op(self, eng, emit, reads=(), writes=(), tag=""):
        o = _Op(eng, emit, tuple(reads), tuple(writes), len(self.ops), "op", None, tag)
        self._touch(o)
        self.ops.append(o)
        return o

    def dma(self, queue, emit, reads=(), writes=(), tag=""):
        o = _Op(queue, emit, tuple(reads), tuple(writes), len(self.ops), "dma", queue, tag)
        skey = None
        for b in (*o.writes, *o.reads):
            if b.space == "sbuf":
                skey = b
                break
        if skey is None:
            skey = (o.writes + o.reads)[0]
        o.skey = (skey, "sw" if queue == "gpsimd" else "hw")
        self._touch(o)
        self.ops.append(o)
        return o

    def build(self, nc, ctx):
        for o in self.ops:
            for p in o.waits:
                if p.kind == "dma":
                    continue
                if p.eng == o.eng:
                    if p.eng == "tensor" and o.kind == "op":
                        continue
                    continue
                p.inc_needed = True
        eng_cnt = {e: 0 for e in ENGINES}
        skey_cnt = {}
        dma_by_queue = {e: [] for e in ENGINES}
        for o in self.ops:
            if o.kind == "dma":
                skey_cnt[o.skey] = skey_cnt.get(o.skey, 0) + 16
                o.dma_val = skey_cnt[o.skey]
                dma_by_queue[o.queue].append(o)
            elif o.inc_needed:
                eng_cnt[o.eng] += 1
                o.inc_val = eng_cnt[o.eng]
        self.final_dma = dict(skey_cnt)

        sems = {}
        for e in ENGINES:
            if eng_cnt[e]:
                sems[("eng", e)] = nc.alloc_semaphore(f"s_{e}")
        for sk in skey_cnt:
            sems[("dma", sk)] = nc.alloc_semaphore(f"d_{sk[0].name}_{sk[1]}")
        self.sems = sems

        streams = {e: [] for e in ENGINES}
        hw = {e: {} for e in ENGINES}
        drained = {e: -1 for e in ENGINES}
        lastseq = {e: -1 for e in ENGINES}

        for o in self.ops:
            needs_drain = False
            waitlist = {}
            for p in o.waits:
                if p.kind == "dma":
                    key = ("dma", p.skey)
                    waitlist[key] = max(waitlist.get(key, 0), p.dma_val)
                elif p.eng == o.eng:
                    if p.eng == "tensor" and o.kind == "op":
                        continue
                    if p.seq > drained[o.eng]:
                        needs_drain = True
                else:
                    key = ("eng", p.eng)
                    if p.inc_val is None:
                        raise RuntimeError(f"producer without inc: {p.tag} -> {o.tag}")
                    waitlist[key] = max(waitlist.get(key, 0), p.inc_val)
            if needs_drain:
                drained[o.eng] = lastseq[o.eng]
            ww = []
            for key, val in waitlist.items():
                if hw[o.eng].get(key, 0) < val:
                    hw[o.eng][key] = val
                    ww.append((key, val))
            streams[o.eng].append((o, needs_drain, ww))
            lastseq[o.eng] = o.seq

        def replay(ename, eng):
            for (o, needs_drain, ww) in streams[ename]:
                if needs_drain:
                    eng.drain()
                for key, val in ww:
                    eng.wait_ge(sems[key], val)
                ins = o.emit(eng)
                if o.kind == "dma":
                    ins.then_inc(sems[("dma", o.skey)], 16)
                elif o.inc_needed:
                    ins.then_inc(sems[("eng", o.eng)], 1)
            finals = {}
            for o in dma_by_queue[ename]:
                finals[o.skey] = skey_cnt[o.skey]
            for sk, v in finals.items():
                eng.wait_ge(sems[("dma", sk)], v)

        with nc.Block() as block:
            if streams["sync"]:
                block.sync(lambda eng: replay("sync", eng))
            if streams["vector"]:
                block.vector(lambda eng: replay("vector", eng))
            if streams["scalar"]:
                block.scalar(lambda eng: replay("scalar", eng))
            if streams["tensor"]:
                block.tensor(lambda eng: replay("tensor", eng))
            if streams["gpsimd"]:
                block.gpsimd(lambda eng: replay("gpsimd", eng))



SBUF_BASE = 18432


class Arena:
    def __init__(self, nc, prog, limit_bytes):
        self.nc = nc
        self.prog = prog
        self.limit = limit_bytes
        self.off = SBUF_BASE
        self.n = 0

    def alloc(self, name, parts, cols, dt):
        esz = 4 if dt == F32 else 2
        nbytes = cols * esz
        nbytes_al = (nbytes + 31) & ~31
        t = self.nc.alloc_sbuf_tensor_at(name, [parts, cols], dt, offset=self.off)
        b = self.prog.buf("sbuf", self.off, self.off + nbytes_al, name)
        self.off += nbytes_al
        if self.off > self.limit:
            raise RuntimeError(f"SBUF arena overflow at {name}: {self.off} > {self.limit}")
        return t, b

    def mark(self):
        return self.off

    def reset(self, off):
        self.off = off


def ap3(t, offset, pattern):
    return bass.AP(t, offset, pattern)



def emit_phase1(nc, P, A, dram, pb):
    TT = 256
    NT = T // TT
    NTT = NT * BL

    x_h, x_b = dram["x0"]
    x1_h, x1_b = dram["x1"]
    x1_subs = dram["x1_subs"]
    wk_h, wk_b = dram["WkT"]
    wv_h, wv_b = dram["WvT"]
    wr_h, wr_b = dram["WrT"]
    wo_h, wo_b = dram["WoT"]
    tmk_h, tmk_b = dram["tmk"]
    tmv_h, tmv_b = dram["tmv"]
    tmr_h, tmr_b = dram["tmr"]
    tdec_h, tdec_b = dram["tdec"]
    tfir_h, tfir_b = dram["tfir"]

    base = A.mark()
    wkb, wkbB = A.alloc("p1_wk", 128, EC * 1024, BF16)
    wvb, wvbB = A.alloc("p1_wv", 128, EC * 1024, BF16)
    wrb, wrbB = A.alloc("p1_wr", 128, EC * 1024, BF16)
    wob, wobB = A.alloc("p1_wo", 128, EC * 1024, BF16)
    tmk, tmkB = A.alloc("p1_tmk", 128, EC, F32)
    tmv, tmvB = A.alloc("p1_tmv", 128, EC, F32)
    tmr, tmrB = A.alloc("p1_tmr", 128, EC, F32)
    wdec, wdecB = A.alloc("p1_wdec", 128, EC, F32)
    ufir, ufirB = A.alloc("p1_u", 128, EC, F32)
    xin = [A.alloc(f"p1_xin{i}", 128, EC * TT, F32) for i in range(2)]
    xb, xbB = A.alloc("p1_xb", 128, EC * TT, BF16)
    sqx, sqxB = A.alloc("p1_sqx", 128, EC * TT, BF16)
    xln, xlnB = A.alloc("p1_xln", 128, EC * TT, BF16)
    xkb, xkbB = A.alloc("p1_xk", 128, EC * TT, BF16)
    xvb, xvbB = A.alloc("p1_xv", 128, EC * TT, BF16)
    xrb, xrbB = A.alloc("p1_xr", 128, EC * TT, BF16)
    kbuf = [A.alloc(f"p1_k{i}", 128, EC * TT, F32) for i in range(2)]
    vbuf = [A.alloc(f"p1_v{i}", 128, EC * TT, BF16) for i in range(2)]
    srb = [A.alloc(f"p1_sr{i}", 128, EC * TT, BF16) for i in range(2)]
    psub = {}
    for lst, esz in ((kbuf, 4), (vbuf, 2), (srb, 2)):
        for t_, b_ in lst:
            psub[b_] = [P.buf("sbuf", b_.start + g * 2 * TT * esz,
                              b_.start + (g + 1) * 2 * TT * esz,
                              f"{b_.name}_g{g}") for g in range(4)]
    ppx, ppxB = A.alloc("p1_ppx", 128, EC * (TT + 1), F32)
    aax, aaxB = A.alloc("p1_aax", 128, EC * (TT + 1), F32)
    bbx, bbxB = A.alloc("p1_bbx", 128, EC * (TT + 1), F32)
    ppxC = [P.buf("sbuf", ppxB.start + c * (TT + 1) * 4,
                  ppxB.start + (c + 1) * (TT + 1) * 4, f"p1_ppx_c{c}") for c in range(EC)]
    aaxC = [P.buf("sbuf", aaxB.start + c * (TT + 1) * 4,
                  aaxB.start + (c + 1) * (TT + 1) * 4, f"p1_aax_c{c}") for c in range(EC)]
    bbxC = [P.buf("sbuf", bbxB.start + c * (TT + 1) * 4,
                  bbxB.start + (c + 1) * (TT + 1) * 4, f"p1_bbx_c{c}") for c in range(EC)]
    s1, s1B = A.alloc("p1_s1", 128, EC * TT, F32)
    s2, s2B = A.alloc("p1_s2", 128, EC * TT, F32)
    s3, s3B = A.alloc("p1_s3", 128, EC * TT, F32)
    rwb = [A.alloc(f"p1_rw{i}", 128, EC * TT, BF16) for i in range(2)]
    xb2, xb2B = A.alloc("p1_xb2", 128, EC * TT, BF16)
    xb2C = [P.buf("sbuf", xb2B.start + g * 2 * TT * 2,
                  xb2B.start + (g + 1) * 2 * TT * 2, f"p1_xb2_g{g}") for g in range(4)]
    sqx2, sqx2B = sqx, sqxB
    muB, muBB = A.alloc("p1_muB", 128, TT, F32)
    rstdB, rstdBB = A.alloc("p1_rstdB", 128, TT, F32)
    muB2, muB2B = A.alloc("p1_muB2", 128, TT, F32)
    rstdB2, rstdB2B = A.alloc("p1_rstdB2", 128, TT, F32)
    stat, statB = A.alloc("p1_stat", 1, 4 * TT, F32)
    stat2, stat2B = stat, statB
    onesr, onesrB = A.alloc("p1_onesr", 1, 128, F32)
    onesc, onescB = A.alloc("p1_onesc", 128, 1, BF16)
    lnc, lncB = A.alloc("p1_lnc", 1, 2, F32)
    carry = {}
    carryB = {}
    for s in range(BL):
        for nm in ("pp", "aa", "bb"):
            carry[(nm, s)], carryB[(nm, s)] = A.alloc(f"p1_c{nm}{s}", 128, EC, F32)
        carry[("xc", s)], carryB[("xc", s)] = A.alloc(f"p1_cx{s}", 128, EC, BF16)

    for (hb, hbuf), (sb, sbuf) in (((wk_h, wk_b), (wkb, wkbB)),
                                   ((wv_h, wv_b), (wvb, wvbB)),
                                   ((wr_h, wr_b), (wrb, wrbB)),
                                   ((wo_h, wo_b), (wob, wobB))):
        src = hb.ap().rearrange("(c p) d -> p c d", p=128)
        P.dma("gpsimd", lambda e, d=sb, s=src: e.dma_start(
                out=ap3(d, 0, [[EC * 1024, 128], [1024, EC], [1, 1024]]), in_=s),
              reads=[hbuf], writes=[sbuf], tag="w_ld")
    for (hb, hbuf), (sb, sbuf) in (((tmk_h, tmk_b), (tmk, tmkB)),
                                   ((tmv_h, tmv_b), (tmv, tmvB)),
                                   ((tmr_h, tmr_b), (tmr, tmrB)),
                                   ((tfir_h, tfir_b), (ufir, ufirB))):
        P.dma("sync", lambda e, d=sb, s=hb: e.dma_start(out=d[:, :], in_=s.ap()),
              reads=[hbuf], writes=[sbuf], tag="v_ld")
    P.dma("sync", lambda e: e.dma_start(out=wdec[:, :], in_=tdec_h.ap()),
          reads=[tdec_b], writes=[wdecB], tag="tdec_ld")
    P.op("scalar", lambda e: e.activation(out=wdec[:, :], in_=wdec[:, :], func=AF.Exp),
         reads=[wdecB], writes=[wdecB], tag="wdec_exp")
    P.op("vector", lambda e: e.tensor_scalar_mul(out=wdec[:, :], in0=wdec[:, :], scalar1=-1.0),
         reads=[wdecB], writes=[wdecB], tag="wdec_neg")
    P.op("vector", lambda e: e.memset(onesr[:, :], 1.0), writes=[onesrB], tag="ones_r")
    P.op("vector", lambda e: e.memset(onesc[:, :], 1.0), writes=[onescB], tag="ones_c")
    P.op("vector", lambda e: e.memset(lnc[0:1, 0:1], LN_EPS), writes=[lncB], tag="p1lne")
    P.op("vector", lambda e: e.memset(lnc[0:1, 1:2], -0.5), writes=[lncB], tag="p1lnh")
    for s in range(BL):
        P.op("vector", lambda e, s=s: e.memset(carry[("pp", s)][:, :], -1e38),
             writes=[carryB[("pp", s)]], tag="c_pp")
        P.op("vector", lambda e, s=s: e.memset(carry[("aa", s)][:, :], 0.0),
             writes=[carryB[("aa", s)]], tag="c_aa")
        P.op("vector", lambda e, s=s: e.memset(carry[("bb", s)][:, :], 0.0),
             writes=[carryB[("bb", s)]], tag="c_bb")
        P.op("vector", lambda e, s=s: e.memset(carry[("xc", s)][:, :], 0.0),
             writes=[carryB[("xc", s)]], tag="c_x")

    def v3(t, off=0, n=EC):
        return ap3(t, off, [[EC * TT, 128], [TT, n], [1, TT]])

    def b3(t, n=EC):
        return ap3(t, 0, [[TT, 128], [0, n], [1, TT]])

    def pv3(t, a=0, n=EC):
        return ap3(t, a, [[EC * (TT + 1), 128], [TT + 1, n], [1, TT]])

    def cb3(t, c0=0, n=EC):
        return ap3(t, c0, [[EC, 128], [1, n], [0, TT]])

    def emit_ln_stats(src_t, srcB, sq_t, sqB_, st_, stB_, mu_t, muB_, rs_t, rsB_,
                      bsum, bsq, tag, delta=None):
        if delta is not None:
            for c in range(EC):
                P.op("tensor", lambda e, c=c: e.matmul(
                        pb[bsum][0][0:1, 0:2 * TT], onesc[:, 0:1],
                        ap3(src_t, c * TT, [[EC * TT, 128], [delta, 2], [1, TT]]),
                        start=(c == 0), stop=(c == EC - 1)),
                     reads=[srcB, sqB_, onescB], writes=[pb[bsum][1]], tag=f"{tag}_sum")
        else:
            for c in range(EC):
                P.op("tensor", lambda e, c=c: e.matmul(pb[bsum][0][0:1, 0:TT], onesc[:, 0:1],
                                                       ap3(src_t, c * TT, [[EC * TT, 128], [1, TT]]),
                                                       start=(c == 0), stop=(c == EC - 1)),
                     reads=[srcB, onescB], writes=[pb[bsum][1]], tag=f"{tag}_sum")
            for c in range(EC):
                P.op("tensor", lambda e, c=c: e.matmul(pb[bsum][0][0:1, TT:2 * TT], onesc[:, 0:1],
                                                       ap3(sq_t, c * TT, [[EC * TT, 128], [1, TT]]),
                                                       start=(c == 0), stop=(c == EC - 1)),
                     reads=[sqB_, onescB], writes=[pb[bsum][1]], tag=f"{tag}_ssq")
        n = EC * 128
        P.op("vector", lambda e: e.tensor_scalar_mul(out=st_[0:1, 0:2 * TT],
                                                     in0=pb[bsum][0][0:1, 0:2 * TT], scalar1=1.0 / n),
             reads=[pb[bsum][1]], writes=[stB_], tag=f"{tag}_mu")
        P.op("vector", lambda e: e.tensor_mul(out=st_[0:1, 2 * TT:3 * TT],
                                              in0=st_[0:1, 0:TT], in1=st_[0:1, 0:TT]),
             reads=[stB_], writes=[stB_], tag=f"{tag}_musq")
        P.op("vector", lambda e: e.tensor_sub(out=st_[0:1, 2 * TT:3 * TT],
                                              in0=st_[0:1, TT:2 * TT], in1=st_[0:1, 2 * TT:3 * TT]),
             reads=[stB_], writes=[stB_], tag=f"{tag}_var")
        P.op("scalar", lambda e: e.activation(out=st_[0:1, 2 * TT:3 * TT],
                                              in_=st_[0:1, 2 * TT:3 * TT], func=AF.Ln,
                                              bias=lnc[0:1, 0:1]),
             reads=[stB_, lncB], writes=[stB_], tag=f"{tag}_lnv")
        P.op("scalar", lambda e: e.activation(out=st_[0:1, TT:2 * TT],
                                              in_=st_[0:1, 2 * TT:3 * TT], func=AF.Exp,
                                              scale=lnc[0:1, 1:2]),
             reads=[stB_, lncB], writes=[stB_], tag=f"{tag}_rstd")
        P.op("tensor", lambda e: e.matmul(pb[bsum][0][:, 0:2 * TT], onesr[0:1, :],
                                          st_[0:1, 0:2 * TT], start=True, stop=True),
             reads=[stB_, onesrB], writes=[pb[bsum][1]], tag=f"{tag}_bmu")
        P.op("scalar", lambda e: e.copy(out=mu_t[:, :], in_=pb[bsum][0][:, 0:TT]),
             reads=[pb[bsum][1]], writes=[muB_], tag=f"{tag}_emu")
        P.op("scalar", lambda e: e.copy(out=rs_t[:, :], in_=pb[bsum][0][:, TT:2 * TT]),
             reads=[pb[bsum][1]], writes=[rsB_], tag=f"{tag}_ers")

    def stA(ti):
        tt, s = divmod(ti, BL)
        t0 = tt * TT
        xt, xtB = xin[ti % 2]
        src = x_h.ap()[s].rearrange("(c p) t -> p c t", p=128)[:, :, t0:t0 + TT]
        P.dma("sync", lambda e, d=xt, sr=src: e.dma_start(
                out=ap3(d, 0, [[EC * TT, 128], [TT, EC], [1, TT]]), in_=sr),
              reads=[x_b], writes=[xtB], tag=f"x_ld{ti}")
        xdst = x1_h.ap()[s].rearrange("(c p) t -> p c t", p=128)[:, :, t0:t0 + TT]
        P.dma("gpsimd", lambda e, xt=xt, d=xdst: e.dma_start(
                out=d, in_=ap3(xt, 0, [[EC * TT, 128], [TT, EC], [1, TT]])),
              reads=[xtB], writes=[x1_subs[s * 8 + t0 // 256]], tag=f"x_cp{ti}")
        P.op("scalar", lambda e, xt=xt: e.copy(out=xb[:, :], in_=xt[:, :]),
             reads=[xtB], writes=[xbB], tag=f"xb{ti}")
        P.op("gpsimd", lambda e: e.tensor_mul(out=sqx[:, :], in0=xb[:, :], in1=xb[:, :]),
             reads=[xbB], writes=[sqxB], tag=f"sq{ti}")
        emit_ln_stats(xb, xbB, sqx, sqxB, stat, statB, muB, muBB, rstdB, rstdBB,
                      4, 5, f"ln1_{ti}")
        P.op("vector", lambda e, xt=xt: e.tensor_tensor(
                out=v3(s1), in0=v3(xt), in1=b3(muB), op=ALU.subtract),
             reads=[xtB, muBB], writes=[s1B], tag=f"lnsub{ti}")
        P.op("vector", lambda e: e.tensor_tensor(
                out=v3(xln), in0=v3(s1), in1=b3(rstdB), op=ALU.mult),
             reads=[s1B, rstdBB], writes=[xlnB], tag=f"lnmul{ti}")
        P.op("scalar", lambda e, s=s: e.copy(
                out=ap3(sqx, 0, [[EC * TT, 128], [TT, EC], [1, 1]]),
                in_=ap3(carry[("xc", s)], 0, [[EC, 128], [1, EC], [0, 1]])),
             reads=[carryB[("xc", s)]], writes=[sqxB], tag=f"xx0_{ti}")
        P.op("scalar", lambda e: e.copy(
                out=ap3(sqx, 1, [[EC * TT, 128], [TT, EC], [1, TT - 1]]),
                in_=ap3(xln, 0, [[EC * TT, 128], [TT, EC], [1, TT - 1]])),
             reads=[xlnB], writes=[sqxB], tag=f"xxs_{ti}")
        P.op("vector", lambda e, s=s: e.tensor_copy(
                out=carry[("xc", s)][:, :],
                in_=ap3(xln, TT - 1, [[EC * TT, 128], [TT, EC]])),
             reads=[xlnB], writes=[carryB[("xc", s)]], tag=f"xcu_{ti}")
        P.op("vector", lambda e: e.tensor_sub(out=s1[:, :], in0=xln[:, :], in1=sqx[:, :]),
             reads=[xlnB, sqxB], writes=[s1B], tag=f"diff{ti}")
        for (tm, tmB, dst, dstB, nm, eng, sc) in (
                (tmk, tmkB, xkb, xkbB, "k", "vector", s2),
                (tmv, tmvB, xvb, xvbB, "v", "gpsimd", s3),
                (tmr, tmrB, xrb, xrbB, "r", "vector", s2)):
            scB = s2B if sc is s2 else s3B
            P.op(eng, lambda e, tm=tm, sc=sc: e.tensor_tensor(
                    out=v3(sc), in0=v3(s1), in1=cb3(tm), op=ALU.mult),
                 reads=[s1B, tmB], writes=[scB], tag=f"xm{nm}{ti}")
            P.op(eng, lambda e, dst=dst, sc=sc: e.tensor_add(
                    out=dst[:, :], in0=sc[:, :], in1=sqx[:, :]),
                 reads=[scB, sqxB], writes=[dstB], tag=f"xa{nm}{ti}")

    def stB(ti):
        kb_, kbB_ = kbuf[ti % 2]
        vb_, vbB_ = vbuf[ti % 2]
        sb_, sbB_ = srb[ti % 2]
        for (wt, wtB, src_t, srcB, banks, dst, dstB, nm, sig) in (
                (wkb, wkbB, xkb, xkbB, (0, 5), kb_, kbB_, "k", False),
                (wvb, wvbB, xvb, xvbB, (1, 7), vb_, vbB_, "v", False),
                (wrb, wrbB, xrb, xrbB, (2, 2), sb_, sbB_, "r", True)):
            for dc in range(EC):
                bank = banks[(dc // 2) % 2]
                col = (dc % 2) * TT
                for ec in range(EC):
                    P.op("tensor", lambda e, wt=wt, src_t=src_t, dc=dc, ec=ec, bank=bank, col=col:
                         e.matmul(pb[bank][0][:, col:col + TT],
                                  wt[:, ec * 1024 + dc * 128: ec * 1024 + dc * 128 + 128],
                                  src_t[:, ec * TT:(ec + 1) * TT],
                                  start=(ec == 0), stop=(ec == EC - 1)),
                         reads=[wtB, srcB], writes=[pb[bank][1]], tag=f"mm{nm}{ti}_{dc}_{ec}")
                if dc % 2 == 0:
                    continue
                if sig:
                    P.op("scalar", lambda e, dc=dc, dst=dst, bank=bank: e.activation(
                            out=dst[:, (dc - 1) * TT:(dc + 1) * TT], in_=pb[bank][0][:, 0:2 * TT],
                            func=AF.Sigmoid),
                         reads=[pb[bank][1]], writes=[psub[dstB][dc // 2]], tag=f"ev{nm}{ti}_{dc}")
                else:
                    P.op("scalar", lambda e, dc=dc, dst=dst, bank=bank: e.copy(
                            out=dst[:, (dc - 1) * TT:(dc + 1) * TT], in_=pb[bank][0][:, 0:2 * TT]),
                         reads=[pb[bank][1]], writes=[psub[dstB][dc // 2]], tag=f"ev{nm}{ti}_{dc}")

    def _cenv(ti):
        tt, s = divmod(ti, BL)
        kb_, kbB_ = kbuf[ti % 2]
        vb_, vbB_ = vbuf[ti % 2]
        sb_, sbB_ = srb[ti % 2]
        rw_, rwB_ = rwb[ti % 2]
        cpp, cppB = carry[("pp", s)], carryB[("pp", s)]
        caa, caaB = carry[("aa", s)], carryB[("aa", s)]
        cbb, cbbB = carry[("bb", s)], carryB[("bb", s)]
        return (kb_, kbB_, vb_, vbB_, sb_, sbB_, rw_, rwB_,
                cpp, cppB, caa, caaB, cbb, cbbB)

    def stCa(ti):
        (kb_, kbB_, vb_, vbB_, sb_, sbB_, rw_, rwB_,
         cpp, cppB, caa, caaB, cbb, cbbB) = _cenv(ti)
        P.op("vector", lambda e: e.tensor_copy(
                out=ap3(ppx, 0, [[EC * (TT + 1), 128], [TT + 1, EC], [1, 1]]),
                in_=ap3(cpp, 0, [[EC, 128], [1, EC], [0, 1]])),
             reads=[cppB], writes=[ppxB], tag=f"pp0_{ti}")
        for c in range(EC):
            P.op("vector", lambda e, c=c: e.tensor_tensor_scan(
                    out=ppx[:, c * (TT + 1) + 1: c * (TT + 1) + TT + 1],
                    data0=ap3(wdec, c, [[EC, 128], [0, TT]]),
                    data1=kb_[:, c * TT:(c + 1) * TT],
                    initial=cpp[:, c:c + 1],
                    op0=ALU.add, op1=ALU.max),
                 reads=[wdecB, kbB_, cppB], writes=[ppxC[c]], tag=f"ppscan{ti}_{c}")
        P.op("vector", lambda e: e.tensor_tensor(
                out=v3(s1), in0=pv3(ppx, 0), in1=cb3(wdec), op=ALU.add),
             reads=[ppxB, wdecB], writes=[s1B], tag=f"ga{ti}")
        P.op("vector", lambda e: e.tensor_tensor(
                out=v3(s1), in0=v3(s1), in1=pv3(ppx, 1), op=ALU.subtract),
             reads=[s1B, ppxB], writes=[s1B], tag=f"gs{ti}")
        P.op("vector", lambda e: e.tensor_tensor(
                out=v3(s2), in0=v3(kb_), in1=pv3(ppx, 1), op=ALU.subtract),
             reads=[kbB_, ppxB], writes=[s2B], tag=f"hs{ti}")
        P.op("scalar", lambda e: e.activation(out=s1[:, :], in_=s1[:, :], func=AF.Exp),
             reads=[s1B], writes=[s1B], tag=f"gact{ti}")
        P.op("scalar", lambda e: e.activation(out=s2[:, :], in_=s2[:, :], func=AF.Exp),
             reads=[s2B], writes=[s2B], tag=f"hact{ti}")
    def stCb(ti):
        (kb_, kbB_, vb_, vbB_, sb_, sbB_, rw_, rwB_,
         cpp, cppB, caa, caaB, cbb, cbbB) = _cenv(ti)
        P.op("gpsimd", lambda e: e.tensor_mul(out=s3[:, :], in0=s2[:, :], in1=vb_[:, :]),
             reads=[s2B, vbB_], writes=[s3B], tag=f"hv{ti}")
        P.op("vector", lambda e: e.tensor_copy(
                out=ap3(aax, 0, [[EC * (TT + 1), 128], [TT + 1, EC], [1, 1]]),
                in_=ap3(caa, 0, [[EC, 128], [1, EC], [0, 1]])),
             reads=[caaB], writes=[aaxB], tag=f"aa0_{ti}")
        P.op("vector", lambda e: e.tensor_copy(
                out=ap3(bbx, 0, [[EC * (TT + 1), 128], [TT + 1, EC], [1, 1]]),
                in_=ap3(cbb, 0, [[EC, 128], [1, EC], [0, 1]])),
             reads=[cbbB], writes=[bbxB], tag=f"bb0_{ti}")
        for c in range(EC):
            P.op("vector", lambda e, c=c: e.tensor_tensor_scan(
                    out=aax[:, c * (TT + 1) + 1: c * (TT + 1) + TT + 1],
                    data0=s1[:, c * TT:(c + 1) * TT], data1=s3[:, c * TT:(c + 1) * TT],
                    initial=caa[:, c:c + 1], op0=ALU.mult, op1=ALU.add),
                 reads=[s1B, s3B, caaB], writes=[aaxC[c]], tag=f"aascan{ti}_{c}")
            P.op("vector", lambda e, c=c: e.tensor_tensor_scan(
                    out=bbx[:, c * (TT + 1) + 1: c * (TT + 1) + TT + 1],
                    data0=s1[:, c * TT:(c + 1) * TT], data1=s2[:, c * TT:(c + 1) * TT],
                    initial=cbb[:, c:c + 1], op0=ALU.mult, op1=ALU.add),
                 reads=[s1B, s2B, cbbB], writes=[bbxC[c]], tag=f"bbscan{ti}_{c}")
        for nm, t_, tB_, cb_, cbB2 in (("pp", ppx, ppxB, cpp, cppB),
                                       ("aa", aax, aaxB, caa, caaB),
                                       ("bb", bbx, bbxB, cbb, cbbB)):
            P.op("vector", lambda e, t_=t_, cb_=cb_: e.tensor_copy(
                    out=cb_[:, :],
                    in_=ap3(t_, TT, [[EC * (TT + 1), 128], [TT + 1, EC]])),
                 reads=[tB_], writes=[cbB2], tag=f"cu{nm}_{ti}")
        P.op("gpsimd", lambda e: e.tensor_tensor(
                out=v3(s2), in0=v3(kb_), in1=cb3(ufir), op=ALU.add),
             reads=[kbB_, ufirB], writes=[s2B], tag=f"uk{ti}")
        P.op("vector", lambda e: e.tensor_tensor(
                out=v3(s3), in0=v3(s2), in1=pv3(ppx, 0), op=ALU.max),
             reads=[s2B, ppxB], writes=[s3B], tag=f"qm{ti}")
        P.op("vector", lambda e: e.tensor_tensor(
                out=v3(s1), in0=pv3(ppx, 0), in1=v3(s3), op=ALU.subtract),
             reads=[ppxB, s3B], writes=[s1B], tag=f"asub{ti}")
        P.op("vector", lambda e: e.tensor_sub(out=s2[:, :], in0=s2[:, :], in1=s3[:, :]),
             reads=[s2B, s3B], writes=[s2B], tag=f"bsub{ti}")
        P.op("scalar", lambda e: e.activation(out=s1[:, :], in_=s1[:, :], func=AF.Exp),
             reads=[s1B], writes=[s1B], tag=f"aact{ti}")
        P.op("scalar", lambda e: e.activation(out=s2[:, :], in_=s2[:, :], func=AF.Exp),
             reads=[s2B], writes=[s2B], tag=f"bact{ti}")
    def stCc(ti):
        (kb_, kbB_, vb_, vbB_, sb_, sbB_, rw_, rwB_,
         cpp, cppB, caa, caaB, cbb, cbbB) = _cenv(ti)
        P.op("gpsimd", lambda e: e.tensor_tensor(
                out=v3(s3), in0=v3(s1), in1=pv3(bbx, 0), op=ALU.mult),
             reads=[s1B, bbxB], writes=[s3B], tag=f"den1{ti}")
        P.op("vector", lambda e: e.tensor_add(out=s3[:, :], in0=s3[:, :], in1=s2[:, :]),
             reads=[s3B, s2B], writes=[s3B], tag=f"den2{ti}")
        P.op("vector", lambda e: e.tensor_scalar_max(out=s3[:, :], in0=s3[:, :], scalar1=1e-30),
             reads=[s3B], writes=[s3B], tag=f"denc{ti}")
        P.op("scalar", lambda e: e.activation(out=s3[:, :], in_=s3[:, :], func=AF.Ln),
             reads=[s3B], writes=[s3B], tag=f"rdl{ti}")
        P.op("vector", lambda e: e.tensor_scalar_mul(out=s3[:, :], in0=s3[:, :], scalar1=-1.0),
             reads=[s3B], writes=[s3B], tag=f"rdn{ti}")
        P.op("scalar", lambda e: e.activation(out=s3[:, :], in_=s3[:, :], func=AF.Exp),
             reads=[s3B], writes=[s3B], tag=f"rde{ti}")
        P.op("vector", lambda e: e.tensor_tensor(
                out=v3(s1), in0=v3(s1), in1=pv3(aax, 0), op=ALU.mult),
             reads=[s1B, aaxB], writes=[s1B], tag=f"num1{ti}")
        P.op("gpsimd", lambda e: e.tensor_mul(out=s2[:, :], in0=s2[:, :], in1=vb_[:, :]),
             reads=[s2B, vbB_], writes=[s2B], tag=f"num2{ti}")
        P.op("gpsimd", lambda e: e.tensor_add(out=s1[:, :], in0=s1[:, :], in1=s2[:, :]),
             reads=[s1B, s2B], writes=[s1B], tag=f"num3{ti}")
        P.op("vector", lambda e: e.tensor_mul(out=s1[:, :], in0=s1[:, :], in1=s3[:, :]),
             reads=[s1B, s3B], writes=[s1B], tag=f"y{ti}")
        P.op("gpsimd", lambda e, rw_=rw_: e.tensor_mul(out=rw_[:, :], in0=s1[:, :], in1=sb_[:, :]),
             reads=[s1B, sbB_], writes=[rwB_], tag=f"rw{ti}")

    def stD(ti):
        tt, s = divmod(ti, BL)
        t0 = tt * TT
        xt, xtB = xin[ti % 2]
        rw_, rwB_ = rwb[ti % 2]
        for cc in range(EC):
            bank = 3
            col = (cc % 2) * TT
            for dc in range(EC):
                P.op("tensor", lambda e, cc=cc, dc=dc, bank=bank, col=col, rw_=rw_: e.matmul(
                        pb[bank][0][:, col:col + TT],
                        wob[:, dc * 1024 + cc * 128: dc * 1024 + cc * 128 + 128],
                        rw_[:, dc * TT:(dc + 1) * TT],
                        start=(dc == 0), stop=(dc == EC - 1)),
                     reads=[wobB, rwB_], writes=[pb[bank][1]], tag=f"mmo{ti}_{cc}_{dc}")
            if cc % 2 == 1:
                P.op("scalar", lambda e, cc=cc, bank=bank: e.copy(
                        out=xb2[:, (cc - 1) * TT:(cc + 1) * TT], in_=pb[bank][0][:, 0:2 * TT]),
                     reads=[pb[bank][1]], writes=[xb2C[cc // 2]], tag=f"evo{ti}_{cc}")
        P.op("gpsimd", lambda e: e.tensor_mul(out=sqx2[:, :], in0=xb2[:, :], in1=xb2[:, :]),
             reads=[xb2B], writes=[sqx2B], tag=f"aosq{ti}")
        emit_ln_stats(xb2, xb2B, sqx2, sqx2B, stat2, stat2B, muB2, muB2B,
                      rstdB2, rstdB2B, 6, 7, f"ln2_{ti}")
        P.op("vector", lambda e: e.tensor_tensor(
                out=v3(s1), in0=v3(xb2), in1=b3(muB2), op=ALU.subtract),
             reads=[xb2B, muB2B], writes=[s1B], tag=f"posub{ti}")
        P.op("vector", lambda e: e.tensor_tensor(
                out=v3(s1), in0=v3(s1), in1=b3(rstdB2), op=ALU.mult),
             reads=[s1B, rstdB2B], writes=[s1B], tag=f"pomul{ti}")
        dst = x1_h.ap()[s].rearrange("(c p) t -> p c t", p=128)[:, :, t0:t0 + TT]
        P.dma("gpsimd", lambda e, d=dst: e.dma_start(
                out=d, in_=ap3(s1, 0, [[EC * TT, 128], [TT, EC], [1, TT]]),
                accum_op=ALU.add),
              reads=[s1B], writes=[x1_subs[s * 8 + t0 // 256]], tag=f"x1_st{ti}")

    order = [tt * BL + s for tt in range(NT) for s in range(BL)]
    stA(order[0])
    stB(order[0])
    stA(order[1])
    for i, ti in enumerate(order):
        if i > 0:
            stD(order[i - 1])
        if i + 1 < NTT:
            stB(order[i + 1])
        stCa(ti)
        stCb(ti)
        stCc(ti)
        if i + 2 < NTT:
            stA(order[i + 2])
    stD(order[NTT - 1])
    A.reset(base)



SBUF_LIMIT = 229376


def _prep_shared(inputs):
    f = np.ascontiguousarray
    w = {}
    w["WkT"] = f(inputs["L0_attn_Wk"].T)
    w["WvT"] = f(inputs["L0_attn_Wv"].T)
    w["WrT"] = f(inputs["L0_attn_Wr"].T)
    w["WoT"] = f(inputs["L0_attn_Wo"].T)
    for nm, key in (("tmk", "L0_attn_time_mix_k"), ("tmv", "L0_attn_time_mix_v"),
                    ("tmr", "L0_attn_time_mix_r"), ("tdec", "L0_attn_time_decay"),
                    ("tfir", "L0_attn_time_first")):
        w[nm] = f(inputs[key].reshape(EC, 128).T)
    return w



def emit_phase2(nc, P, A, dram, pb):
    TT = 256
    NT = (BL * T) // TT
    HC = HFF // 128

    x1_h, x1_b = dram["x1"]
    x1_subs = dram["x1_subs"]
    x2_h, x2_b = dram["x2"]
    x2_subs = dram["x2_subs"]
    wfc_h, wfc_b = dram["WfcT"]
    wpr_h, wpr_b = dram["WprT"]
    lnw_h, lnw_b = dram["lnw"]
    lnb_h, lnb_b = dram["lnb"]

    base = A.mark()
    wfq, wfqB = A.alloc("p2_wfq", 128, EC * HFF, BF16)
    wpq, wpqB = A.alloc("p2_wpq", 128, HC * D, BF16)
    lnw, lnwB = A.alloc("p2_lnw", 128, HC, F32)
    lnb, lnbB = A.alloc("p2_lnb", 128, HC, F32)
    stat, statB = A.alloc("p2_stat", 1, 6 * TT, F32)
    onesr, onesrB = A.alloc("p2_onesr", 1, 128, F32)
    onesc, onescB = A.alloc("p2_onesc", 128, 1, BF16)
    lnc, lncB = A.alloc("p2_lnc", 1, 2, F32)
    thrB_t, thrB_B = A.alloc("p2_thr", 128, 2, F32)
    muB, muBB = A.alloc("p2_muB", 128, TT, F32)
    rstdB, rstdBB = A.alloc("p2_rstdB", 128, TT, F32)

    tile_base = A.mark()
    qstg = [A.alloc(f"p2_qstg{i}", 128, 4096, F32) for i in range(2)]
    sgn, sgnB = A.alloc("p2_sgn", 128, 4096, BF16)
    bstg = [A.alloc(f"p2_bstg{i}", 128, 4096, BF16) for i in range(2)]
    A.reset(tile_base)
    xin = [A.alloc(f"p2_xin{i}", 128, EC * TT, F32) for i in range(2)]
    xb, xbB = A.alloc("p2_xb", 128, EC * TT, BF16)
    sqx, sqxB = A.alloc("p2_sqx", 128, EC * TT, BF16)
    xln, xlnB = A.alloc("p2_xln", 128, EC * TT, BF16)
    hb, hbB_whole = A.alloc("p2_hb", 128, HC * TT, BF16)
    hbC = [P.buf("sbuf", hbB_whole.start + c * TT * 2,
                 hbB_whole.start + (c + 1) * TT * 2, f"p2_hb_c{c}") for c in range(HC)]
    m2 = A.mark()
    h2b, h2bB = A.alloc("p2_h2b", 128, HC * TT, BF16)
    A.reset(m2)
    xout, xoutB = A.alloc("p2_xout", 128, EC * TT, F32)
    xoutC = [P.buf("sbuf", xoutB.start + g * 2 * TT * 4,
                   xoutB.start + (g + 1) * 2 * TT * 4, f"p2_xout_g{g}") for g in range(4)]
    A.reset(max(A.mark(), m2 + HC * TT * 2))
    s1, s1B = A.alloc("p2_s1", 128, EC * TT, F32)


    P.op("vector", lambda e: e.memset(onesr[:, :], 1.0), writes=[onesrB], tag="p2ones_r")
    P.op("vector", lambda e: e.memset(onesc[:, :], 1.0), writes=[onescB], tag="p2ones_c")
    P.op("vector", lambda e: e.memset(lnc[0:1, 0:1], LN_EPS), writes=[lncB], tag="p2lne")
    P.op("vector", lambda e: e.memset(lnc[0:1, 1:2], -0.5), writes=[lncB], tag="p2lnh")
    P.dma("sync", lambda e: e.dma_start(out=lnw[:, :], in_=lnw_h.ap()),
          reads=[lnw_b], writes=[lnwB], tag="lnw_ld")
    P.dma("sync", lambda e: e.dma_start(out=lnb[:, :], in_=lnb_h.ap()),
          reads=[lnb_b], writes=[lnbB], tag="lnb_ld")

    def quant(w_h, w_b, w16_h, w16_b, ntiles, cols, dest, destB, pfx):
        for i in range(ntiles):
            bs, bsB = bstg[i % 2]
            src = w16_h.ap()[128 * i:128 * (i + 1), :]
            P.dma("sync", lambda e, bs=bs, src=src, cols=cols: e.dma_start(
                    out=bs[:, 0:cols], in_=src),
                  reads=[w16_b], writes=[bsB], tag=f"{pfx}qld{i}")
            P.op("scalar", lambda e, bs=bs, cols=cols: e.activation(
                    out=bs[:, 0:cols], in_=bs[:, 0:cols], func=AF.Abs),
                 reads=[bsB], writes=[bsB], tag=f"{pfx}qabs{i}")
            nch = cols // 512
            for j in range(nch):
                first = (i == 0 and j == 0)
                last = (i == ntiles - 1 and j == nch - 1)
                P.op("tensor", lambda e, bs=bs, j=j, first=first, last=last: e.matmul(
                        pb[4][0][0:1, 0:512], onesc[:, 0:1], bs[:, 512 * j:512 * (j + 1)],
                        start=first, stop=last),
                     reads=[bsB, onescB], writes=[pb[4][1]], tag=f"{pfx}qmm{i}_{j}")
        n = ntiles * 128 * cols
        P.op("vector", lambda e: e.tensor_reduce(out=stat[0:1, 0:1], in_=pb[4][0][0:1, 0:512],
                                                 axis=AX.X, op=ALU.add),
             reads=[pb[4][1]], writes=[statB], tag=f"{pfx}qsum")
        P.op("vector", lambda e, n=n: e.tensor_scalar(out=stat[0:1, 1:2], in0=stat[0:1, 0:1],
                                                      scalar1=0.5 / n, scalar2=0.5e-6,
                                                      op0=ALU.mult, op1=ALU.add),
             reads=[statB], writes=[statB], tag=f"{pfx}qthr")
        P.op("tensor", lambda e: e.matmul(pb[5][0][:, 0:1], onesr[0:1, :], stat[0:1, 1:2],
                                          start=True, stop=True),
             reads=[statB, onesrB], writes=[pb[5][1]], tag=f"{pfx}qbct")
        P.op("scalar", lambda e: e.copy(out=thrB_t[:, 0:1], in_=pb[5][0][:, 0:1]),
             reads=[pb[5][1]], writes=[thrB_B], tag=f"{pfx}qbce")
        for i in range(ntiles):
            qs, qsB = qstg[i % 2]
            src = w_h.ap()[128 * i:128 * (i + 1), :]
            P.dma("sync", lambda e, qs=qs, src=src, cols=cols: e.dma_start(
                    out=qs[:, 0:cols], in_=src),
                  reads=[w_b], writes=[qsB], tag=f"{pfx}tld{i}")
            P.op("scalar", lambda e, qs=qs, cols=cols: e.activation(
                    out=sgn[:, 0:cols], in_=qs[:, 0:cols], func=AF.Sign),
                 reads=[qsB], writes=[sgnB], tag=f"{pfx}tsg{i}")
            P.op("scalar", lambda e, qs=qs, cols=cols: e.activation(
                    out=qs[:, 0:cols], in_=qs[:, 0:cols], func=AF.Abs),
                 reads=[qsB], writes=[qsB], tag=f"{pfx}tab{i}")
            P.op("vector", lambda e, qs=qs, cols=cols: e.tensor_scalar(
                    out=qs[:, 0:cols], in0=qs[:, 0:cols], scalar1=thrB_t[:, 0:1],
                    scalar2=None, op0=ALU.is_ge),
                 reads=[qsB, thrB_B], writes=[qsB], tag=f"{pfx}tge{i}")
            P.op("vector", lambda e, qs=qs, i=i, cols=cols: e.tensor_mul(
                    out=dest[:, i * cols:(i + 1) * cols], in0=qs[:, 0:cols], in1=sgn[:, 0:cols]),
                 reads=[qsB, sgnB], writes=[destB], tag=f"{pfx}tmq{i}")

    wfc16_h, wfc16_b = dram["WfcB16"]
    wpr16_h, wpr16_b = dram["WprB16"]
    quant(wfc_h, wfc_b, wfc16_h, wfc16_b, EC, HFF, wfq, wfqB, "f")
    quant(wpr_h, wpr_b, wpr16_h, wpr16_b, HC, D, wpq, wpqB, "p")

    def emit_ln_stats(src_t, srcB, sq_t, sqB_, nchunks, tag, delta=None):
        srcBl = srcB if isinstance(srcB, list) else [srcB]
        if delta is not None:
            for c in range(nchunks):
                rb_ = srcBl[min(c * len(srcBl) // nchunks, len(srcBl) - 1)]
                P.op("tensor",
                     lambda e, c=c: e.matmul(
                         pb[4][0][0:1, 0:2 * TT], onesc[:, 0:1],
                         ap3(src_t, c * TT, [[nchunks * TT, 128], [delta, 2], [1, TT]]),
                         start=(c == 0), stop=(c == nchunks - 1)),
                     reads=[rb_, sqB_, onescB], writes=[pb[4][1]], tag=f"{tag}_sum")
        else:
            for c in range(nchunks):
                rb_ = srcBl[min(c * len(srcBl) // nchunks, len(srcBl) - 1)]
                P.op("tensor",
                     lambda e, c=c: e.matmul(pb[4][0][0:1, 0:TT], onesc[:, 0:1],
                                             ap3(src_t, c * TT, [[nchunks * TT, 128], [1, TT]]),
                                             start=(c == 0), stop=(c == nchunks - 1)),
                     reads=[rb_, onescB], writes=[pb[4][1]], tag=f"{tag}_sum")
            for c in range(nchunks):
                P.op("tensor",
                     lambda e, c=c: e.matmul(pb[4][0][0:1, TT:2 * TT], onesc[:, 0:1],
                                             ap3(sq_t, c * TT, [[nchunks * TT, 128], [1, TT]]),
                                             start=(c == 0), stop=(c == nchunks - 1)),
                     reads=[sqB_, onescB], writes=[pb[4][1]], tag=f"{tag}_sumsq")
        n = nchunks * 128
        P.op("vector", lambda e, n=n: e.tensor_scalar_mul(out=stat[0:1, 0:2 * TT],
                                                          in0=pb[4][0][0:1, 0:2 * TT], scalar1=1.0 / n),
             reads=[pb[4][1]], writes=[statB], tag=f"{tag}_mu")
        P.op("vector", lambda e: e.tensor_mul(out=stat[0:1, 2 * TT:3 * TT],
                                              in0=stat[0:1, 0:TT], in1=stat[0:1, 0:TT]),
             reads=[statB], writes=[statB], tag=f"{tag}_musq")
        P.op("vector", lambda e: e.tensor_sub(out=stat[0:1, 2 * TT:3 * TT],
                                              in0=stat[0:1, TT:2 * TT], in1=stat[0:1, 2 * TT:3 * TT]),
             reads=[statB], writes=[statB], tag=f"{tag}_var")
        P.op("scalar", lambda e: e.activation(out=stat[0:1, 2 * TT:3 * TT],
                                              in_=stat[0:1, 2 * TT:3 * TT], func=AF.Ln,
                                              bias=lnc[0:1, 0:1]),
             reads=[statB, lncB], writes=[statB], tag=f"{tag}_lnv")
        P.op("scalar", lambda e: e.activation(out=stat[0:1, TT:2 * TT],
                                              in_=stat[0:1, 2 * TT:3 * TT], func=AF.Exp,
                                              scale=lnc[0:1, 1:2]),
             reads=[statB, lncB], writes=[statB], tag=f"{tag}_rstd")
        P.op("tensor", lambda e: e.matmul(pb[5][0][:, 0:2 * TT], onesr[0:1, :],
                                          stat[0:1, 0:2 * TT], start=True, stop=True),
             reads=[statB, onesrB], writes=[pb[5][1]], tag=f"{tag}_bmu")
        P.op("scalar", lambda e: e.copy(out=muB[:, :], in_=pb[5][0][:, 0:TT]),
             reads=[pb[5][1]], writes=[muBB], tag=f"{tag}_emu")
        P.op("scalar", lambda e: e.copy(out=rstdB[:, :], in_=pb[5][0][:, TT:2 * TT]),
             reads=[pb[5][1]], writes=[rstdBB], tag=f"{tag}_ers")

    def b3(t, nch, off=0):
        return ap3(t, off, [[TT, 128], [0, nch], [1, TT]])

    def v3(t, nch, off=0):
        return ap3(t, off, [[nch * TT, 128], [TT, nch], [1, TT]])

    def stA2(ti):
        s, tt = divmod(ti, NT // BL)
        t0 = tt * TT
        xt, xtB = xin[ti % 2]
        src = x1_h.ap()[s].rearrange("(c p) t -> p c t", p=128)[:, :, t0:t0 + TT]
        P.dma("sync", lambda e, d=xt, sr=src: e.dma_start(
                out=ap3(d, 0, [[EC * TT, 128], [TT, EC], [1, TT]]), in_=sr),
              reads=[x1_subs[s * 8 + t0 // 256]], writes=[xtB], tag=f"p2xld{ti}")
        P.op("vector", lambda e, xt=xt: e.tensor_copy(out=xb[:, :], in_=xt[:, :]),
             reads=[xtB], writes=[xbB], tag=f"p2xb{ti}")
        P.op("gpsimd", lambda e: e.tensor_mul(out=sqx[:, :], in0=xb[:, :], in1=xb[:, :]),
             reads=[xbB], writes=[sqxB], tag=f"p2sq{ti}")
        emit_ln_stats(xb, xbB, sqx, sqxB, EC, f"p2l1_{ti}")
        P.op("vector", lambda e, xt=xt: e.tensor_tensor(out=v3(s1, EC), in0=v3(xt, EC),
                                                        in1=b3(muB, EC), op=ALU.subtract),
             reads=[xtB, muBB], writes=[s1B], tag=f"p2ls{ti}")
        P.op("vector", lambda e: e.tensor_tensor(out=v3(xln, EC), in0=v3(s1, EC),
                                                 in1=b3(rstdB, EC), op=ALU.mult),
             reads=[s1B, rstdBB], writes=[xlnB], tag=f"p2lm{ti}")

    def stF2(ti):
        for hc in range(HC):
            bank = (hc // 2) % 2
            col = (hc % 2) * TT
            for ec in range(EC):
                P.op("tensor", lambda e, hc=hc, ec=ec, bank=bank, col=col: e.matmul(
                        pb[bank][0][:, col:col + TT],
                        wfq[:, ec * HFF + hc * 128: ec * HFF + hc * 128 + 128],
                        xln[:, ec * TT:(ec + 1) * TT],
                        start=(ec == 0), stop=(ec == EC - 1)),
                     reads=[wfqB, xlnB], writes=[pb[bank][1]], tag=f"p2fc{ti}_{hc}_{ec}")
            if hc % 2 == 1:
                P.op("scalar", lambda e, hc=hc, bank=bank: e.activation(
                        out=hb[:, (hc - 1) * TT:(hc + 1) * TT], in_=pb[bank][0][:, 0:2 * TT],
                        func=AF.Gelu),
                     reads=[pb[bank][1]], writes=[hbC[hc - 1], hbC[hc]], tag=f"p2ge{ti}_{hc}")

    def stH2(ti):
        P.op("gpsimd", lambda e: e.tensor_mul(out=h2b[:, :], in0=hb[:, :], in1=hb[:, :]),
             reads=list(hbC), writes=[h2bB], tag=f"p2hsq{ti}")
        emit_ln_stats(hb, hbC, h2b, h2bB, HC, f"p2l2_{ti}")
        for g in range(4):
            off = g * 8 * TT
            P.op("vector", lambda e, off=off: e.tensor_tensor(
                    out=v3(s1, EC),
                    in0=ap3(hb, off, [[HC * TT, 128], [TT, 8], [1, TT]]),
                    in1=b3(muB, EC), op=ALU.subtract),
                 reads=hbC[8 * g:8 * g + 8] + [muBB], writes=[s1B], tag=f"p2hs{ti}_{g}")
            P.op("vector", lambda e: e.tensor_tensor(
                    out=v3(s1, EC), in0=v3(s1, EC), in1=b3(rstdB, EC), op=ALU.mult),
                 reads=[s1B, rstdBB], writes=[s1B], tag=f"p2hm{ti}_{g}")
            for j in range(8):
                hc = g * 8 + j
                P.op("vector", lambda e, hc=hc, j=j: e.tensor_scalar(
                        out=hb[:, hc * TT:(hc + 1) * TT], in0=s1[:, j * TT:(j + 1) * TT],
                        scalar1=lnw[:, hc:hc + 1], scalar2=lnb[:, hc:hc + 1],
                        op0=ALU.mult, op1=ALU.add),
                     reads=[s1B, lnwB, lnbB], writes=[hbC[hc]], tag=f"p2ha{ti}_{hc}")

    def stC2(ti):
        s, tt = divmod(ti, NT // BL)
        t0 = tt * TT
        xt, xtB = xin[ti % 2]
        for dc in range(EC):
            bank = 2 + (dc // 2) % 2
            col = (dc % 2) * TT
            for hc in range(HC):
                P.op("tensor", lambda e, dc=dc, hc=hc, bank=bank, col=col: e.matmul(
                        pb[bank][0][:, col:col + TT],
                        wpq[:, hc * D + dc * 128: hc * D + dc * 128 + 128],
                        hb[:, hc * TT:(hc + 1) * TT],
                        start=(hc == 0), stop=(hc == HC - 1)),
                     reads=[wpqB, hbC[hc]], writes=[pb[bank][1]], tag=f"p2pj{ti}_{dc}_{hc}")
            if dc % 2 == 1:
                P.op("scalar", lambda e, dc=dc, bank=bank: e.copy(
                        out=xout[:, (dc - 1) * TT:(dc + 1) * TT], in_=pb[bank][0][:, 0:2 * TT]),
                     reads=[pb[bank][1]], writes=[xoutC[dc // 2]], tag=f"p2pe{ti}_{dc}")
        P.op("vector", lambda e: e.tensor_copy(out=xb[:, :], in_=xout[:, :]),
             reads=list(xoutC), writes=[xbB], tag=f"p2ob{ti}")
        P.op("gpsimd", lambda e: e.tensor_mul(out=sqx[:, :], in0=xb[:, :], in1=xb[:, :]),
             reads=[xbB], writes=[sqxB], tag=f"p2osq{ti}")
        emit_ln_stats(xb, xbB, sqx, sqxB, EC, f"p2l3_{ti}")
        P.op("vector", lambda e: e.tensor_tensor(out=v3(s1, EC), in0=v3(xout, EC),
                                                 in1=b3(muB, EC), op=ALU.subtract),
             reads=xoutC + [muBB], writes=[s1B], tag=f"p2os{ti}")
        P.op("vector", lambda e: e.tensor_tensor(out=v3(s1, EC), in0=v3(s1, EC),
                                                 in1=b3(rstdB, EC), op=ALU.mult),
             reads=[s1B, rstdBB], writes=[s1B], tag=f"p2om{ti}")
        P.op("vector", lambda e, xt=xt: e.tensor_add(out=xt[:, :], in0=xt[:, :], in1=s1[:, :]),
             reads=[s1B, xtB], writes=[xtB], tag=f"p2rs{ti}")
        dst = x2_h.ap()[s].rearrange("(c p) t -> p c t", p=128)[:, :, t0:t0 + TT]
        P.dma("gpsimd", lambda e, xt=xt, d=dst: e.dma_start(
                out=d, in_=ap3(xt, 0, [[EC * TT, 128], [TT, EC], [1, TT]])),
              reads=[xtB], writes=[x2_subs[s * 8 + t0 // 256]], tag=f"p2st{ti}")

    stA2(0)
    stF2(0)
    stA2(1)
    for ti in range(NT):
        stH2(ti)
        stC2(ti)
        if ti + 1 < NT:
            stF2(ti + 1)
        if ti + 2 < NT:
            stA2(ti + 2)
    A.reset(base)



def emit_phase3(nc, P, A, dram, pb):
    TT = 512
    NT = (BL * T) // TT
    NPR = H // 2

    x2_h, x2_b = dram["x2"]
    x2_subs = dram["x2_subs"]
    qh_h, qh_b = dram["qh"]
    qh_subs = dram["qh_subs"]
    kh_h, kh_b = dram["kh"]
    kh_subs = dram["kh_subs"]
    vh_h, vh_b = dram["vh"]
    vh_subs = dram["vh_subs"]
    wq_h, wq_b = dram["wqr"]
    wk_h, wk_b = dram["wkr"]
    wv_h, wv_b = dram["wvr"]

    base = A.mark()
    wqb, wqbB = A.alloc("p3_wq", 128, EC * D, BF16)
    wkb, wkbB = A.alloc("p3_wk", 128, EC * D, BF16)
    wvb, wvbB = A.alloc("p3_wv", 128, EC * D, BF16)
    xin = [A.alloc(f"p3_xin{i}", 128, EC * TT, F32) for i in range(2)]
    xb, xbB = A.alloc("p3_xb", 128, EC * TT, BF16)
    sqx, sqxB = A.alloc("p3_sqx", 128, EC * TT, BF16)
    xln, xlnB = A.alloc("p3_xln", 128, EC * TT, BF16)
    s1, s1B = A.alloc("p3_s1", 128, EC * TT, F32)
    qstg, qstgB = A.alloc("p3_qstg", 128, NPR * TT, BF16)
    kstg, kstgB = A.alloc("p3_kstg", 128, NPR * TT, BF16)
    vstg, vstgB = A.alloc("p3_vstg", 128, 4 * H * 65, BF16)
    qstgC = [P.buf("sbuf", qstgB.start + p * TT * 2, qstgB.start + (p + 1) * TT * 2,
                   f"p3_qstg_c{p}") for p in range(NPR)]
    kstgC = [P.buf("sbuf", kstgB.start + p * TT * 2, kstgB.start + (p + 1) * TT * 2,
                   f"p3_kstg_c{p}") for p in range(NPR)]
    vstgC = [P.buf("sbuf", vstgB.start + w * 520 * 2, vstgB.start + (w + 1) * 520 * 2,
                   f"p3_vstg_c{w}") for w in range(8)]
    muB, muBB = A.alloc("p3_muB", 128, TT, F32)
    rstdB, rstdBB = A.alloc("p3_rstdB", 128, TT, F32)
    stat, statB = A.alloc("p3_stat", 1, 6 * TT, F32)
    onesr, onesrB = A.alloc("p3_onesr", 1, 128, F32)
    onesc, onescB = A.alloc("p3_onesc", 128, 1, BF16)
    lnc, lncB = A.alloc("p3_lnc", 1, 2, F32)


    P.op("vector", lambda e: e.memset(onesr[:, :], 1.0), writes=[onesrB], tag="p3or")
    P.op("vector", lambda e: e.memset(onesc[:, :], 1.0), writes=[onescB], tag="p3oc")
    P.op("vector", lambda e: e.memset(lnc[0:1, 0:1], LN_EPS), writes=[lncB], tag="p3lne")
    P.op("vector", lambda e: e.memset(lnc[0:1, 1:2], -0.5), writes=[lncB], tag="p3lnh")
    P.op("vector", lambda e: e.memset(
            ap3(vstg, 64, [[4 * H * 65, 128], [1040, 4], [65, H], [1, 1]]), 1.0),
         writes=[vstgB], tag="p3vones")
    for (hb_, hbuf), (sb, sbuf) in (((wq_h, wq_b), (wqb, wqbB)),
                                    ((wk_h, wk_b), (wkb, wkbB)),
                                    ((wv_h, wv_b), (wvb, wvbB))):
        src = hb_.ap().rearrange("(c p) d -> p c d", p=128)
        P.dma("gpsimd", lambda e, d=sb, s=src: e.dma_start(
                out=ap3(d, 0, [[EC * D, 128], [D, EC], [1, D]]), in_=s),
              reads=[hbuf], writes=[sbuf], tag="p3wld")

    def emit_ln_stats(src_t, srcB, sq_t, sqB_, tag):
        for c in range(EC):
            P.op("tensor", lambda e, c=c: e.matmul(pb[4][0][0:1, 0:TT], onesc[:, 0:1],
                                                   ap3(src_t, c * TT, [[EC * TT, 128], [1, TT]]),
                                                   start=(c == 0), stop=(c == EC - 1)),
                 reads=[srcB, onescB], writes=[pb[4][1]], tag=f"{tag}_sum")
        for c in range(EC):
            P.op("tensor", lambda e, c=c: e.matmul(pb[5][0][0:1, 0:TT], onesc[:, 0:1],
                                                   ap3(sq_t, c * TT, [[EC * TT, 128], [1, TT]]),
                                                   start=(c == 0), stop=(c == EC - 1)),
                 reads=[sqB_, onescB], writes=[pb[5][1]], tag=f"{tag}_ssq")
        n = EC * 128
        P.op("vector", lambda e: e.tensor_scalar_mul(out=stat[0:1, 0:TT],
                                                     in0=pb[4][0][0:1, 0:TT], scalar1=1.0 / n),
             reads=[pb[4][1]], writes=[statB], tag=f"{tag}_mu")
        P.op("vector", lambda e: e.tensor_scalar_mul(out=stat[0:1, TT:2 * TT],
                                                     in0=pb[5][0][0:1, 0:TT], scalar1=1.0 / n),
             reads=[pb[5][1]], writes=[statB], tag=f"{tag}_ex2")
        P.op("vector", lambda e: e.tensor_mul(out=stat[0:1, 2 * TT:3 * TT],
                                              in0=stat[0:1, 0:TT], in1=stat[0:1, 0:TT]),
             reads=[statB], writes=[statB], tag=f"{tag}_musq")
        P.op("vector", lambda e: e.tensor_sub(out=stat[0:1, 3 * TT:4 * TT],
                                              in0=stat[0:1, TT:2 * TT], in1=stat[0:1, 2 * TT:3 * TT]),
             reads=[statB], writes=[statB], tag=f"{tag}_var")
        P.op("scalar", lambda e: e.activation(out=stat[0:1, 3 * TT:4 * TT],
                                              in_=stat[0:1, 3 * TT:4 * TT], func=AF.Ln,
                                              bias=lnc[0:1, 0:1]),
             reads=[statB, lncB], writes=[statB], tag=f"{tag}_lnv")
        P.op("scalar", lambda e: e.activation(out=stat[0:1, 5 * TT:6 * TT],
                                              in_=stat[0:1, 3 * TT:4 * TT], func=AF.Exp,
                                              scale=lnc[0:1, 1:2]),
             reads=[statB, lncB], writes=[statB], tag=f"{tag}_rstd")
        P.op("tensor", lambda e: e.matmul(pb[6][0][:, 0:TT], onesr[0:1, :],
                                          stat[0:1, 0:TT], start=True, stop=True),
             reads=[statB, onesrB], writes=[pb[6][1]], tag=f"{tag}_bmu")
        P.op("tensor", lambda e: e.matmul(pb[7][0][:, 0:TT], onesr[0:1, :],
                                          stat[0:1, 5 * TT:6 * TT], start=True, stop=True),
             reads=[statB, onesrB], writes=[pb[7][1]], tag=f"{tag}_brs")
        P.op("scalar", lambda e: e.copy(out=muB[:, :], in_=pb[6][0][:, 0:TT]),
             reads=[pb[6][1]], writes=[muBB], tag=f"{tag}_emu")
        P.op("scalar", lambda e: e.copy(out=rstdB[:, :], in_=pb[7][0][:, 0:TT]),
             reads=[pb[7][1]], writes=[rstdBB], tag=f"{tag}_ers")

    for ti in range(NT):
        s, tt = divmod(ti, NT // BL)
        t0 = tt * TT
        xt, xtB = xin[ti % 2]
        src = x2_h.ap()[s].rearrange("(c p) t -> p c t", p=128)[:, :, t0:t0 + TT]
        P.dma("sync", lambda e, d=xt, sr=src: e.dma_start(
                out=ap3(d, 0, [[EC * TT, 128], [TT, EC], [1, TT]]), in_=sr),
              reads=[x2_subs[s * 8 + t0 // 256], x2_subs[s * 8 + t0 // 256 + 1]],
              writes=[xtB], tag=f"p3xld{ti}")
        P.op("vector", lambda e, xt=xt: e.tensor_copy(out=xb[:, :], in_=xt[:, :]),
             reads=[xtB], writes=[xbB], tag=f"p3xb{ti}")
        P.op("scalar", lambda e: e.activation(out=sqx[:, :], in_=xb[:, :], func=AF.Square),
             reads=[xbB], writes=[sqxB], tag=f"p3sq{ti}")
        emit_ln_stats(xb, xbB, sqx, sqxB, f"p3l_{ti}")
        P.op("vector", lambda e, xt=xt: e.tensor_tensor(
                out=ap3(s1, 0, [[EC * TT, 128], [TT, EC], [1, TT]]),
                in0=ap3(xt, 0, [[EC * TT, 128], [TT, EC], [1, TT]]),
                in1=ap3(muB, 0, [[TT, 128], [0, EC], [1, TT]]), op=ALU.subtract),
             reads=[xtB, muBB], writes=[s1B], tag=f"p3ls{ti}")
        P.op("vector", lambda e: e.tensor_tensor(
                out=ap3(xln, 0, [[EC * TT, 128], [TT, EC], [1, TT]]),
                in0=ap3(s1, 0, [[EC * TT, 128], [TT, EC], [1, TT]]),
                in1=ap3(rstdB, 0, [[TT, 128], [0, EC], [1, TT]]), op=ALU.mult),
             reads=[s1B, rstdBB], writes=[xlnB], tag=f"p3lm{ti}")
        for pr in range(NPR):
            bank = pr % 2
            for ec in range(EC):
                P.op("tensor", lambda e, pr=pr, ec=ec, bank=bank: e.matmul(
                        pb[bank][0][:, 0:TT],
                        wqb[:, ec * D + pr * 128: ec * D + pr * 128 + 128],
                        xln[:, ec * TT:(ec + 1) * TT],
                        start=(ec == 0), stop=(ec == EC - 1)),
                     reads=[wqbB, xlnB], writes=[pb[bank][1]], tag=f"p3q{ti}_{pr}_{ec}")
            P.op("vector", lambda e, pr=pr, bank=bank: e.tensor_scalar_mul(
                    out=qstg[:, pr * TT:(pr + 1) * TT], in0=pb[bank][0][:, 0:TT],
                    scalar1=0.125),
                 reads=[pb[bank][1]], writes=[qstgC[pr]], tag=f"p3qe{ti}_{pr}")
            bank = 2 + pr % 2
            for ec in range(EC):
                P.op("tensor", lambda e, pr=pr, ec=ec, bank=bank: e.matmul(
                        pb[bank][0][:, 0:TT],
                        wkb[:, ec * D + pr * 128: ec * D + pr * 128 + 128],
                        xln[:, ec * TT:(ec + 1) * TT],
                        start=(ec == 0), stop=(ec == EC - 1)),
                     reads=[wkbB, xlnB], writes=[pb[bank][1]], tag=f"p3k{ti}_{pr}_{ec}")
            P.op("scalar", lambda e, pr=pr, bank=bank: e.copy(
                    out=kstg[:, pr * TT:(pr + 1) * TT], in_=pb[bank][0][:, 0:TT]),
                 reads=[pb[bank][1]], writes=[kstgC[pr]], tag=f"p3ke{ti}_{pr}")
        for j in range(TT // 128):
            for half in range(2):
                bank = 2 * half
                for ec in range(EC):
                    P.op("tensor", lambda e, j=j, half=half, ec=ec, bank=bank: e.matmul(
                            pb[bank][0][:, 0:512],
                            xln[:, ec * TT + j * 128: ec * TT + j * 128 + 128],
                            wvb[:, ec * D + half * 512: ec * D + half * 512 + 512],
                            start=(ec == 0), stop=(ec == EC - 1)),
                         reads=[wvbB, xlnB], writes=[pb[bank][1]], tag=f"p3v{ti}_{j}_{half}_{ec}")
                P.op("vector", lambda e, j=j, half=half, bank=bank: e.tensor_copy(
                        out=ap3(vstg, j * 1040 + half * 8 * 65, [[4 * H * 65, 128], [65, 8], [1, 64]]),
                        in_=ap3(pb[bank][0], 0, [[512, 128], [64, 8], [1, 64]])),
                     reads=[pb[bank][1]], writes=[vstgC[2 * j + half]], tag=f"p3ve{ti}_{j}_{half}")
        qdst = qh_h.ap()[s].rearrange("(h2 hh) d t -> (hh d) h2 t", hh=2)[:, :, t0:t0 + TT]
        P.dma("gpsimd", lambda e, d=qdst: e.dma_start(
                out=d, in_=ap3(qstg, 0, [[NPR * TT, 128], [TT, NPR], [1, TT]])),
              reads=[qstgB], writes=[qh_subs[s * 4 + t0 // 512]], tag=f"p3qs{ti}")
        kdst = kh_h.ap()[s].rearrange("(h2 hh) d t -> (hh d) h2 t", hh=2)[:, :, t0:t0 + TT]
        P.dma("gpsimd", lambda e, d=kdst: e.dma_start(
                out=d, in_=ap3(kstg, 0, [[NPR * TT, 128], [TT, NPR], [1, TT]])),
              reads=[kstgB], writes=[kh_subs[s * 4 + t0 // 512]], tag=f"p3ks{ti}")
        vdst = vh_h.ap()[s].rearrange("j p c -> p j c")[:, 4 * tt:4 * tt + 4, :]
        P.dma("gpsimd", lambda e, d=vdst: e.dma_start(
                out=d, in_=ap3(vstg, 0, [[4 * H * 65, 128], [1040, 4], [1, 1040]])),
              reads=[vstgB], writes=[vh_subs[s * 4 + t0 // 512]], tag=f"p3vs{ti}")
    A.reset(base)



def emit_phase4(nc, P, A, dram, pb):
    QB = 512
    NQ = T // QB
    x2_h, x2_b = dram["x2"]
    x2_subs = dram["x2_subs"]
    qh_h, qh_b = dram["qh"]
    qh_subs = dram["qh_subs"]
    kh_h, kh_b = dram["kh"]
    kh_subs = dram["kh_subs"]
    vh_h, vh_b = dram["vh"]
    vh_subs = dram["vh_subs"]
    out_h, out_b = dram["out"]
    wo_h, wo_b = dram["woT"]
    bw_h, bw_b = dram["bwT"]
    bb_h, bb_b = dram["bb"]
    la_h, la_b = dram["loraA"]
    lb_h, lb_b = dram["loraB"]
    rw_h, rw_b = dram["rwT"]
    rb_h, rb_b = dram["rb"]

    base = A.mark()
    wob, wobB = A.alloc("p4_wo", 128, EC * D, BF16)
    bwb, bwbB = A.alloc("p4_bw", 128, EC * D, BF16)
    bbt, bbtB = A.alloc("p4_bb", 128, EC, F32)
    lab, labB = A.alloc("p4_la", 128, EC * 8, BF16)
    lbb, lbbB = A.alloc("p4_lb", 64, D, BF16)
    rwt, rwtB = A.alloc("p4_rw", 128, EC * 8, BF16)
    rbt, rbtB = A.alloc("p4_rb", 8, 1, F32)
    vse, vseB = A.alloc("p4_vse", 128, (T // 128) * H * 65, BF16)
    khw = [A.alloc(f"p4_khw{i}", 65, T, BF16) for i in range(3)]
    qa, qaB_whole = A.alloc("p4_qa", 65, H * QB, BF16)
    qaB = [P.buf("sbuf", qaB_whole.start + h * QB * 2,
                 qaB_whole.start + (h + 1) * QB * 2, f"p4_qa_h{h}") for h in range(H)]
    x2t, x2tB = A.alloc("p4_x2t", 128, EC * QB, F32)
    ptb = [A.alloc(f"p4_ptb{i}", 128, QB, BF16) for i in range(8)]
    otb, otbB = A.alloc("p4_otb", 128, EC * QB, BF16)
    m4 = A.mark()
    aout, aoutB = A.alloc("p4_aout", 128, EC * QB, F32)
    A.reset(m4)
    fout, foutB = A.alloc("p4_fout", 128, EC * QB, F32)
    x3ln, x3lnB = A.alloc("p4_x3ln", 128, EC * QB, BF16)
    xb, xbB = A.alloc("p4_xb", 128, EC * QB, BF16)
    sqx, sqxB = A.alloc("p4_sqx", 128, EC * QB, BF16)
    s1, s1B = A.alloc("p4_s1", 128, EC * QB // 2, F32)
    muB, muBB = A.alloc("p4_muB", 128, QB, F32)
    rstdB, rstdBB = A.alloc("p4_rstdB", 128, QB, F32)
    stat, statB = A.alloc("p4_stat", 1, 3 * QB, F32)
    onesr, onesrB = A.alloc("p4_onesr", 1, 128, F32)
    onesrb, onesrbB = A.alloc("p4_onesrb", 1, 64, BF16)
    onesc, onescB = A.alloc("p4_onesc", 128, 1, BF16)
    lnc, lncB = A.alloc("p4_lnc", 1, 2, F32)
    rbc = [A.alloc(f"p4_rbc{i}", 64, QB, F32) for i in range(2)]
    srow = [A.alloc(f"p4_srow{i}", 1, QB, BF16) for i in range(2)]
    _kq = A.alloc("p4_ksqb", 64, 512, BF16)
    ksqb = [_kq, _kq]
    qsq4 = A.alloc("p4_qsq4", 64, 4 * QB, BF16)
    mkrow = [A.alloc(f"p4_mkrow{i}", 1, 8, F32) for i in range(2)]
    mks, mksB = A.alloc("p4_mks", 1, 2 * H, F32)
    ones64, ones64B = A.alloc("p4_ones64", 64, 1, BF16)
    i128, i128B = A.alloc("p4_i128", 128, 128, BF16)
    selE, selEB = A.alloc("p4_selE", 8, 64, BF16)
    selR, selRB = A.alloc("p4_selR", 8, 64, BF16)
    it1, it1B = A.alloc("p4_it1", 128, 64, F32)
    itp, itpB = A.alloc("p4_itp", 128, 1, F32)
    gb, gbB = A.alloc("p4_gb", 128, 4 * 8, BF16)
    gsum, gsumB = A.alloc("p4_gsum", 128, 2, F32)
    gex, gexB = A.alloc("p4_gex", 128, 8, F32)
    rlb, rlbB = A.alloc("p4_rlb", 8, QB, BF16)
    gTt, gTtB = A.alloc("p4_gT", 8, QB, BF16)
    mb, mbB = A.alloc("p4_mb", 8, QB, BF16)
    geb, gebB = A.alloc("p4_geb", 64, QB, BF16)
    meb, mebB = A.alloc("p4_meb", 64, QB, BF16)
    p64, p64B = A.alloc("p4_p64", 64, QB, BF16)


    P.op("vector", lambda e: e.memset(onesr[:, :], 1.0), writes=[onesrB], tag="p4or")
    P.op("vector", lambda e: e.memset(onesrb[:, :], 1.0), writes=[onesrbB], tag="p4orb")
    P.op("vector", lambda e: e.memset(ones64[:, :], 1.0), writes=[ones64B], tag="p4o64")
    P.op("vector", lambda e: e.memset(onesc[:, :], 1.0), writes=[onescB], tag="p4oc")
    P.op("vector", lambda e: e.memset(lnc[0:1, 0:1], LN_EPS), writes=[lncB], tag="p4lne")
    P.op("vector", lambda e: e.memset(lnc[0:1, 1:2], -0.5), writes=[lncB], tag="p4lnh")
    P.op("vector", lambda e: e.memset(i128[:, :], 1.0), writes=[i128B], tag="p4i1")
    P.op("gpsimd", lambda e: e.affine_select(
            out=i128[:, :], in_=i128[:, :], pattern=[[-1, 128]],
            compare_op=ALU.is_equal, fill=0.0, base=0, channel_multiplier=1),
         reads=[i128B], writes=[i128B], tag="p4i2")
    P.op("gpsimd", lambda e: e.iota(it1[0:8, :], [[1, 8], [0, 8]],
                                    channel_multiplier=0,
                                    allow_small_or_imprecise_dtypes=True),
         writes=[it1B], tag="p4it1")
    P.op("gpsimd", lambda e: e.iota(itp[0:8, :], [[0, 1]], channel_multiplier=1,
                                    allow_small_or_imprecise_dtypes=True),
         writes=[itpB], tag="p4itp")
    P.op("vector", lambda e: e.tensor_scalar(out=selE[:, :], in0=it1[0:8, :],
                                             scalar1=itp[0:8, :], scalar2=None,
                                             op0=ALU.is_equal),
         reads=[it1B, itpB], writes=[selEB], tag="p4se")
    P.op("gpsimd", lambda e: e.iota(it1[0:8, :], [[0, 8], [1, 8]],
                                    channel_multiplier=0,
                                    allow_small_or_imprecise_dtypes=True),
         writes=[it1B], tag="p4it2")
    P.op("vector", lambda e: e.tensor_scalar(out=selR[:, :], in0=it1[0:8, :],
                                             scalar1=itp[0:8, :], scalar2=None,
                                             op0=ALU.is_equal),
         reads=[it1B, itpB], writes=[selRB], tag="p4sr")
    for (hb_, hbuf), (sb, sbuf) in (((wo_h, wo_b), (wob, wobB)),
                                    ((bw_h, bw_b), (bwb, bwbB))):
        src = hb_.ap().rearrange("(c p) d -> p c d", p=128)
        P.dma("gpsimd", lambda e, d=sb, s=src: e.dma_start(
                out=ap3(d, 0, [[EC * D, 128], [D, EC], [1, D]]), in_=s),
              reads=[hbuf], writes=[sbuf], tag="p4wld")
    P.dma("sync", lambda e: e.dma_start(out=bbt[:, :], in_=bb_h.ap()),
          reads=[bb_b], writes=[bbtB], tag="p4bb")
    src = la_h.ap().rearrange("(c p) r -> p c r", p=128)
    P.dma("gpsimd", lambda e, s=src: e.dma_start(
            out=ap3(lab, 0, [[EC * 8, 128], [8, EC], [1, 8]]), in_=s),
          reads=[la_b], writes=[labB], tag="p4la")
    P.dma("gpsimd", lambda e: e.dma_start(out=lbb[:, :], in_=lb_h.ap()),
          reads=[lb_b], writes=[lbbB], tag="p4lb")
    src = rw_h.ap().rearrange("(c p) r -> p c r", p=128)
    P.dma("gpsimd", lambda e, s=src: e.dma_start(
            out=ap3(rwt, 0, [[EC * 8, 128], [8, EC], [1, 8]]), in_=s),
          reads=[rw_b], writes=[rwtB], tag="p4rw")
    P.dma("sync", lambda e: e.dma_start(out=rbt[:, :], in_=rb_h.ap()),
          reads=[rb_b], writes=[rbtB], tag="p4rb")
    for i in range(3):
        P.op("vector", lambda e, i=i: e.memset(khw[i][0][64:65, :], 1.0),
             writes=[khw[i][1]], tag=f"p4ko{i}")

    def emit_ln_stats(src_t, srcB, sq_t, sqB_, tag):
        for c in range(EC):
            P.op("tensor", lambda e, c=c: e.matmul(pb[6][0][0:1, 0:QB], onesc[:, 0:1],
                                                   ap3(src_t, c * QB, [[EC * QB, 128], [1, QB]]),
                                                   start=(c == 0), stop=(c == EC - 1)),
                 reads=[srcB, onescB], writes=[pb[6][1]], tag=f"{tag}_sum")
        for c in range(EC):
            P.op("tensor", lambda e, c=c: e.matmul(pb[7][0][0:1, 0:QB], onesc[:, 0:1],
                                                   ap3(sq_t, c * QB, [[EC * QB, 128], [1, QB]]),
                                                   start=(c == 0), stop=(c == EC - 1)),
                 reads=[sqB_, onescB], writes=[pb[7][1]], tag=f"{tag}_ssq")
        n = EC * 128
        P.op("vector", lambda e: e.tensor_scalar_mul(out=stat[0:1, 0:QB],
                                                     in0=pb[6][0][0:1, 0:QB], scalar1=1.0 / n),
             reads=[pb[6][1]], writes=[statB], tag=f"{tag}_mu")
        P.op("vector", lambda e: e.tensor_scalar_mul(out=stat[0:1, QB:2 * QB],
                                                     in0=pb[7][0][0:1, 0:QB], scalar1=1.0 / n),
             reads=[pb[7][1]], writes=[statB], tag=f"{tag}_ex2")
        P.op("vector", lambda e: e.tensor_mul(out=stat[0:1, 2 * QB:3 * QB],
                                              in0=stat[0:1, 0:QB], in1=stat[0:1, 0:QB]),
             reads=[statB], writes=[statB], tag=f"{tag}_musq")
        P.op("vector", lambda e: e.tensor_sub(out=stat[0:1, QB:2 * QB],
                                              in0=stat[0:1, QB:2 * QB], in1=stat[0:1, 2 * QB:3 * QB]),
             reads=[statB], writes=[statB], tag=f"{tag}_var")
        P.op("scalar", lambda e: e.activation(out=stat[0:1, QB:2 * QB],
                                              in_=stat[0:1, QB:2 * QB], func=AF.Ln,
                                              bias=lnc[0:1, 0:1]),
             reads=[statB, lncB], writes=[statB], tag=f"{tag}_lnv")
        P.op("scalar", lambda e: e.activation(out=stat[0:1, 2 * QB:3 * QB],
                                              in_=stat[0:1, QB:2 * QB], func=AF.Exp,
                                              scale=lnc[0:1, 1:2]),
             reads=[statB, lncB], writes=[statB], tag=f"{tag}_rstd")
        P.op("tensor", lambda e: e.matmul(pb[6][0][:, 0:QB], onesr[0:1, :],
                                          stat[0:1, 0:QB], start=True, stop=True),
             reads=[statB, onesrB], writes=[pb[6][1]], tag=f"{tag}_bmu")
        P.op("tensor", lambda e: e.matmul(pb[7][0][:, 0:QB], onesr[0:1, :],
                                          stat[0:1, 2 * QB:3 * QB], start=True, stop=True),
             reads=[statB, onesrB], writes=[pb[7][1]], tag=f"{tag}_brs")
        P.op("scalar", lambda e: e.copy(out=muB[:, :], in_=pb[6][0][:, 0:QB]),
             reads=[pb[6][1]], writes=[muBB], tag=f"{tag}_emu")
        P.op("scalar", lambda e: e.copy(out=rstdB[:, :], in_=pb[7][0][:, 0:QB]),
             reads=[pb[7][1]], writes=[rstdBB], tag=f"{tag}_ers")

    def ln_apply(src_t, srcB, dst_t, dstB, tag):
        for g in range(2):
            off = g * 4 * QB
            P.op("vector", lambda e, off=off, src_t=src_t: e.tensor_tensor(
                    out=ap3(s1, 0, [[4 * QB, 128], [QB, 4], [1, QB]]),
                    in0=ap3(src_t, off, [[EC * QB, 128], [QB, 4], [1, QB]]),
                    in1=ap3(muB, 0, [[QB, 128], [0, 4], [1, QB]]), op=ALU.subtract),
                 reads=[srcB, muBB], writes=[s1B], tag=f"{tag}_s{g}")
            P.op("vector", lambda e, off=off, dst_t=dst_t: e.tensor_tensor(
                    out=ap3(dst_t, off, [[EC * QB, 128], [QB, 4], [1, QB]]),
                    in0=ap3(s1, 0, [[4 * QB, 128], [QB, 4], [1, QB]]),
                    in1=ap3(rstdB, 0, [[QB, 128], [0, 4], [1, QB]]), op=ALU.mult),
                 reads=[s1B, rstdBB], writes=[dstB], tag=f"{tag}_m{g}")

    for s in range(BL):
        vsrc = vh_h.ap()[s].rearrange("j p c -> p j c")
        P.dma("sync", lambda e, sr=vsrc: e.dma_start(
                out=ap3(vse, 0, [[(T // 128) * H * 65, 128], [1040, T // 128], [1, 1040]]), in_=sr),
              reads=[vh_subs[s * 4 + i] for i in range(4)], writes=[vseB], tag=f"p4vld{s}")
        for qb in range(NQ):
            blk = f"b{s}_{qb}"
            qsrc = qh_h.ap()[s].rearrange("h d t -> d h t")[:, :, qb * QB:(qb + 1) * QB]
            P.dma("sync", lambda e, sr=qsrc: e.dma_start(
                    out=ap3(qa, 0, [[H * QB, 64], [QB, H], [1, QB]]), in_=sr),
                  reads=[qh_subs[s * 4 + qb]], writes=[qaB_whole], tag=f"{blk}_qld")
            xsrc = x2_h.ap()[s].rearrange("(c p) t -> p c t", p=128)[:, :, qb * QB:(qb + 1) * QB]
            P.dma("sync", lambda e, sr=xsrc: e.dma_start(
                    out=ap3(x2t, 0, [[EC * QB, 128], [QB, EC], [1, QB]]), in_=sr),
                  reads=[x2_subs[s * 8 + qb * 2], x2_subs[s * 8 + qb * 2 + 1]],
                  writes=[x2tB], tag=f"{blk}_xld")
            mext = (qb + 1) * 512
            nk128 = mext // 128

            def kload(h):
                kb, kbB = khw[h % 3]
                ksrc = kh_h.ap()[s][h][:, 0:T]
                P.dma("sync", lambda e, sr=ksrc, kb=kb: e.dma_start(
                        out=kb[0:64, 0:T], in_=sr),
                      reads=[kh_subs[s * 4 + i] for i in range(4)],
                      writes=[kbB], tag=f"{blk}_kld{h}")

            def knorm(h):
                kb, kbB = khw[h % 3]
                ks, ksB = ksqb[h % 2]
                mk, mkB = mkrow[h % 2]
                for j in range(T // 512):
                    P.op("scalar", lambda e, j=j, kb=kb, ks=ks: e.activation(
                            out=ks[:, :], in_=kb[0:64, j * 512:(j + 1) * 512],
                            func=AF.Square),
                         reads=[kbB], writes=[ksB], tag=f"{blk}_ksq{h}_{j}")
                    P.op("tensor", lambda e, ks=ks: e.matmul(
                            pb[1][0][0:1, 0:512], ones64[:, 0:1], ks[:, :],
                            start=True, stop=True),
                         reads=[ksB, ones64B], writes=[pb[1][1]], tag=f"{blk}_kmm{h}_{j}")
                    P.op("vector", lambda e, j=j, mk=mk: e.reduce_max(
                            out=mk[0:1, j:j + 1], in_=pb[1][0][0:1, 0:512], axis=AX.X),
                         reads=[pb[1][1]], writes=[mkB], tag=f"{blk}_kmx{h}_{j}")
                P.op("vector", lambda e, mk=mk, h=h: e.tensor_reduce(
                        out=mks[0:1, s * H + h: s * H + h + 1], in_=mk[0:1, 0:4],
                        axis=AX.X, op=ALU.max),
                     reads=[mkB], writes=[mksB], tag=f"{blk}_kmc{h}")

            def stHq(h4):
                qs, qsB = qsq4
                P.op("scalar", lambda e, h4=h4, qs=qs: e.activation(
                        out=qs[:, :], in_=qa[0:64, h4 * QB: (h4 + 4) * QB], func=AF.Square),
                     reads=[qaB[h4], qaB[h4 + 1], qaB[h4 + 2], qaB[h4 + 3]],
                     writes=[qsB], tag=f"{blk}_qsq{h4}")
                for h in range(h4, h4 + 4):
                    bank = h % 2
                    P.op("tensor", lambda e, qs=qs, h=h, h4=h4, bank=bank: e.matmul(
                            pb[bank][0][0:1, 0:QB], ones64[:, 0:1],
                            qs[:, (h - h4) * QB: (h - h4 + 1) * QB],
                            start=True, stop=True),
                         reads=[qsB, ones64B], writes=[pb[bank][1]], tag=f"{blk}_qmm{h}")
                    P.op("vector", lambda e, h=h, bank=bank: e.tensor_scalar_mul(
                            out=qa[64:65, h * QB: h * QB + QB],
                            in0=pb[bank][0][0:1, 0:QB],
                            scalar1=mks[0:1, s * H + h: s * H + h + 1]),
                         reads=[pb[bank][1], mksB], writes=[qaB[h]], tag=f"{blk}_qm2{h}")
                P.op("scalar", lambda e, h4=h4: e.activation(
                        out=qa[64:65, h4 * QB: (h4 + 4) * QB],
                        in_=qa[64:65, h4 * QB: (h4 + 4) * QB], func=AF.Sqrt),
                     reads=[qaB[h4], qaB[h4 + 1], qaB[h4 + 2], qaB[h4 + 3]],
                     writes=[qaB[h4], qaB[h4 + 1], qaB[h4 + 2], qaB[h4 + 3]],
                     tag=f"{blk}_qsr{h4}")
                P.op("vector", lambda e, h4=h4: e.tensor_scalar_mul(
                        out=qa[64:65, h4 * QB: (h4 + 4) * QB],
                        in0=qa[64:65, h4 * QB: (h4 + 4) * QB], scalar1=-1.0),
                     reads=[qaB[h4], qaB[h4 + 1], qaB[h4 + 2], qaB[h4 + 3]],
                     writes=[qaB[h4], qaB[h4 + 1], qaB[h4 + 2], qaB[h4 + 3]],
                     tag=f"{blk}_qng{h4}")

            def stB_st(h, k0, k1):
                kb, kbB = khw[h % 3]
                for km in range(k0, k1):
                    bank = 2 + km % 2
                    P.op("tensor", lambda e, h=h, km=km, kb=kb, bank=bank: e.matmul(
                            pb[bank][0][:, 0:QB],
                            kb[0:65, km * 128: km * 128 + 128],
                            qa[0:65, h * QB: h * QB + QB],
                            start=True, stop=True),
                         reads=[qaB[h], kbB], writes=[pb[bank][1]], tag=f"{blk}_st{h}_{km}")
                    pt, ptB = ptb[km % 8]
                    P.op("scalar", lambda e, pt=pt, bank=bank: e.activation(
                            out=pt[:, :], in_=pb[bank][0][:, 0:QB], func=AF.Exp),
                         reads=[pb[bank][1]], writes=[ptB], tag=f"{blk}_pe{h}_{km}")
                    if km >= 4 * qb:
                        P.op("gpsimd", lambda e, pt=pt, km=km, qb=qb: e.affine_select(
                                out=pt[:, :], in_=pt[:, :], pattern=[[1, QB]],
                                compare_op=ALU.is_ge, fill=0.0,
                                base=qb * 512 - km * 128, channel_multiplier=-1),
                             reads=[ptB], writes=[ptB], tag=f"{blk}_msk{h}_{km}")

            def stB_pv(h, k0, k1):
                ob = pb[4 + h % 2]
                for km in range(k0, k1):
                    pt, ptB = ptb[km % 8]
                    P.op("tensor", lambda e, h=h, km=km, pt=pt, nk=nk128, ob=ob: e.matmul(
                            ob[0][0:65, 0:QB],
                            vse[:, (km * H + h) * 65: (km * H + h) * 65 + 65],
                            pt[:, :],
                            start=(km == 0), stop=(km == nk - 1)),
                         reads=[vseB, ptB], writes=[ob[1]], tag=f"{blk}_pv{h}_{km}")

            def stB_fin(h):
                ob = pb[4 + h % 2]
                rc, rcB = rbc[h % 2]
                sr_, srB_ = srow[h % 2]
                def _cast(e, ob=ob, sr_=sr_):
                    with nc.allow_low_precision(reason="bf16 softmax normalizer"):
                        return e.copy(out=sr_[0:1, 0:QB], in_=ob[0][64:65, 0:QB])
                P.op("scalar", _cast, reads=[ob[1]], writes=[srB_], tag=f"{blk}_rs{h}")
                P.op("tensor", lambda e, sr_=sr_: e.matmul(
                        pb[7][0][0:64, 0:QB], onesrb[0:1, :], sr_[0:1, 0:QB],
                        start=True, stop=True),
                     reads=[srB_, onesrbB], writes=[pb[7][1]], tag=f"{blk}_rb{h}")
                P.op("scalar", lambda e, rc=rc: e.activation(out=rc[:, :],
                                                             in_=pb[7][0][0:64, 0:QB],
                                                             func=AF.Ln),
                     reads=[pb[7][1]], writes=[rcB], tag=f"{blk}_rbe{h}")
                P.op("vector", lambda e, rc=rc: e.tensor_scalar_mul(
                        out=rc[:, :], in0=rc[:, :], scalar1=-1.0),
                     reads=[rcB], writes=[rcB], tag=f"{blk}_rbn{h}")
                P.op("scalar", lambda e, rc=rc: e.activation(out=rc[:, :], in_=rc[:, :],
                                                             func=AF.Exp),
                     reads=[rcB], writes=[rcB], tag=f"{blk}_rbx{h}")
                P.op("vector", lambda e, h=h, ob=ob, rc=rc: e.tensor_tensor(
                        out=otb[64 * (h % 2):64 * (h % 2) + 64, (h // 2) * QB:(h // 2 + 1) * QB],
                        in0=ob[0][0:64, 0:QB], in1=rc[:, :], op=ALU.mult),
                     reads=[ob[1], rcB], writes=[otbB], tag=f"{blk}_ot{h}")

            kload(0)
            kload(1)
            if qb == 0:
                for h in range(H):
                    if h + 2 < H:
                        kload(h + 2)
                    knorm(h)
                kload(0)
                kload(1)
            for h4 in range(0, H, 4):
                stHq(h4)
            for h in range(H):
                if h + 2 < H:
                    kload(h + 2)
                nh = min(nk128, 8)
                stB_st(h, 0, nh)
                stB_pv(h, 0, nh)
                if nk128 > 8:
                    stB_st(h, 8, nk128)
                    stB_pv(h, 8, nk128)
                if h > 0:
                    stB_fin(h - 1)
            stB_fin(H - 1)
            for ecc in range(EC):
                bank = ecc % 2
                for hdc in range(EC):
                    P.op("tensor", lambda e, ecc=ecc, hdc=hdc, bank=bank: e.matmul(
                            pb[bank][0][:, 0:QB],
                            wob[:, hdc * D + ecc * 128: hdc * D + ecc * 128 + 128],
                            otb[:, hdc * QB:(hdc + 1) * QB],
                            start=(hdc == 0), stop=(hdc == EC - 1)),
                         reads=[wobB, otbB], writes=[pb[bank][1]], tag=f"{blk}_op{ecc}_{hdc}")
                P.op("scalar", lambda e, ecc=ecc, bank=bank: e.copy(
                        out=aout[:, ecc * QB:(ecc + 1) * QB], in_=pb[bank][0][:, 0:QB]),
                     reads=[pb[bank][1]], writes=[aoutB], tag=f"{blk}_oe{ecc}")
            P.op("vector", lambda e: e.tensor_copy(out=xb[:, :], in_=aout[:, :]),
                 reads=[aoutB], writes=[xbB], tag=f"{blk}_ab")
            P.op("scalar", lambda e: e.activation(out=sqx[:, :], in_=xb[:, :], func=AF.Square),
                 reads=[xbB], writes=[sqxB], tag=f"{blk}_asq")
            emit_ln_stats(xb, xbB, sqx, sqxB, f"{blk}_l2")
            ln_apply(aout, aoutB, aout, aoutB, f"{blk}_l2a")
            P.op("vector", lambda e: e.tensor_add(out=x2t[:, :], in0=x2t[:, :], in1=aout[:, :]),
                 reads=[aoutB, x2tB], writes=[x2tB], tag=f"{blk}_rs1")
            P.op("vector", lambda e: e.tensor_copy(out=xb[:, :], in_=x2t[:, :]),
                 reads=[x2tB], writes=[xbB], tag=f"{blk}_fb")
            P.op("scalar", lambda e: e.activation(out=sqx[:, :], in_=xb[:, :], func=AF.Square),
                 reads=[xbB], writes=[sqxB], tag=f"{blk}_fsq")
            emit_ln_stats(xb, xbB, sqx, sqxB, f"{blk}_l3")
            ln_apply(x2t, x2tB, x3ln, x3lnB, f"{blk}_l3a")
            for ec in range(EC):
                P.op("tensor", lambda e, ec=ec: e.matmul(
                        pb[5][0][0:8, 0:QB], rwt[:, ec * 8:(ec + 1) * 8],
                        x3ln[:, ec * QB:(ec + 1) * QB],
                        start=(ec == 0), stop=(ec == EC - 1)),
                     reads=[rwtB, x3lnB], writes=[pb[5][1]], tag=f"{blk}_ro{ec}")
            P.op("vector", lambda e: e.tensor_scalar_add(out=rlb[:, :], in0=pb[5][0][0:8, 0:QB],
                                                         scalar1=rbt[:, 0:1]),
                 reads=[pb[5][1], rbtB], writes=[rlbB], tag=f"{blk}_rle")
            for j in range(4):
                P.op("tensor", lambda e, j=j: e.matmul(
                        pb[6][0][:, j * 8:(j + 1) * 8], rlb[:, j * 128:(j + 1) * 128],
                        i128[0:8, 0:8], start=True, stop=True),
                     reads=[rlbB, i128B], writes=[pb[6][1]], tag=f"{blk}_rt{j}")
            for j in range(4):
                P.op("vector", lambda e, j=j: e.reduce_max(
                        out=gsum[:, 1:2], in_=pb[6][0][:, j * 8:(j + 1) * 8],
                        axis=AX.X, negate=True),
                     reads=[pb[6][1]], writes=[gsumB], tag=f"{blk}_gm{j}")
                P.op("scalar", lambda e, j=j: e.activation(
                        out=gex[:, :], in_=pb[6][0][:, j * 8:(j + 1) * 8],
                        func=AF.Exp, bias=gsum[:, 1:2], accum_out=gsum[:, 0:1]),
                     reads=[pb[6][1], gsumB], writes=[gexB, gsumB], tag=f"{blk}_ge{j}")
                P.op("vector", lambda e: e.reciprocal(out=gsum[:, 0:1], in_=gsum[:, 0:1]),
                     reads=[gsumB], writes=[gsumB], tag=f"{blk}_gr{j}")
                P.op("vector", lambda e, j=j: e.tensor_scalar_mul(
                        out=gb[:, j * 8:(j + 1) * 8], in0=gex[:, :], scalar1=gsum[:, 0:1]),
                     reads=[gexB, gsumB], writes=[gbB], tag=f"{blk}_gn{j}")
                P.op("tensor", lambda e, j=j: e.matmul(
                        pb[7][0][0:8, j * 128:(j + 1) * 128], gb[:, j * 8:(j + 1) * 8],
                        i128[:, :], start=True, stop=True),
                     reads=[gbB, i128B], writes=[pb[7][1]], tag=f"{blk}_gt{j}")
            P.op("vector", lambda e: e.tensor_copy(out=gTt[:, :], in_=pb[7][0][0:8, 0:QB]),
                 reads=[pb[7][1]], writes=[gTtB], tag=f"{blk}_gte")
            for ec in range(EC):
                P.op("tensor", lambda e, ec=ec: e.matmul(
                        pb[5][0][0:8, 0:QB], lab[:, ec * 8:(ec + 1) * 8],
                        x3ln[:, ec * QB:(ec + 1) * QB],
                        start=(ec == 0), stop=(ec == EC - 1)),
                     reads=[labB, x3lnB], writes=[pb[5][1]], tag=f"{blk}_mi{ec}")
            P.op("vector", lambda e: e.tensor_scalar_mul(out=mb[:, :],
                                                         in0=pb[5][0][0:8, 0:QB], scalar1=2.0),
                 reads=[pb[5][1]], writes=[mbB], tag=f"{blk}_me")
            P.op("tensor", lambda e: e.matmul(pb[5][0][0:64, 0:QB], selE[:, :], gTt[:, :],
                                              start=True, stop=True),
                 reads=[selEB, gTtB], writes=[pb[5][1]], tag=f"{blk}_sge")
            P.op("scalar", lambda e: e.copy(out=geb[:, :], in_=pb[5][0][0:64, 0:QB]),
                 reads=[pb[5][1]], writes=[gebB], tag=f"{blk}_sgee")
            P.op("tensor", lambda e: e.matmul(pb[5][0][0:64, 0:QB], selR[:, :], mb[:, :],
                                              start=True, stop=True),
                 reads=[selRB, mbB], writes=[pb[5][1]], tag=f"{blk}_smr")
            P.op("scalar", lambda e: e.copy(out=meb[:, :], in_=pb[5][0][0:64, 0:QB]),
                 reads=[pb[5][1]], writes=[mebB], tag=f"{blk}_smre")
            P.op("vector", lambda e: e.tensor_mul(out=p64[:, :], in0=geb[:, :], in1=meb[:, :]),
                 reads=[gebB, mebB], writes=[p64B], tag=f"{blk}_p64")
            for oc in range(EC):
                bank = 2 + oc % 2
                for ec in range(EC):
                    P.op("tensor", lambda e, oc=oc, ec=ec, bank=bank: e.matmul(
                            pb[bank][0][:, 0:QB],
                            bwb[:, ec * D + oc * 128: ec * D + oc * 128 + 128],
                            x3ln[:, ec * QB:(ec + 1) * QB],
                            start=(ec == 0), stop=(ec == EC - 1)),
                         reads=[bwbB, x3lnB], writes=[pb[bank][1]], tag=f"{blk}_bs{oc}_{ec}")
                lbank = 4 + oc % 2
                P.op("tensor", lambda e, oc=oc, lbank=lbank: e.matmul(
                        pb[lbank][0][:, 0:QB], lbb[:, oc * 128:(oc + 1) * 128], p64[:, :],
                        start=True, stop=True),
                     reads=[lbbB, p64B], writes=[pb[lbank][1]], tag=f"{blk}_lo{oc}")
                P.op("vector", lambda e, oc=oc, bank=bank: e.tensor_scalar_add(
                        out=fout[:, oc * QB:(oc + 1) * QB], in0=pb[bank][0][:, 0:QB],
                        scalar1=bbt[:, oc:oc + 1]),
                     reads=[pb[bank][1], bbtB], writes=[foutB], tag=f"{blk}_be{oc}")
                P.op("vector", lambda e, oc=oc, lbank=lbank: e.tensor_add(
                        out=fout[:, oc * QB:(oc + 1) * QB],
                        in0=fout[:, oc * QB:(oc + 1) * QB], in1=pb[lbank][0][:, 0:QB]),
                     reads=[pb[lbank][1], foutB], writes=[foutB], tag=f"{blk}_ba{oc}")
            P.op("vector", lambda e: e.tensor_copy(out=xb[:, :], in_=fout[:, :]),
                 reads=[foutB], writes=[xbB], tag=f"{blk}_ob")
            P.op("scalar", lambda e: e.activation(out=sqx[:, :], in_=xb[:, :], func=AF.Square),
                 reads=[xbB], writes=[sqxB], tag=f"{blk}_osq")
            emit_ln_stats(xb, xbB, sqx, sqxB, f"{blk}_l4")
            ln_apply(fout, foutB, fout, foutB, f"{blk}_l4a")
            P.op("vector", lambda e: e.tensor_add(out=x2t[:, :], in0=x2t[:, :], in1=fout[:, :]),
                 reads=[foutB, x2tB], writes=[x2tB], tag=f"{blk}_rs2")
            dst = out_h.ap()[s].rearrange("(c p) t -> p c t", p=128)[:, :, qb * QB:(qb + 1) * QB]
            P.dma("gpsimd", lambda e, d=dst: e.dma_start(
                    out=d, in_=ap3(x2t, 0, [[EC * QB, 128], [QB, EC], [1, QB]])),
                  reads=[x2tB], writes=[out_b], tag=f"{blk}_st")
    A.reset(base)


_STREAMS = [
    ("x0", (BL, D, T), F32, None, (1,)),
    ("x1", (BL, D, T), F32, 1, (2,)),
    ("x2", (BL, D, T), F32, 2, (3, 4)),
    ("qh", (BL, H, DH, T), BF16, 3, (4,)),
    ("kh", (BL, H, DH, T), BF16, 3, (4,)),
    ("vh", (BL, T // 128, 128, H * 65), BF16, 3, (4,)),
    ("out", (BL, D, T), F32, 4, ()),
]

_PHASE_WEIGHTS = {
    1: [("WkT", (D, D), F32), ("WvT", (D, D), F32), ("WrT", (D, D), F32),
        ("WoT", (D, D), F32), ("tmk", (128, EC), F32), ("tmv", (128, EC), F32),
        ("tmr", (128, EC), F32), ("tdec", (128, EC), F32), ("tfir", (128, EC), F32)],
    2: [("WfcT", (D, HFF), F32), ("WprT", (HFF, D), F32),
        ("lnw", (128, HFF // 128), F32), ("lnb", (128, HFF // 128), F32)],
    3: [("wqr", (D, D), F32), ("wkr", (D, D), F32), ("wvr", (D, D), F32)],
    4: [("woT", (D, D), F32), ("bwT", (D, D), F32), ("bb", (128, EC), F32),
        ("loraA", (D, 8), F32), ("loraB", (64, D), F32),
        ("rwT", (D, 8), F32), ("rb", (8, 1), F32)],
}

_EMITTERS = {1: emit_phase1, 2: emit_phase2, 3: emit_phase3, 4: emit_phase4}


def build_nc(phases):
    nc = bass.Bass("TRN2", target_bir_lowering=False, debug=False)
    P = Prog()
    A = Arena(nc, P, SBUF_LIMIT)
    dram = {}
    pset = set(phases)
    nbuf = [0]
    def _dbuf(name):
        nbuf[0] += 1
        return P.buf("dram", nbuf[0] - 1, nbuf[0], name)
    for name, shape, dt, prod, cons in _STREAMS:
        needed = (prod in pset) or (pset & set(cons))
        if not needed:
            continue
        if prod in pset and (set(cons) - pset or not cons):
            kind = "ExternalOutput"
        elif prod in pset:
            kind = "Internal"
        else:
            kind = "ExternalInput"
        h = nc.dram_tensor(name, list(shape), dt, kind=kind)
        if name in ("x1", "x2", "qh", "kh", "vh"):
            n = 16 if name in ("x1", "x2") else 8
            b0 = nbuf[0]
            nbuf[0] += n
            parent = P.buf("dram", b0, b0 + n, name)
            dram[name] = (h, parent)
            dram[name + "_subs"] = [
                P.buf("dram", b0 + i, b0 + i + 1, f"{name}_sub{i}") for i in range(n)]
        else:
            dram[name] = (h, _dbuf(name))
    for ph in phases:
        for name, shape, dt in _PHASE_WEIGHTS[ph]:
            h = nc.dram_tensor(name, list(shape), dt, kind="ExternalInput")
            dram[name] = (h, _dbuf(name))
    if 2 in pset:
        for nm, srcn, shape in (("WfcB16", "WfcT", (D, HFF)),
                                ("WprB16", "WprT", (HFF, D))):
            h16 = nc.dram_tensor(nm, list(shape), BF16, kind="Internal")
            b16 = _dbuf(nm)
            dram[nm] = (h16, b16)
            sh, sb = dram[srcn]
            P.dma("gpsimd", lambda e, h16=h16, sh=sh: e.dma_start(
                    out=h16.ap(), in_=sh.ap()),
                  reads=[sb], writes=[b16], tag=f"cast_{nm}")
    pb = []
    for i in range(8):
        t = nc.alloc_psum_tensor(f"pb{i}", [128, 512], F32)
        pb.append((t, P.buf("psum", i * 2048, (i + 1) * 2048, f"pb{i}")))
    for ph in phases:
        _EMITTERS[ph](nc, P, A, dram, pb)
    P.build(nc, None)
    outs = [n for n, _, _, prod, cons in _STREAMS
            if prod in pset and (set(cons) - pset or not cons)]
    ins = [n for n, _, _, prod, cons in _STREAMS
           if prod not in pset and (pset & set(cons))]
    wts = [name for ph in phases for name, _, _ in _PHASE_WEIGHTS[ph]]
    return nc, ins, wts, outs


def _prep_all(inputs):
    f = np.ascontiguousarray
    w = _prep_shared(inputs)
    w["WfcT"] = f(inputs["L0_ffn_W_fc"].T)
    w["WprT"] = f(inputs["L0_ffn_W_proj"].T)
    w["lnw"] = f(inputs["L0_ffn_ln_w"].reshape(HFF // 128, 128).T)
    w["lnb"] = f(inputs["L0_ffn_ln_b"].reshape(HFF // 128, 128).T)
    w["wqr"] = f(inputs["L1_attn_w_q"].reshape(D, D))
    w["wkr"] = f(inputs["L1_attn_w_k"].reshape(D, D))
    w["wvr"] = f(inputs["L1_attn_w_v"].reshape(D, D))
    w["woT"] = f(inputs["L1_attn_w_o"].reshape(D, D).T)
    w["bwT"] = f(inputs["L1_ffn_base_w"].T)
    w["bb"] = f(inputs["L1_ffn_base_b"].reshape(EC, 128).T)
    w["loraA"] = f(inputs["L1_ffn_lora_A"])
    w["loraB"] = f(inputs["L1_ffn_lora_B"].reshape(64, D))
    w["rwT"] = f(inputs["L1_ffn_router_w"].T)
    w["rb"] = f(inputs["L1_ffn_router_b"].reshape(8, 1))
    return w


PHASE_GROUPS = [[1, 2, 3, 4]]


def kernel(**inputs):
    w = _prep_all(inputs)
    x = np.ascontiguousarray(inputs["L0_attn_x"])

    stream_vals = [dict() for _ in range(NCORES)]
    for c in range(NCORES):
        stream_vals[c]["x0"] = np.ascontiguousarray(
            np.transpose(x[2 * c:2 * c + 2], (0, 2, 1)))

    final = None
    for grp in PHASE_GROUPS:
        nc, ins, wts, outs = build_nc(grp)
        in_maps = []
        for c in range(NCORES):
            m = {k: stream_vals[c][k] for k in ins}
            m.update({k: w[k] for k in wts})
            in_maps.append(m)
        res = run_bass_kernel_spmd(nc, in_maps, core_ids=list(range(NCORES)))
        for c in range(NCORES):
            for o in outs:
                stream_vals[c][o] = res.results[c][o]
        final = outs[-1] if outs else final

    key = "out" if any("out" in sv for sv in stream_vals) else final
    out = []
    for c in range(NCORES):
        out.append(np.transpose(stream_vals[c][key], (0, 2, 1)))
    return np.concatenate(out, axis=0).astype(np.float32)


# version 38
# speedup vs baseline: 1.0032x; 1.0032x over previous
import math
import numpy as np
import concourse.bass as bass
import concourse.mybir as mybir
from concourse.bass_utils import run_bass_kernel_spmd

F32 = mybir.dt.float32
BF16 = mybir.dt.bfloat16
AF = mybir.ActivationFunctionType
ALU = mybir.AluOpType
AX = mybir.AxisListType

NCORES = 8
BL = 2
T = 2048
D = 1024
EC = 8
H = 16
DH = 64
HFF = 4096
LN_EPS = 1e-5


class Buf:
    __slots__ = ("space", "start", "end", "name", "overlaps")

    def __init__(self, space, start, end, name):
        self.space, self.start, self.end, self.name = space, start, end, name
        self.overlaps = []


class _Op:
    __slots__ = ("eng", "emit", "reads", "writes", "seq", "kind", "queue",
                 "inc_needed", "inc_val", "dma_val", "waits", "tag", "skey")

    def __init__(self, eng, emit, reads, writes, seq, kind, queue, tag):
        self.eng, self.emit = eng, emit
        self.reads, self.writes = reads, writes
        self.seq, self.kind, self.queue = seq, kind, queue
        self.inc_needed = False
        self.inc_val = None
        self.dma_val = None
        self.waits = []
        self.tag = tag


ENGINES = ("sync", "vector", "scalar", "tensor", "gpsimd")
DRAIN_ENGINES = ("vector", "scalar", "gpsimd")


class Prog:
    def __init__(self):
        self.ops = []
        self.state = {}
        self._bufs = []

    def buf(self, space, start, end, name):
        b = Buf(space, start, end, name)
        for o in self._bufs:
            if o.space == space and not (end <= o.start or start >= o.end):
                o.overlaps.append(b)
                b.overlaps.append(o)
        self._bufs.append(b)
        return b

    def _touch(self, op):
        def dep(producer):
            if producer is None or producer is op:
                return
            op.waits.append(producer)
        for r in op.reads:
            for b in (r, *r.overlaps):
                st = self.state.get(b)
                if st:
                    dep(st[0])
        for w in op.writes:
            for b in (w, *w.overlaps):
                st = self.state.get(b)
                if st:
                    dep(st[0])
                    for rd in st[1]:
                        dep(rd)
        for r in op.reads:
            st = self.state.get(r)
            if st is None:
                self.state[r] = (None, [op])
            else:
                st[1].append(op)
        for w in op.writes:
            self.state[w] = (op, [])

    def op(self, eng, emit, reads=(), writes=(), tag=""):
        o = _Op(eng, emit, tuple(reads), tuple(writes), len(self.ops), "op", None, tag)
        self._touch(o)
        self.ops.append(o)
        return o

    def dma(self, queue, emit, reads=(), writes=(), tag=""):
        o = _Op(queue, emit, tuple(reads), tuple(writes), len(self.ops), "dma", queue, tag)
        skey = None
        for b in (*o.writes, *o.reads):
            if b.space == "sbuf":
                skey = b
                break
        if skey is None:
            skey = (o.writes + o.reads)[0]
        o.skey = (skey, "sw" if queue == "gpsimd" else "hw")
        self._touch(o)
        self.ops.append(o)
        return o

    def build(self, nc, ctx):
        for o in self.ops:
            for p in o.waits:
                if p.kind == "dma":
                    continue
                if p.eng == o.eng:
                    if p.eng == "tensor" and o.kind == "op":
                        continue
                    continue
                p.inc_needed = True
        eng_cnt = {e: 0 for e in ENGINES}
        skey_cnt = {}
        dma_by_queue = {e: [] for e in ENGINES}
        for o in self.ops:
            if o.kind == "dma":
                skey_cnt[o.skey] = skey_cnt.get(o.skey, 0) + 16
                o.dma_val = skey_cnt[o.skey]
                dma_by_queue[o.queue].append(o)
            elif o.inc_needed:
                eng_cnt[o.eng] += 1
                o.inc_val = eng_cnt[o.eng]
        self.final_dma = dict(skey_cnt)

        sems = {}
        for e in ENGINES:
            if eng_cnt[e]:
                sems[("eng", e)] = nc.alloc_semaphore(f"s_{e}")
        for sk in skey_cnt:
            sems[("dma", sk)] = nc.alloc_semaphore(f"d_{sk[0].name}_{sk[1]}")
        self.sems = sems

        streams = {e: [] for e in ENGINES}
        hw = {e: {} for e in ENGINES}
        drained = {e: -1 for e in ENGINES}
        lastseq = {e: -1 for e in ENGINES}

        for o in self.ops:
            needs_drain = False
            waitlist = {}
            for p in o.waits:
                if p.kind == "dma":
                    key = ("dma", p.skey)
                    waitlist[key] = max(waitlist.get(key, 0), p.dma_val)
                elif p.eng == o.eng:
                    if p.eng == "tensor" and o.kind == "op":
                        continue
                    if p.seq > drained[o.eng]:
                        needs_drain = True
                else:
                    key = ("eng", p.eng)
                    if p.inc_val is None:
                        raise RuntimeError(f"producer without inc: {p.tag} -> {o.tag}")
                    waitlist[key] = max(waitlist.get(key, 0), p.inc_val)
            if needs_drain:
                drained[o.eng] = lastseq[o.eng]
            ww = []
            for key, val in waitlist.items():
                if hw[o.eng].get(key, 0) < val:
                    hw[o.eng][key] = val
                    ww.append((key, val))
            streams[o.eng].append((o, needs_drain, ww))
            lastseq[o.eng] = o.seq

        def replay(ename, eng):
            for (o, needs_drain, ww) in streams[ename]:
                if needs_drain:
                    eng.drain()
                for key, val in ww:
                    eng.wait_ge(sems[key], val)
                ins = o.emit(eng)
                if o.kind == "dma":
                    ins.then_inc(sems[("dma", o.skey)], 16)
                elif o.inc_needed:
                    ins.then_inc(sems[("eng", o.eng)], 1)
            finals = {}
            for o in dma_by_queue[ename]:
                finals[o.skey] = skey_cnt[o.skey]
            for sk, v in finals.items():
                eng.wait_ge(sems[("dma", sk)], v)

        with nc.Block() as block:
            if streams["sync"]:
                block.sync(lambda eng: replay("sync", eng))
            if streams["vector"]:
                block.vector(lambda eng: replay("vector", eng))
            if streams["scalar"]:
                block.scalar(lambda eng: replay("scalar", eng))
            if streams["tensor"]:
                block.tensor(lambda eng: replay("tensor", eng))
            if streams["gpsimd"]:
                block.gpsimd(lambda eng: replay("gpsimd", eng))



SBUF_BASE = 18432


class Arena:
    def __init__(self, nc, prog, limit_bytes):
        self.nc = nc
        self.prog = prog
        self.limit = limit_bytes
        self.off = SBUF_BASE
        self.n = 0

    def alloc(self, name, parts, cols, dt):
        esz = 4 if dt == F32 else 2
        nbytes = cols * esz
        nbytes_al = (nbytes + 31) & ~31
        t = self.nc.alloc_sbuf_tensor_at(name, [parts, cols], dt, offset=self.off)
        b = self.prog.buf("sbuf", self.off, self.off + nbytes_al, name)
        self.off += nbytes_al
        if self.off > self.limit:
            raise RuntimeError(f"SBUF arena overflow at {name}: {self.off} > {self.limit}")
        return t, b

    def mark(self):
        return self.off

    def reset(self, off):
        self.off = off


def ap3(t, offset, pattern):
    return bass.AP(t, offset, pattern)



def emit_phase1(nc, P, A, dram, pb):
    TT = 256
    NT = T // TT
    NTT = NT * BL

    x_h, x_b = dram["x0"]
    x1_h, x1_b = dram["x1"]
    x1_subs = dram["x1_subs"]
    wk_h, wk_b = dram["WkT"]
    wv_h, wv_b = dram["WvT"]
    wr_h, wr_b = dram["WrT"]
    wo_h, wo_b = dram["WoT"]
    tmk_h, tmk_b = dram["tmk"]
    tmv_h, tmv_b = dram["tmv"]
    tmr_h, tmr_b = dram["tmr"]
    tdec_h, tdec_b = dram["tdec"]
    tfir_h, tfir_b = dram["tfir"]

    base = A.mark()
    wkb, wkbB = A.alloc("p1_wk", 128, EC * 1024, BF16)
    wvb, wvbB = A.alloc("p1_wv", 128, EC * 1024, BF16)
    wrb, wrbB = A.alloc("p1_wr", 128, EC * 1024, BF16)
    wob, wobB = A.alloc("p1_wo", 128, EC * 1024, BF16)
    tmk, tmkB = A.alloc("p1_tmk", 128, EC, F32)
    tmv, tmvB = A.alloc("p1_tmv", 128, EC, F32)
    tmr, tmrB = A.alloc("p1_tmr", 128, EC, F32)
    wdec, wdecB = A.alloc("p1_wdec", 128, EC, F32)
    ufir, ufirB = A.alloc("p1_u", 128, EC, F32)
    xin = [A.alloc(f"p1_xin{i}", 128, EC * TT, F32) for i in range(2)]
    xb, xbB = A.alloc("p1_xb", 128, EC * TT, BF16)
    sqx, sqxB = A.alloc("p1_sqx", 128, EC * TT, BF16)
    xln, xlnB = A.alloc("p1_xln", 128, EC * TT, BF16)
    xkb, xkbB = A.alloc("p1_xk", 128, EC * TT, BF16)
    xvb, xvbB = A.alloc("p1_xv", 128, EC * TT, BF16)
    xrb, xrbB = A.alloc("p1_xr", 128, EC * TT, BF16)
    kbuf = [A.alloc(f"p1_k{i}", 128, EC * TT, F32) for i in range(2)]
    vbuf = [A.alloc(f"p1_v{i}", 128, EC * TT, BF16) for i in range(2)]
    srb = [A.alloc(f"p1_sr{i}", 128, EC * TT, BF16) for i in range(2)]
    psub = {}
    for lst, esz in ((kbuf, 4), (vbuf, 2), (srb, 2)):
        for t_, b_ in lst:
            psub[b_] = [P.buf("sbuf", b_.start + g * 2 * TT * esz,
                              b_.start + (g + 1) * 2 * TT * esz,
                              f"{b_.name}_g{g}") for g in range(4)]
    ppx, ppxB = A.alloc("p1_ppx", 128, EC * (TT + 1), F32)
    aax, aaxB = A.alloc("p1_aax", 128, EC * (TT + 1), F32)
    bbx, bbxB = A.alloc("p1_bbx", 128, EC * (TT + 1), F32)
    s1, s1B = A.alloc("p1_s1", 128, EC * TT, F32)
    s2, s2B = A.alloc("p1_s2", 128, EC * TT, F32)
    s3, s3B = A.alloc("p1_s3", 128, EC * TT, F32)
    rwb = [A.alloc(f"p1_rw{i}", 128, EC * TT, BF16) for i in range(2)]
    xb2, xb2B = A.alloc("p1_xb2", 128, EC * TT, BF16)
    xb2C = [P.buf("sbuf", xb2B.start + g * 2 * TT * 2,
                  xb2B.start + (g + 1) * 2 * TT * 2, f"p1_xb2_g{g}") for g in range(4)]
    sqx2, sqx2B = sqx, sqxB
    muB, muBB = A.alloc("p1_muB", 128, TT, F32)
    rstdB, rstdBB = A.alloc("p1_rstdB", 128, TT, F32)
    muB2, muB2B = A.alloc("p1_muB2", 128, TT, F32)
    rstdB2, rstdB2B = A.alloc("p1_rstdB2", 128, TT, F32)
    stat, statB = A.alloc("p1_stat", 1, 4 * TT, F32)
    stat2, stat2B = stat, statB
    onesr, onesrB = A.alloc("p1_onesr", 1, 128, F32)
    onesc, onescB = A.alloc("p1_onesc", 128, 1, BF16)
    lnc, lncB = A.alloc("p1_lnc", 1, 2, F32)
    carry = {}
    carryB = {}
    for s in range(BL):
        for nm in ("pp", "aa", "bb"):
            carry[(nm, s)], carryB[(nm, s)] = A.alloc(f"p1_c{nm}{s}", 128, EC, F32)
        carry[("xc", s)], carryB[("xc", s)] = A.alloc(f"p1_cx{s}", 128, EC, BF16)

    for (hb, hbuf), (sb, sbuf) in (((wk_h, wk_b), (wkb, wkbB)),
                                   ((wv_h, wv_b), (wvb, wvbB)),
                                   ((wr_h, wr_b), (wrb, wrbB)),
                                   ((wo_h, wo_b), (wob, wobB))):
        src = hb.ap().rearrange("(c p) d -> p c d", p=128)
        P.dma("gpsimd", lambda e, d=sb, s=src: e.dma_start(
                out=ap3(d, 0, [[EC * 1024, 128], [1024, EC], [1, 1024]]), in_=s),
              reads=[hbuf], writes=[sbuf], tag="w_ld")
    for (hb, hbuf), (sb, sbuf) in (((tmk_h, tmk_b), (tmk, tmkB)),
                                   ((tmv_h, tmv_b), (tmv, tmvB)),
                                   ((tmr_h, tmr_b), (tmr, tmrB)),
                                   ((tfir_h, tfir_b), (ufir, ufirB))):
        P.dma("sync", lambda e, d=sb, s=hb: e.dma_start(out=d[:, :], in_=s.ap()),
              reads=[hbuf], writes=[sbuf], tag="v_ld")
    P.dma("sync", lambda e: e.dma_start(out=wdec[:, :], in_=tdec_h.ap()),
          reads=[tdec_b], writes=[wdecB], tag="tdec_ld")
    P.op("scalar", lambda e: e.activation(out=wdec[:, :], in_=wdec[:, :], func=AF.Exp),
         reads=[wdecB], writes=[wdecB], tag="wdec_exp")
    P.op("vector", lambda e: e.tensor_scalar_mul(out=wdec[:, :], in0=wdec[:, :], scalar1=-1.0),
         reads=[wdecB], writes=[wdecB], tag="wdec_neg")
    P.op("vector", lambda e: e.memset(onesr[:, :], 1.0), writes=[onesrB], tag="ones_r")
    P.op("vector", lambda e: e.memset(onesc[:, :], 1.0), writes=[onescB], tag="ones_c")
    P.op("vector", lambda e: e.memset(lnc[0:1, 0:1], LN_EPS), writes=[lncB], tag="p1lne")
    P.op("vector", lambda e: e.memset(lnc[0:1, 1:2], -0.5), writes=[lncB], tag="p1lnh")
    for s in range(BL):
        P.op("vector", lambda e, s=s: e.memset(carry[("pp", s)][:, :], -1e38),
             writes=[carryB[("pp", s)]], tag="c_pp")
        P.op("vector", lambda e, s=s: e.memset(carry[("aa", s)][:, :], 0.0),
             writes=[carryB[("aa", s)]], tag="c_aa")
        P.op("vector", lambda e, s=s: e.memset(carry[("bb", s)][:, :], 0.0),
             writes=[carryB[("bb", s)]], tag="c_bb")
        P.op("vector", lambda e, s=s: e.memset(carry[("xc", s)][:, :], 0.0),
             writes=[carryB[("xc", s)]], tag="c_x")

    def v3(t, off=0, n=EC):
        return ap3(t, off, [[EC * TT, 128], [TT, n], [1, TT]])

    def b3(t, n=EC):
        return ap3(t, 0, [[TT, 128], [0, n], [1, TT]])

    def pv3(t, a=0, n=EC):
        return ap3(t, a, [[EC * (TT + 1), 128], [TT + 1, n], [1, TT]])

    def cb3(t, c0=0, n=EC):
        return ap3(t, c0, [[EC, 128], [1, n], [0, TT]])

    def emit_ln_stats(src_t, srcB, sq_t, sqB_, st_, stB_, mu_t, muB_, rs_t, rsB_,
                      bsum, bsq, tag, delta=None):
        if delta is not None:
            for c in range(EC):
                P.op("tensor", lambda e, c=c: e.matmul(
                        pb[bsum][0][0:1, 0:2 * TT], onesc[:, 0:1],
                        ap3(src_t, c * TT, [[EC * TT, 128], [delta, 2], [1, TT]]),
                        start=(c == 0), stop=(c == EC - 1)),
                     reads=[srcB, sqB_, onescB], writes=[pb[bsum][1]], tag=f"{tag}_sum")
        else:
            for c in range(EC):
                P.op("tensor", lambda e, c=c: e.matmul(pb[bsum][0][0:1, 0:TT], onesc[:, 0:1],
                                                       ap3(src_t, c * TT, [[EC * TT, 128], [1, TT]]),
                                                       start=(c == 0), stop=(c == EC - 1)),
                     reads=[srcB, onescB], writes=[pb[bsum][1]], tag=f"{tag}_sum")
            for c in range(EC):
                P.op("tensor", lambda e, c=c: e.matmul(pb[bsum][0][0:1, TT:2 * TT], onesc[:, 0:1],
                                                       ap3(sq_t, c * TT, [[EC * TT, 128], [1, TT]]),
                                                       start=(c == 0), stop=(c == EC - 1)),
                     reads=[sqB_, onescB], writes=[pb[bsum][1]], tag=f"{tag}_ssq")
        n = EC * 128
        P.op("vector", lambda e: e.tensor_scalar_mul(out=st_[0:1, 0:2 * TT],
                                                     in0=pb[bsum][0][0:1, 0:2 * TT], scalar1=1.0 / n),
             reads=[pb[bsum][1]], writes=[stB_], tag=f"{tag}_mu")
        P.op("vector", lambda e: e.tensor_mul(out=st_[0:1, 2 * TT:3 * TT],
                                              in0=st_[0:1, 0:TT], in1=st_[0:1, 0:TT]),
             reads=[stB_], writes=[stB_], tag=f"{tag}_musq")
        P.op("vector", lambda e: e.tensor_sub(out=st_[0:1, 2 * TT:3 * TT],
                                              in0=st_[0:1, TT:2 * TT], in1=st_[0:1, 2 * TT:3 * TT]),
             reads=[stB_], writes=[stB_], tag=f"{tag}_var")
        P.op("scalar", lambda e: e.activation(out=st_[0:1, 2 * TT:3 * TT],
                                              in_=st_[0:1, 2 * TT:3 * TT], func=AF.Ln,
                                              bias=lnc[0:1, 0:1]),
             reads=[stB_, lncB], writes=[stB_], tag=f"{tag}_lnv")
        P.op("scalar", lambda e: e.activation(out=st_[0:1, TT:2 * TT],
                                              in_=st_[0:1, 2 * TT:3 * TT], func=AF.Exp,
                                              scale=lnc[0:1, 1:2]),
             reads=[stB_, lncB], writes=[stB_], tag=f"{tag}_rstd")
        P.op("tensor", lambda e: e.matmul(pb[bsum][0][:, 0:2 * TT], onesr[0:1, :],
                                          st_[0:1, 0:2 * TT], start=True, stop=True),
             reads=[stB_, onesrB], writes=[pb[bsum][1]], tag=f"{tag}_bmu")
        P.op("scalar", lambda e: e.copy(out=mu_t[:, :], in_=pb[bsum][0][:, 0:TT]),
             reads=[pb[bsum][1]], writes=[muB_], tag=f"{tag}_emu")
        P.op("scalar", lambda e: e.copy(out=rs_t[:, :], in_=pb[bsum][0][:, TT:2 * TT]),
             reads=[pb[bsum][1]], writes=[rsB_], tag=f"{tag}_ers")

    def stA(ti):
        tt, s = divmod(ti, BL)
        t0 = tt * TT
        xt, xtB = xin[ti % 2]
        src = x_h.ap()[s].rearrange("(c p) t -> p c t", p=128)[:, :, t0:t0 + TT]
        P.dma("sync", lambda e, d=xt, sr=src: e.dma_start(
                out=ap3(d, 0, [[EC * TT, 128], [TT, EC], [1, TT]]), in_=sr),
              reads=[x_b], writes=[xtB], tag=f"x_ld{ti}")
        xdst = x1_h.ap()[s].rearrange("(c p) t -> p c t", p=128)[:, :, t0:t0 + TT]
        P.dma("gpsimd", lambda e, xt=xt, d=xdst: e.dma_start(
                out=d, in_=ap3(xt, 0, [[EC * TT, 128], [TT, EC], [1, TT]])),
              reads=[xtB], writes=[x1_subs[s * 8 + t0 // 256]], tag=f"x_cp{ti}")
        P.op("scalar", lambda e, xt=xt: e.copy(out=xb[:, :], in_=xt[:, :]),
             reads=[xtB], writes=[xbB], tag=f"xb{ti}")
        P.op("gpsimd", lambda e: e.tensor_mul(out=sqx[:, :], in0=xb[:, :], in1=xb[:, :]),
             reads=[xbB], writes=[sqxB], tag=f"sq{ti}")
        emit_ln_stats(xb, xbB, sqx, sqxB, stat, statB, muB, muBB, rstdB, rstdBB,
                      4, 5, f"ln1_{ti}")
        P.op("vector", lambda e, xt=xt: e.tensor_tensor(
                out=v3(s1), in0=v3(xt), in1=b3(muB), op=ALU.subtract),
             reads=[xtB, muBB], writes=[s1B], tag=f"lnsub{ti}")
        P.op("vector", lambda e: e.tensor_tensor(
                out=v3(xln), in0=v3(s1), in1=b3(rstdB), op=ALU.mult),
             reads=[s1B, rstdBB], writes=[xlnB], tag=f"lnmul{ti}")
        P.op("scalar", lambda e, s=s: e.copy(
                out=ap3(sqx, 0, [[EC * TT, 128], [TT, EC], [1, 1]]),
                in_=ap3(carry[("xc", s)], 0, [[EC, 128], [1, EC], [0, 1]])),
             reads=[carryB[("xc", s)]], writes=[sqxB], tag=f"xx0_{ti}")
        P.op("scalar", lambda e: e.copy(
                out=ap3(sqx, 1, [[EC * TT, 128], [TT, EC], [1, TT - 1]]),
                in_=ap3(xln, 0, [[EC * TT, 128], [TT, EC], [1, TT - 1]])),
             reads=[xlnB], writes=[sqxB], tag=f"xxs_{ti}")
        P.op("vector", lambda e, s=s: e.tensor_copy(
                out=carry[("xc", s)][:, :],
                in_=ap3(xln, TT - 1, [[EC * TT, 128], [TT, EC]])),
             reads=[xlnB], writes=[carryB[("xc", s)]], tag=f"xcu_{ti}")
        P.op("vector", lambda e: e.tensor_sub(out=s1[:, :], in0=xln[:, :], in1=sqx[:, :]),
             reads=[xlnB, sqxB], writes=[s1B], tag=f"diff{ti}")
        for (tm, tmB, dst, dstB, nm, eng, sc) in (
                (tmk, tmkB, xkb, xkbB, "k", "vector", s2),
                (tmv, tmvB, xvb, xvbB, "v", "gpsimd", s3),
                (tmr, tmrB, xrb, xrbB, "r", "vector", s2)):
            scB = s2B if sc is s2 else s3B
            P.op(eng, lambda e, tm=tm, sc=sc: e.tensor_tensor(
                    out=v3(sc), in0=v3(s1), in1=cb3(tm), op=ALU.mult),
                 reads=[s1B, tmB], writes=[scB], tag=f"xm{nm}{ti}")
            P.op(eng, lambda e, dst=dst, sc=sc: e.tensor_add(
                    out=dst[:, :], in0=sc[:, :], in1=sqx[:, :]),
                 reads=[scB, sqxB], writes=[dstB], tag=f"xa{nm}{ti}")

    def stB(ti):
        kb_, kbB_ = kbuf[ti % 2]
        vb_, vbB_ = vbuf[ti % 2]
        sb_, sbB_ = srb[ti % 2]
        for (wt, wtB, src_t, srcB, banks, dst, dstB, nm, sig) in (
                (wkb, wkbB, xkb, xkbB, (0, 5), kb_, kbB_, "k", False),
                (wvb, wvbB, xvb, xvbB, (1, 7), vb_, vbB_, "v", False),
                (wrb, wrbB, xrb, xrbB, (2, 5), sb_, sbB_, "r", True)):
            for dc in range(EC):
                bank = banks[(dc // 2) % 2]
                col = (dc % 2) * TT
                for ec in range(EC):
                    P.op("tensor", lambda e, wt=wt, src_t=src_t, dc=dc, ec=ec, bank=bank, col=col:
                         e.matmul(pb[bank][0][:, col:col + TT],
                                  wt[:, ec * 1024 + dc * 128: ec * 1024 + dc * 128 + 128],
                                  src_t[:, ec * TT:(ec + 1) * TT],
                                  start=(ec == 0), stop=(ec == EC - 1)),
                         reads=[wtB, srcB], writes=[pb[bank][1]], tag=f"mm{nm}{ti}_{dc}_{ec}")
                if dc % 2 == 0:
                    continue
                if sig:
                    P.op("scalar", lambda e, dc=dc, dst=dst, bank=bank: e.activation(
                            out=dst[:, (dc - 1) * TT:(dc + 1) * TT], in_=pb[bank][0][:, 0:2 * TT],
                            func=AF.Sigmoid),
                         reads=[pb[bank][1]], writes=[psub[dstB][dc // 2]], tag=f"ev{nm}{ti}_{dc}")
                else:
                    P.op("scalar", lambda e, dc=dc, dst=dst, bank=bank: e.copy(
                            out=dst[:, (dc - 1) * TT:(dc + 1) * TT], in_=pb[bank][0][:, 0:2 * TT]),
                         reads=[pb[bank][1]], writes=[psub[dstB][dc // 2]], tag=f"ev{nm}{ti}_{dc}")

    def _cenv(ti):
        tt, s = divmod(ti, BL)
        kb_, kbB_ = kbuf[ti % 2]
        vb_, vbB_ = vbuf[ti % 2]
        sb_, sbB_ = srb[ti % 2]
        rw_, rwB_ = rwb[ti % 2]
        cpp, cppB = carry[("pp", s)], carryB[("pp", s)]
        caa, caaB = carry[("aa", s)], carryB[("aa", s)]
        cbb, cbbB = carry[("bb", s)], carryB[("bb", s)]
        return (kb_, kbB_, vb_, vbB_, sb_, sbB_, rw_, rwB_,
                cpp, cppB, caa, caaB, cbb, cbbB)

    def stCa(ti):
        (kb_, kbB_, vb_, vbB_, sb_, sbB_, rw_, rwB_,
         cpp, cppB, caa, caaB, cbb, cbbB) = _cenv(ti)
        P.op("vector", lambda e: e.tensor_copy(
                out=ap3(ppx, 0, [[EC * (TT + 1), 128], [TT + 1, EC], [1, 1]]),
                in_=ap3(cpp, 0, [[EC, 128], [1, EC], [0, 1]])),
             reads=[cppB], writes=[ppxB], tag=f"pp0_{ti}")
        for c in range(EC):
            P.op("vector", lambda e, c=c: e.tensor_tensor_scan(
                    out=ppx[:, c * (TT + 1) + 1: c * (TT + 1) + TT + 1],
                    data0=ap3(wdec, c, [[EC, 128], [0, TT]]),
                    data1=kb_[:, c * TT:(c + 1) * TT],
                    initial=cpp[:, c:c + 1],
                    op0=ALU.add, op1=ALU.max),
                 reads=[wdecB, kbB_, cppB], writes=[ppxB], tag=f"ppscan{ti}_{c}")
        P.op("vector", lambda e: e.tensor_tensor(
                out=v3(s1), in0=pv3(ppx, 0), in1=cb3(wdec), op=ALU.add),
             reads=[ppxB, wdecB], writes=[s1B], tag=f"ga{ti}")
        P.op("vector", lambda e: e.tensor_tensor(
                out=v3(s1), in0=v3(s1), in1=pv3(ppx, 1), op=ALU.subtract),
             reads=[s1B, ppxB], writes=[s1B], tag=f"gs{ti}")
        P.op("vector", lambda e: e.tensor_tensor(
                out=v3(s2), in0=v3(kb_), in1=pv3(ppx, 1), op=ALU.subtract),
             reads=[kbB_, ppxB], writes=[s2B], tag=f"hs{ti}")
        P.op("scalar", lambda e: e.activation(out=s1[:, :], in_=s1[:, :], func=AF.Exp),
             reads=[s1B], writes=[s1B], tag=f"gact{ti}")
        P.op("scalar", lambda e: e.activation(out=s2[:, :], in_=s2[:, :], func=AF.Exp),
             reads=[s2B], writes=[s2B], tag=f"hact{ti}")
    def stCb(ti):
        (kb_, kbB_, vb_, vbB_, sb_, sbB_, rw_, rwB_,
         cpp, cppB, caa, caaB, cbb, cbbB) = _cenv(ti)
        P.op("gpsimd", lambda e: e.tensor_mul(out=s3[:, :], in0=s2[:, :], in1=vb_[:, :]),
             reads=[s2B, vbB_], writes=[s3B], tag=f"hv{ti}")
        P.op("vector", lambda e: e.tensor_copy(
                out=ap3(aax, 0, [[EC * (TT + 1), 128], [TT + 1, EC], [1, 1]]),
                in_=ap3(caa, 0, [[EC, 128], [1, EC], [0, 1]])),
             reads=[caaB], writes=[aaxB], tag=f"aa0_{ti}")
        P.op("vector", lambda e: e.tensor_copy(
                out=ap3(bbx, 0, [[EC * (TT + 1), 128], [TT + 1, EC], [1, 1]]),
                in_=ap3(cbb, 0, [[EC, 128], [1, EC], [0, 1]])),
             reads=[cbbB], writes=[bbxB], tag=f"bb0_{ti}")
        for c in range(EC):
            P.op("vector", lambda e, c=c: e.tensor_tensor_scan(
                    out=aax[:, c * (TT + 1) + 1: c * (TT + 1) + TT + 1],
                    data0=s1[:, c * TT:(c + 1) * TT], data1=s3[:, c * TT:(c + 1) * TT],
                    initial=caa[:, c:c + 1], op0=ALU.mult, op1=ALU.add),
                 reads=[s1B, s3B, caaB], writes=[aaxB], tag=f"aascan{ti}_{c}")
            P.op("vector", lambda e, c=c: e.tensor_tensor_scan(
                    out=bbx[:, c * (TT + 1) + 1: c * (TT + 1) + TT + 1],
                    data0=s1[:, c * TT:(c + 1) * TT], data1=s2[:, c * TT:(c + 1) * TT],
                    initial=cbb[:, c:c + 1], op0=ALU.mult, op1=ALU.add),
                 reads=[s1B, s2B, cbbB], writes=[bbxB], tag=f"bbscan{ti}_{c}")
        for nm, t_, tB_, cb_, cbB2 in (("pp", ppx, ppxB, cpp, cppB),
                                       ("aa", aax, aaxB, caa, caaB),
                                       ("bb", bbx, bbxB, cbb, cbbB)):
            P.op("vector", lambda e, t_=t_, cb_=cb_: e.tensor_copy(
                    out=cb_[:, :],
                    in_=ap3(t_, TT, [[EC * (TT + 1), 128], [TT + 1, EC]])),
                 reads=[tB_], writes=[cbB2], tag=f"cu{nm}_{ti}")
        P.op("gpsimd", lambda e: e.tensor_tensor(
                out=v3(s2), in0=v3(kb_), in1=cb3(ufir), op=ALU.add),
             reads=[kbB_, ufirB], writes=[s2B], tag=f"uk{ti}")
        P.op("vector", lambda e: e.tensor_tensor(
                out=v3(s3), in0=v3(s2), in1=pv3(ppx, 0), op=ALU.max),
             reads=[s2B, ppxB], writes=[s3B], tag=f"qm{ti}")
        P.op("vector", lambda e: e.tensor_tensor(
                out=v3(s1), in0=pv3(ppx, 0), in1=v3(s3), op=ALU.subtract),
             reads=[ppxB, s3B], writes=[s1B], tag=f"asub{ti}")
        P.op("vector", lambda e: e.tensor_sub(out=s2[:, :], in0=s2[:, :], in1=s3[:, :]),
             reads=[s2B, s3B], writes=[s2B], tag=f"bsub{ti}")
        P.op("scalar", lambda e: e.activation(out=s1[:, :], in_=s1[:, :], func=AF.Exp),
             reads=[s1B], writes=[s1B], tag=f"aact{ti}")
        P.op("scalar", lambda e: e.activation(out=s2[:, :], in_=s2[:, :], func=AF.Exp),
             reads=[s2B], writes=[s2B], tag=f"bact{ti}")
    def stCc(ti):
        (kb_, kbB_, vb_, vbB_, sb_, sbB_, rw_, rwB_,
         cpp, cppB, caa, caaB, cbb, cbbB) = _cenv(ti)
        P.op("gpsimd", lambda e: e.tensor_tensor(
                out=v3(s3), in0=v3(s1), in1=pv3(bbx, 0), op=ALU.mult),
             reads=[s1B, bbxB], writes=[s3B], tag=f"den1{ti}")
        P.op("vector", lambda e: e.tensor_add(out=s3[:, :], in0=s3[:, :], in1=s2[:, :]),
             reads=[s3B, s2B], writes=[s3B], tag=f"den2{ti}")
        P.op("vector", lambda e: e.tensor_scalar_max(out=s3[:, :], in0=s3[:, :], scalar1=1e-30),
             reads=[s3B], writes=[s3B], tag=f"denc{ti}")
        P.op("scalar", lambda e: e.activation(out=s3[:, :], in_=s3[:, :], func=AF.Ln),
             reads=[s3B], writes=[s3B], tag=f"rdl{ti}")
        P.op("vector", lambda e: e.tensor_scalar_mul(out=s3[:, :], in0=s3[:, :], scalar1=-1.0),
             reads=[s3B], writes=[s3B], tag=f"rdn{ti}")
        P.op("scalar", lambda e: e.activation(out=s3[:, :], in_=s3[:, :], func=AF.Exp),
             reads=[s3B], writes=[s3B], tag=f"rde{ti}")
        P.op("vector", lambda e: e.tensor_tensor(
                out=v3(s1), in0=v3(s1), in1=pv3(aax, 0), op=ALU.mult),
             reads=[s1B, aaxB], writes=[s1B], tag=f"num1{ti}")
        P.op("gpsimd", lambda e: e.tensor_mul(out=s2[:, :], in0=s2[:, :], in1=vb_[:, :]),
             reads=[s2B, vbB_], writes=[s2B], tag=f"num2{ti}")
        P.op("gpsimd", lambda e: e.tensor_add(out=s1[:, :], in0=s1[:, :], in1=s2[:, :]),
             reads=[s1B, s2B], writes=[s1B], tag=f"num3{ti}")
        P.op("vector", lambda e: e.tensor_mul(out=s1[:, :], in0=s1[:, :], in1=s3[:, :]),
             reads=[s1B, s3B], writes=[s1B], tag=f"y{ti}")
        P.op("gpsimd", lambda e, rw_=rw_: e.tensor_mul(out=rw_[:, :], in0=s1[:, :], in1=sb_[:, :]),
             reads=[s1B, sbB_], writes=[rwB_], tag=f"rw{ti}")

    def stD(ti):
        tt, s = divmod(ti, BL)
        t0 = tt * TT
        xt, xtB = xin[ti % 2]
        rw_, rwB_ = rwb[ti % 2]
        for cc in range(EC):
            bank = 3
            col = (cc % 2) * TT
            for dc in range(EC):
                P.op("tensor", lambda e, cc=cc, dc=dc, bank=bank, col=col, rw_=rw_: e.matmul(
                        pb[bank][0][:, col:col + TT],
                        wob[:, dc * 1024 + cc * 128: dc * 1024 + cc * 128 + 128],
                        rw_[:, dc * TT:(dc + 1) * TT],
                        start=(dc == 0), stop=(dc == EC - 1)),
                     reads=[wobB, rwB_], writes=[pb[bank][1]], tag=f"mmo{ti}_{cc}_{dc}")
            if cc % 2 == 1:
                P.op("scalar", lambda e, cc=cc, bank=bank: e.copy(
                        out=xb2[:, (cc - 1) * TT:(cc + 1) * TT], in_=pb[bank][0][:, 0:2 * TT]),
                     reads=[pb[bank][1]], writes=[xb2C[cc // 2]], tag=f"evo{ti}_{cc}")
        P.op("gpsimd", lambda e: e.tensor_mul(out=sqx2[:, :], in0=xb2[:, :], in1=xb2[:, :]),
             reads=[xb2B], writes=[sqx2B], tag=f"aosq{ti}")
        emit_ln_stats(xb2, xb2B, sqx2, sqx2B, stat2, stat2B, muB2, muB2B,
                      rstdB2, rstdB2B, 6, 7, f"ln2_{ti}")
        P.op("vector", lambda e: e.tensor_tensor(
                out=v3(s1), in0=v3(xb2), in1=b3(muB2), op=ALU.subtract),
             reads=[xb2B, muB2B], writes=[s1B], tag=f"posub{ti}")
        P.op("vector", lambda e: e.tensor_tensor(
                out=v3(s1), in0=v3(s1), in1=b3(rstdB2), op=ALU.mult),
             reads=[s1B, rstdB2B], writes=[s1B], tag=f"pomul{ti}")
        dst = x1_h.ap()[s].rearrange("(c p) t -> p c t", p=128)[:, :, t0:t0 + TT]
        P.dma("gpsimd", lambda e, d=dst: e.dma_start(
                out=d, in_=ap3(s1, 0, [[EC * TT, 128], [TT, EC], [1, TT]]),
                accum_op=ALU.add),
              reads=[s1B], writes=[x1_subs[s * 8 + t0 // 256]], tag=f"x1_st{ti}")

    order = [tt * BL + s for tt in range(NT) for s in range(BL)]
    stA(order[0])
    stB(order[0])
    stA(order[1])
    for i, ti in enumerate(order):
        if i > 0:
            stD(order[i - 1])
        if i + 1 < NTT:
            stB(order[i + 1])
        stCa(ti)
        stCb(ti)
        stCc(ti)
        if i + 2 < NTT:
            stA(order[i + 2])
    stD(order[NTT - 1])
    A.reset(base)



SBUF_LIMIT = 229376


def _prep_shared(inputs):
    f = np.ascontiguousarray
    w = {}
    w["WkT"] = f(inputs["L0_attn_Wk"].T)
    w["WvT"] = f(inputs["L0_attn_Wv"].T)
    w["WrT"] = f(inputs["L0_attn_Wr"].T)
    w["WoT"] = f(inputs["L0_attn_Wo"].T)
    for nm, key in (("tmk", "L0_attn_time_mix_k"), ("tmv", "L0_attn_time_mix_v"),
                    ("tmr", "L0_attn_time_mix_r"), ("tdec", "L0_attn_time_decay"),
                    ("tfir", "L0_attn_time_first")):
        w[nm] = f(inputs[key].reshape(EC, 128).T)
    return w



def emit_phase2(nc, P, A, dram, pb):
    TT = 256
    NT = (BL * T) // TT
    HC = HFF // 128

    x1_h, x1_b = dram["x1"]
    x1_subs = dram["x1_subs"]
    x2_h, x2_b = dram["x2"]
    x2_subs = dram["x2_subs"]
    wfc_h, wfc_b = dram["WfcT"]
    wpr_h, wpr_b = dram["WprT"]
    lnw_h, lnw_b = dram["lnw"]
    lnb_h, lnb_b = dram["lnb"]

    base = A.mark()
    wfq, wfqB = A.alloc("p2_wfq", 128, EC * HFF, BF16)
    wpq, wpqB = A.alloc("p2_wpq", 128, HC * D, BF16)
    lnw, lnwB = A.alloc("p2_lnw", 128, HC, F32)
    lnb, lnbB = A.alloc("p2_lnb", 128, HC, F32)
    stat, statB = A.alloc("p2_stat", 1, 6 * TT, F32)
    onesr, onesrB = A.alloc("p2_onesr", 1, 128, F32)
    onesc, onescB = A.alloc("p2_onesc", 128, 1, BF16)
    lnc, lncB = A.alloc("p2_lnc", 1, 2, F32)
    thrB_t, thrB_B = A.alloc("p2_thr", 128, 2, F32)
    muB, muBB = A.alloc("p2_muB", 128, TT, F32)
    rstdB, rstdBB = A.alloc("p2_rstdB", 128, TT, F32)

    tile_base = A.mark()
    qstg = [A.alloc(f"p2_qstg{i}", 128, 4096, F32) for i in range(2)]
    sgn, sgnB = A.alloc("p2_sgn", 128, 4096, BF16)
    bstg = [A.alloc(f"p2_bstg{i}", 128, 4096, BF16) for i in range(2)]
    A.reset(tile_base)
    xin = [A.alloc(f"p2_xin{i}", 128, EC * TT, F32) for i in range(2)]
    xb, xbB = A.alloc("p2_xb", 128, EC * TT, BF16)
    sqx, sqxB = A.alloc("p2_sqx", 128, EC * TT, BF16)
    xln, xlnB = A.alloc("p2_xln", 128, EC * TT, BF16)
    hb, hbB_whole = A.alloc("p2_hb", 128, HC * TT, BF16)
    hbC = [P.buf("sbuf", hbB_whole.start + c * TT * 2,
                 hbB_whole.start + (c + 1) * TT * 2, f"p2_hb_c{c}") for c in range(HC)]
    m2 = A.mark()
    h2b, h2bB = A.alloc("p2_h2b", 128, HC * TT, BF16)
    A.reset(m2)
    xout, xoutB = A.alloc("p2_xout", 128, EC * TT, F32)
    xoutC = [P.buf("sbuf", xoutB.start + g * 2 * TT * 4,
                   xoutB.start + (g + 1) * 2 * TT * 4, f"p2_xout_g{g}") for g in range(4)]
    A.reset(max(A.mark(), m2 + HC * TT * 2))
    s1, s1B = A.alloc("p2_s1", 128, EC * TT, F32)


    P.op("vector", lambda e: e.memset(onesr[:, :], 1.0), writes=[onesrB], tag="p2ones_r")
    P.op("vector", lambda e: e.memset(onesc[:, :], 1.0), writes=[onescB], tag="p2ones_c")
    P.op("vector", lambda e: e.memset(lnc[0:1, 0:1], LN_EPS), writes=[lncB], tag="p2lne")
    P.op("vector", lambda e: e.memset(lnc[0:1, 1:2], -0.5), writes=[lncB], tag="p2lnh")
    P.dma("sync", lambda e: e.dma_start(out=lnw[:, :], in_=lnw_h.ap()),
          reads=[lnw_b], writes=[lnwB], tag="lnw_ld")
    P.dma("sync", lambda e: e.dma_start(out=lnb[:, :], in_=lnb_h.ap()),
          reads=[lnb_b], writes=[lnbB], tag="lnb_ld")

    def quant(w_h, w_b, w16_h, w16_b, ntiles, cols, dest, destB, pfx):
        for i in range(ntiles):
            bs, bsB = bstg[i % 2]
            src = w16_h.ap()[128 * i:128 * (i + 1), :]
            P.dma("sync", lambda e, bs=bs, src=src, cols=cols: e.dma_start(
                    out=bs[:, 0:cols], in_=src),
                  reads=[w16_b], writes=[bsB], tag=f"{pfx}qld{i}")
            P.op("scalar", lambda e, bs=bs, cols=cols: e.activation(
                    out=bs[:, 0:cols], in_=bs[:, 0:cols], func=AF.Abs),
                 reads=[bsB], writes=[bsB], tag=f"{pfx}qabs{i}")
            nch = cols // 512
            for j in range(nch):
                first = (i == 0 and j == 0)
                last = (i == ntiles - 1 and j == nch - 1)
                P.op("tensor", lambda e, bs=bs, j=j, first=first, last=last: e.matmul(
                        pb[4][0][0:1, 0:512], onesc[:, 0:1], bs[:, 512 * j:512 * (j + 1)],
                        start=first, stop=last),
                     reads=[bsB, onescB], writes=[pb[4][1]], tag=f"{pfx}qmm{i}_{j}")
        n = ntiles * 128 * cols
        P.op("vector", lambda e: e.tensor_reduce(out=stat[0:1, 0:1], in_=pb[4][0][0:1, 0:512],
                                                 axis=AX.X, op=ALU.add),
             reads=[pb[4][1]], writes=[statB], tag=f"{pfx}qsum")
        P.op("vector", lambda e, n=n: e.tensor_scalar(out=stat[0:1, 1:2], in0=stat[0:1, 0:1],
                                                      scalar1=0.5 / n, scalar2=0.5e-6,
                                                      op0=ALU.mult, op1=ALU.add),
             reads=[statB], writes=[statB], tag=f"{pfx}qthr")
        P.op("tensor", lambda e: e.matmul(pb[5][0][:, 0:1], onesr[0:1, :], stat[0:1, 1:2],
                                          start=True, stop=True),
             reads=[statB, onesrB], writes=[pb[5][1]], tag=f"{pfx}qbct")
        P.op("scalar", lambda e: e.copy(out=thrB_t[:, 0:1], in_=pb[5][0][:, 0:1]),
             reads=[pb[5][1]], writes=[thrB_B], tag=f"{pfx}qbce")
        for i in range(ntiles):
            qs, qsB = qstg[i % 2]
            src = w_h.ap()[128 * i:128 * (i + 1), :]
            P.dma("sync", lambda e, qs=qs, src=src, cols=cols: e.dma_start(
                    out=qs[:, 0:cols], in_=src),
                  reads=[w_b], writes=[qsB], tag=f"{pfx}tld{i}")
            P.op("scalar", lambda e, qs=qs, cols=cols: e.activation(
                    out=sgn[:, 0:cols], in_=qs[:, 0:cols], func=AF.Sign),
                 reads=[qsB], writes=[sgnB], tag=f"{pfx}tsg{i}")
            P.op("scalar", lambda e, qs=qs, cols=cols: e.activation(
                    out=qs[:, 0:cols], in_=qs[:, 0:cols], func=AF.Abs),
                 reads=[qsB], writes=[qsB], tag=f"{pfx}tab{i}")
            P.op("vector", lambda e, qs=qs, cols=cols: e.tensor_scalar(
                    out=qs[:, 0:cols], in0=qs[:, 0:cols], scalar1=thrB_t[:, 0:1],
                    scalar2=None, op0=ALU.is_ge),
                 reads=[qsB, thrB_B], writes=[qsB], tag=f"{pfx}tge{i}")
            P.op("vector", lambda e, qs=qs, i=i, cols=cols: e.tensor_mul(
                    out=dest[:, i * cols:(i + 1) * cols], in0=qs[:, 0:cols], in1=sgn[:, 0:cols]),
                 reads=[qsB, sgnB], writes=[destB], tag=f"{pfx}tmq{i}")

    wfc16_h, wfc16_b = dram["WfcB16"]
    wpr16_h, wpr16_b = dram["WprB16"]
    quant(wfc_h, wfc_b, wfc16_h, wfc16_b, EC, HFF, wfq, wfqB, "f")
    quant(wpr_h, wpr_b, wpr16_h, wpr16_b, HC, D, wpq, wpqB, "p")

    def emit_ln_stats(src_t, srcB, sq_t, sqB_, nchunks, tag, delta=None):
        srcBl = srcB if isinstance(srcB, list) else [srcB]
        if delta is not None:
            for c in range(nchunks):
                rb_ = srcBl[min(c * len(srcBl) // nchunks, len(srcBl) - 1)]
                P.op("tensor",
                     lambda e, c=c: e.matmul(
                         pb[4][0][0:1, 0:2 * TT], onesc[:, 0:1],
                         ap3(src_t, c * TT, [[nchunks * TT, 128], [delta, 2], [1, TT]]),
                         start=(c == 0), stop=(c == nchunks - 1)),
                     reads=[rb_, sqB_, onescB], writes=[pb[4][1]], tag=f"{tag}_sum")
        else:
            for c in range(nchunks):
                rb_ = srcBl[min(c * len(srcBl) // nchunks, len(srcBl) - 1)]
                P.op("tensor",
                     lambda e, c=c: e.matmul(pb[4][0][0:1, 0:TT], onesc[:, 0:1],
                                             ap3(src_t, c * TT, [[nchunks * TT, 128], [1, TT]]),
                                             start=(c == 0), stop=(c == nchunks - 1)),
                     reads=[rb_, onescB], writes=[pb[4][1]], tag=f"{tag}_sum")
            for c in range(nchunks):
                P.op("tensor",
                     lambda e, c=c: e.matmul(pb[4][0][0:1, TT:2 * TT], onesc[:, 0:1],
                                             ap3(sq_t, c * TT, [[nchunks * TT, 128], [1, TT]]),
                                             start=(c == 0), stop=(c == nchunks - 1)),
                     reads=[sqB_, onescB], writes=[pb[4][1]], tag=f"{tag}_sumsq")
        n = nchunks * 128
        P.op("vector", lambda e, n=n: e.tensor_scalar_mul(out=stat[0:1, 0:2 * TT],
                                                          in0=pb[4][0][0:1, 0:2 * TT], scalar1=1.0 / n),
             reads=[pb[4][1]], writes=[statB], tag=f"{tag}_mu")
        P.op("vector", lambda e: e.tensor_mul(out=stat[0:1, 2 * TT:3 * TT],
                                              in0=stat[0:1, 0:TT], in1=stat[0:1, 0:TT]),
             reads=[statB], writes=[statB], tag=f"{tag}_musq")
        P.op("vector", lambda e: e.tensor_sub(out=stat[0:1, 2 * TT:3 * TT],
                                              in0=stat[0:1, TT:2 * TT], in1=stat[0:1, 2 * TT:3 * TT]),
             reads=[statB], writes=[statB], tag=f"{tag}_var")
        P.op("scalar", lambda e: e.activation(out=stat[0:1, 2 * TT:3 * TT],
                                              in_=stat[0:1, 2 * TT:3 * TT], func=AF.Ln,
                                              bias=lnc[0:1, 0:1]),
             reads=[statB, lncB], writes=[statB], tag=f"{tag}_lnv")
        P.op("scalar", lambda e: e.activation(out=stat[0:1, TT:2 * TT],
                                              in_=stat[0:1, 2 * TT:3 * TT], func=AF.Exp,
                                              scale=lnc[0:1, 1:2]),
             reads=[statB, lncB], writes=[statB], tag=f"{tag}_rstd")
        P.op("tensor", lambda e: e.matmul(pb[5][0][:, 0:2 * TT], onesr[0:1, :],
                                          stat[0:1, 0:2 * TT], start=True, stop=True),
             reads=[statB, onesrB], writes=[pb[5][1]], tag=f"{tag}_bmu")
        P.op("scalar", lambda e: e.copy(out=muB[:, :], in_=pb[5][0][:, 0:TT]),
             reads=[pb[5][1]], writes=[muBB], tag=f"{tag}_emu")
        P.op("scalar", lambda e: e.copy(out=rstdB[:, :], in_=pb[5][0][:, TT:2 * TT]),
             reads=[pb[5][1]], writes=[rstdBB], tag=f"{tag}_ers")

    def b3(t, nch, off=0):
        return ap3(t, off, [[TT, 128], [0, nch], [1, TT]])

    def v3(t, nch, off=0):
        return ap3(t, off, [[nch * TT, 128], [TT, nch], [1, TT]])

    def stA2(ti):
        s, tt = divmod(ti, NT // BL)
        t0 = tt * TT
        xt, xtB = xin[ti % 2]
        src = x1_h.ap()[s].rearrange("(c p) t -> p c t", p=128)[:, :, t0:t0 + TT]
        P.dma("sync", lambda e, d=xt, sr=src: e.dma_start(
                out=ap3(d, 0, [[EC * TT, 128], [TT, EC], [1, TT]]), in_=sr),
              reads=[x1_subs[s * 8 + t0 // 256]], writes=[xtB], tag=f"p2xld{ti}")
        P.op("scalar", lambda e, xt=xt: e.copy(out=xb[:, :], in_=xt[:, :]),
             reads=[xtB], writes=[xbB], tag=f"p2xb{ti}")
        P.op("gpsimd", lambda e: e.tensor_mul(out=sqx[:, :], in0=xb[:, :], in1=xb[:, :]),
             reads=[xbB], writes=[sqxB], tag=f"p2sq{ti}")
        emit_ln_stats(xb, xbB, sqx, sqxB, EC, f"p2l1_{ti}")
        P.op("vector", lambda e, xt=xt: e.tensor_tensor(out=v3(s1, EC), in0=v3(xt, EC),
                                                        in1=b3(muB, EC), op=ALU.subtract),
             reads=[xtB, muBB], writes=[s1B], tag=f"p2ls{ti}")
        P.op("vector", lambda e: e.tensor_tensor(out=v3(xln, EC), in0=v3(s1, EC),
                                                 in1=b3(rstdB, EC), op=ALU.mult),
             reads=[s1B, rstdBB], writes=[xlnB], tag=f"p2lm{ti}")

    def stF2(ti):
        for hc in range(HC):
            bank = (hc // 2) % 2
            col = (hc % 2) * TT
            for ec in range(EC):
                P.op("tensor", lambda e, hc=hc, ec=ec, bank=bank, col=col: e.matmul(
                        pb[bank][0][:, col:col + TT],
                        wfq[:, ec * HFF + hc * 128: ec * HFF + hc * 128 + 128],
                        xln[:, ec * TT:(ec + 1) * TT],
                        start=(ec == 0), stop=(ec == EC - 1)),
                     reads=[wfqB, xlnB], writes=[pb[bank][1]], tag=f"p2fc{ti}_{hc}_{ec}")
            if hc % 2 == 1:
                P.op("scalar", lambda e, hc=hc, bank=bank: e.activation(
                        out=hb[:, (hc - 1) * TT:(hc + 1) * TT], in_=pb[bank][0][:, 0:2 * TT],
                        func=AF.Gelu),
                     reads=[pb[bank][1]], writes=[hbC[hc - 1], hbC[hc]], tag=f"p2ge{ti}_{hc}")

    def stH2(ti):
        P.op("gpsimd", lambda e: e.tensor_mul(out=h2b[:, :], in0=hb[:, :], in1=hb[:, :]),
             reads=list(hbC), writes=[h2bB], tag=f"p2hsq{ti}")
        emit_ln_stats(hb, hbC, h2b, h2bB, HC, f"p2l2_{ti}")
        for g in range(4):
            off = g * 8 * TT
            P.op("vector", lambda e, off=off: e.tensor_tensor(
                    out=v3(s1, EC),
                    in0=ap3(hb, off, [[HC * TT, 128], [TT, 8], [1, TT]]),
                    in1=b3(muB, EC), op=ALU.subtract),
                 reads=hbC[8 * g:8 * g + 8] + [muBB], writes=[s1B], tag=f"p2hs{ti}_{g}")
            P.op("vector", lambda e: e.tensor_tensor(
                    out=v3(s1, EC), in0=v3(s1, EC), in1=b3(rstdB, EC), op=ALU.mult),
                 reads=[s1B, rstdBB], writes=[s1B], tag=f"p2hm{ti}_{g}")
            for j in range(8):
                hc = g * 8 + j
                P.op("vector", lambda e, hc=hc, j=j: e.tensor_scalar(
                        out=hb[:, hc * TT:(hc + 1) * TT], in0=s1[:, j * TT:(j + 1) * TT],
                        scalar1=lnw[:, hc:hc + 1], scalar2=lnb[:, hc:hc + 1],
                        op0=ALU.mult, op1=ALU.add),
                     reads=[s1B, lnwB, lnbB], writes=[hbC[hc]], tag=f"p2ha{ti}_{hc}")

    def stC2(ti):
        s, tt = divmod(ti, NT // BL)
        t0 = tt * TT
        xt, xtB = xin[ti % 2]
        for dc in range(EC):
            bank = 2 + (dc // 2) % 2
            col = (dc % 2) * TT
            for hc in range(HC):
                P.op("tensor", lambda e, dc=dc, hc=hc, bank=bank, col=col: e.matmul(
                        pb[bank][0][:, col:col + TT],
                        wpq[:, hc * D + dc * 128: hc * D + dc * 128 + 128],
                        hb[:, hc * TT:(hc + 1) * TT],
                        start=(hc == 0), stop=(hc == HC - 1)),
                     reads=[wpqB, hbC[hc]], writes=[pb[bank][1]], tag=f"p2pj{ti}_{dc}_{hc}")
            if dc % 2 == 1:
                P.op("scalar", lambda e, dc=dc, bank=bank: e.copy(
                        out=xout[:, (dc - 1) * TT:(dc + 1) * TT], in_=pb[bank][0][:, 0:2 * TT]),
                     reads=[pb[bank][1]], writes=[xoutC[dc // 2]], tag=f"p2pe{ti}_{dc}")
        P.op("scalar", lambda e: e.copy(out=xb[:, :], in_=xout[:, :]),
             reads=list(xoutC), writes=[xbB], tag=f"p2ob{ti}")
        P.op("gpsimd", lambda e: e.tensor_mul(out=sqx[:, :], in0=xb[:, :], in1=xb[:, :]),
             reads=[xbB], writes=[sqxB], tag=f"p2osq{ti}")
        emit_ln_stats(xb, xbB, sqx, sqxB, EC, f"p2l3_{ti}")
        P.op("vector", lambda e: e.tensor_tensor(out=v3(s1, EC), in0=v3(xout, EC),
                                                 in1=b3(muB, EC), op=ALU.subtract),
             reads=xoutC + [muBB], writes=[s1B], tag=f"p2os{ti}")
        P.op("vector", lambda e: e.tensor_tensor(out=v3(s1, EC), in0=v3(s1, EC),
                                                 in1=b3(rstdB, EC), op=ALU.mult),
             reads=[s1B, rstdBB], writes=[s1B], tag=f"p2om{ti}")
        P.op("vector", lambda e, xt=xt: e.tensor_add(out=xt[:, :], in0=xt[:, :], in1=s1[:, :]),
             reads=[s1B, xtB], writes=[xtB], tag=f"p2rs{ti}")
        dst = x2_h.ap()[s].rearrange("(c p) t -> p c t", p=128)[:, :, t0:t0 + TT]
        P.dma("gpsimd", lambda e, xt=xt, d=dst: e.dma_start(
                out=d, in_=ap3(xt, 0, [[EC * TT, 128], [TT, EC], [1, TT]])),
              reads=[xtB], writes=[x2_subs[s * 8 + t0 // 256]], tag=f"p2st{ti}")

    stA2(0)
    stF2(0)
    stA2(1)
    for ti in range(NT):
        stH2(ti)
        stC2(ti)
        if ti + 1 < NT:
            stF2(ti + 1)
        if ti + 2 < NT:
            stA2(ti + 2)
    A.reset(base)



def emit_phase3(nc, P, A, dram, pb):
    TT = 512
    NT = (BL * T) // TT
    NPR = H // 2

    x2_h, x2_b = dram["x2"]
    x2_subs = dram["x2_subs"]
    qh_h, qh_b = dram["qh"]
    qh_subs = dram["qh_subs"]
    kh_h, kh_b = dram["kh"]
    kh_subs = dram["kh_subs"]
    vh_h, vh_b = dram["vh"]
    vh_subs = dram["vh_subs"]
    wq_h, wq_b = dram["wqr"]
    wk_h, wk_b = dram["wkr"]
    wv_h, wv_b = dram["wvr"]

    base = A.mark()
    wqb, wqbB = A.alloc("p3_wq", 128, EC * D, BF16)
    wkb, wkbB = A.alloc("p3_wk", 128, EC * D, BF16)
    wvb, wvbB = A.alloc("p3_wv", 128, EC * D, BF16)
    xin = [A.alloc(f"p3_xin{i}", 128, EC * TT, F32) for i in range(2)]
    xb, xbB = A.alloc("p3_xb", 128, EC * TT, BF16)
    sqx, sqxB = A.alloc("p3_sqx", 128, EC * TT, BF16)
    xln, xlnB = A.alloc("p3_xln", 128, EC * TT, BF16)
    s1, s1B = A.alloc("p3_s1", 128, EC * TT, F32)
    qstg, qstgB = A.alloc("p3_qstg", 128, NPR * TT, BF16)
    kstg, kstgB = A.alloc("p3_kstg", 128, NPR * TT, BF16)
    vstg, vstgB = A.alloc("p3_vstg", 128, 4 * H * 65, BF16)
    qstgC = [P.buf("sbuf", qstgB.start + p * TT * 2, qstgB.start + (p + 1) * TT * 2,
                   f"p3_qstg_c{p}") for p in range(NPR)]
    kstgC = [P.buf("sbuf", kstgB.start + p * TT * 2, kstgB.start + (p + 1) * TT * 2,
                   f"p3_kstg_c{p}") for p in range(NPR)]
    vstgC = [P.buf("sbuf", vstgB.start + w * 520 * 2, vstgB.start + (w + 1) * 520 * 2,
                   f"p3_vstg_c{w}") for w in range(8)]
    muB, muBB = A.alloc("p3_muB", 128, TT, F32)
    rstdB, rstdBB = A.alloc("p3_rstdB", 128, TT, F32)
    stat, statB = A.alloc("p3_stat", 1, 6 * TT, F32)
    onesr, onesrB = A.alloc("p3_onesr", 1, 128, F32)
    onesc, onescB = A.alloc("p3_onesc", 128, 1, BF16)
    lnc, lncB = A.alloc("p3_lnc", 1, 2, F32)


    P.op("vector", lambda e: e.memset(onesr[:, :], 1.0), writes=[onesrB], tag="p3or")
    P.op("vector", lambda e: e.memset(onesc[:, :], 1.0), writes=[onescB], tag="p3oc")
    P.op("vector", lambda e: e.memset(lnc[0:1, 0:1], LN_EPS), writes=[lncB], tag="p3lne")
    P.op("vector", lambda e: e.memset(lnc[0:1, 1:2], -0.5), writes=[lncB], tag="p3lnh")
    P.op("vector", lambda e: e.memset(
            ap3(vstg, 64, [[4 * H * 65, 128], [1040, 4], [65, H], [1, 1]]), 1.0),
         writes=[vstgB], tag="p3vones")
    for (hb_, hbuf), (sb, sbuf) in (((wq_h, wq_b), (wqb, wqbB)),
                                    ((wk_h, wk_b), (wkb, wkbB)),
                                    ((wv_h, wv_b), (wvb, wvbB))):
        src = hb_.ap().rearrange("(c p) d -> p c d", p=128)
        P.dma("gpsimd", lambda e, d=sb, s=src: e.dma_start(
                out=ap3(d, 0, [[EC * D, 128], [D, EC], [1, D]]), in_=s),
              reads=[hbuf], writes=[sbuf], tag="p3wld")

    def emit_ln_stats(src_t, srcB, sq_t, sqB_, tag):
        for c in range(EC):
            P.op("tensor", lambda e, c=c: e.matmul(pb[4][0][0:1, 0:TT], onesc[:, 0:1],
                                                   ap3(src_t, c * TT, [[EC * TT, 128], [1, TT]]),
                                                   start=(c == 0), stop=(c == EC - 1)),
                 reads=[srcB, onescB], writes=[pb[4][1]], tag=f"{tag}_sum")
        for c in range(EC):
            P.op("tensor", lambda e, c=c: e.matmul(pb[5][0][0:1, 0:TT], onesc[:, 0:1],
                                                   ap3(sq_t, c * TT, [[EC * TT, 128], [1, TT]]),
                                                   start=(c == 0), stop=(c == EC - 1)),
                 reads=[sqB_, onescB], writes=[pb[5][1]], tag=f"{tag}_ssq")
        n = EC * 128
        P.op("vector", lambda e: e.tensor_scalar_mul(out=stat[0:1, 0:TT],
                                                     in0=pb[4][0][0:1, 0:TT], scalar1=1.0 / n),
             reads=[pb[4][1]], writes=[statB], tag=f"{tag}_mu")
        P.op("vector", lambda e: e.tensor_scalar_mul(out=stat[0:1, TT:2 * TT],
                                                     in0=pb[5][0][0:1, 0:TT], scalar1=1.0 / n),
             reads=[pb[5][1]], writes=[statB], tag=f"{tag}_ex2")
        P.op("vector", lambda e: e.tensor_mul(out=stat[0:1, 2 * TT:3 * TT],
                                              in0=stat[0:1, 0:TT], in1=stat[0:1, 0:TT]),
             reads=[statB], writes=[statB], tag=f"{tag}_musq")
        P.op("vector", lambda e: e.tensor_sub(out=stat[0:1, 3 * TT:4 * TT],
                                              in0=stat[0:1, TT:2 * TT], in1=stat[0:1, 2 * TT:3 * TT]),
             reads=[statB], writes=[statB], tag=f"{tag}_var")
        P.op("scalar", lambda e: e.activation(out=stat[0:1, 3 * TT:4 * TT],
                                              in_=stat[0:1, 3 * TT:4 * TT], func=AF.Ln,
                                              bias=lnc[0:1, 0:1]),
             reads=[statB, lncB], writes=[statB], tag=f"{tag}_lnv")
        P.op("scalar", lambda e: e.activation(out=stat[0:1, 5 * TT:6 * TT],
                                              in_=stat[0:1, 3 * TT:4 * TT], func=AF.Exp,
                                              scale=lnc[0:1, 1:2]),
             reads=[statB, lncB], writes=[statB], tag=f"{tag}_rstd")
        P.op("tensor", lambda e: e.matmul(pb[6][0][:, 0:TT], onesr[0:1, :],
                                          stat[0:1, 0:TT], start=True, stop=True),
             reads=[statB, onesrB], writes=[pb[6][1]], tag=f"{tag}_bmu")
        P.op("tensor", lambda e: e.matmul(pb[7][0][:, 0:TT], onesr[0:1, :],
                                          stat[0:1, 5 * TT:6 * TT], start=True, stop=True),
             reads=[statB, onesrB], writes=[pb[7][1]], tag=f"{tag}_brs")
        P.op("scalar", lambda e: e.copy(out=muB[:, :], in_=pb[6][0][:, 0:TT]),
             reads=[pb[6][1]], writes=[muBB], tag=f"{tag}_emu")
        P.op("scalar", lambda e: e.copy(out=rstdB[:, :], in_=pb[7][0][:, 0:TT]),
             reads=[pb[7][1]], writes=[rstdBB], tag=f"{tag}_ers")

    for ti in range(NT):
        s, tt = divmod(ti, NT // BL)
        t0 = tt * TT
        xt, xtB = xin[ti % 2]
        src = x2_h.ap()[s].rearrange("(c p) t -> p c t", p=128)[:, :, t0:t0 + TT]
        P.dma("sync", lambda e, d=xt, sr=src: e.dma_start(
                out=ap3(d, 0, [[EC * TT, 128], [TT, EC], [1, TT]]), in_=sr),
              reads=[x2_subs[s * 8 + t0 // 256], x2_subs[s * 8 + t0 // 256 + 1]],
              writes=[xtB], tag=f"p3xld{ti}")
        P.op("scalar", lambda e, xt=xt: e.copy(out=xb[:, :], in_=xt[:, :]),
             reads=[xtB], writes=[xbB], tag=f"p3xb{ti}")
        P.op("scalar", lambda e: e.activation(out=sqx[:, :], in_=xb[:, :], func=AF.Square),
             reads=[xbB], writes=[sqxB], tag=f"p3sq{ti}")
        emit_ln_stats(xb, xbB, sqx, sqxB, f"p3l_{ti}")
        P.op("vector", lambda e, xt=xt: e.tensor_tensor(
                out=ap3(s1, 0, [[EC * TT, 128], [TT, EC], [1, TT]]),
                in0=ap3(xt, 0, [[EC * TT, 128], [TT, EC], [1, TT]]),
                in1=ap3(muB, 0, [[TT, 128], [0, EC], [1, TT]]), op=ALU.subtract),
             reads=[xtB, muBB], writes=[s1B], tag=f"p3ls{ti}")
        P.op("vector", lambda e: e.tensor_tensor(
                out=ap3(xln, 0, [[EC * TT, 128], [TT, EC], [1, TT]]),
                in0=ap3(s1, 0, [[EC * TT, 128], [TT, EC], [1, TT]]),
                in1=ap3(rstdB, 0, [[TT, 128], [0, EC], [1, TT]]), op=ALU.mult),
             reads=[s1B, rstdBB], writes=[xlnB], tag=f"p3lm{ti}")
        for pr in range(NPR):
            bank = pr % 2
            for ec in range(EC):
                P.op("tensor", lambda e, pr=pr, ec=ec, bank=bank: e.matmul(
                        pb[bank][0][:, 0:TT],
                        wqb[:, ec * D + pr * 128: ec * D + pr * 128 + 128],
                        xln[:, ec * TT:(ec + 1) * TT],
                        start=(ec == 0), stop=(ec == EC - 1)),
                     reads=[wqbB, xlnB], writes=[pb[bank][1]], tag=f"p3q{ti}_{pr}_{ec}")
            P.op("vector", lambda e, pr=pr, bank=bank: e.tensor_scalar_mul(
                    out=qstg[:, pr * TT:(pr + 1) * TT], in0=pb[bank][0][:, 0:TT],
                    scalar1=0.125),
                 reads=[pb[bank][1]], writes=[qstgC[pr]], tag=f"p3qe{ti}_{pr}")
            bank = 2 + pr % 2
            for ec in range(EC):
                P.op("tensor", lambda e, pr=pr, ec=ec, bank=bank: e.matmul(
                        pb[bank][0][:, 0:TT],
                        wkb[:, ec * D + pr * 128: ec * D + pr * 128 + 128],
                        xln[:, ec * TT:(ec + 1) * TT],
                        start=(ec == 0), stop=(ec == EC - 1)),
                     reads=[wkbB, xlnB], writes=[pb[bank][1]], tag=f"p3k{ti}_{pr}_{ec}")
            P.op("scalar", lambda e, pr=pr, bank=bank: e.copy(
                    out=kstg[:, pr * TT:(pr + 1) * TT], in_=pb[bank][0][:, 0:TT]),
                 reads=[pb[bank][1]], writes=[kstgC[pr]], tag=f"p3ke{ti}_{pr}")
        for j in range(TT // 128):
            for half in range(2):
                bank = 2 * half
                for ec in range(EC):
                    P.op("tensor", lambda e, j=j, half=half, ec=ec, bank=bank: e.matmul(
                            pb[bank][0][:, 0:512],
                            xln[:, ec * TT + j * 128: ec * TT + j * 128 + 128],
                            wvb[:, ec * D + half * 512: ec * D + half * 512 + 512],
                            start=(ec == 0), stop=(ec == EC - 1)),
                         reads=[wvbB, xlnB], writes=[pb[bank][1]], tag=f"p3v{ti}_{j}_{half}_{ec}")
                P.op("vector", lambda e, j=j, half=half, bank=bank: e.tensor_copy(
                        out=ap3(vstg, j * 1040 + half * 8 * 65, [[4 * H * 65, 128], [65, 8], [1, 64]]),
                        in_=ap3(pb[bank][0], 0, [[512, 128], [64, 8], [1, 64]])),
                     reads=[pb[bank][1]], writes=[vstgC[2 * j + half]], tag=f"p3ve{ti}_{j}_{half}")
        qdst = qh_h.ap()[s].rearrange("(h2 hh) d t -> (hh d) h2 t", hh=2)[:, :, t0:t0 + TT]
        P.dma("gpsimd", lambda e, d=qdst: e.dma_start(
                out=d, in_=ap3(qstg, 0, [[NPR * TT, 128], [TT, NPR], [1, TT]])),
              reads=[qstgB], writes=[qh_subs[s * 4 + t0 // 512]], tag=f"p3qs{ti}")
        kdst = kh_h.ap()[s].rearrange("(h2 hh) d t -> (hh d) h2 t", hh=2)[:, :, t0:t0 + TT]
        P.dma("gpsimd", lambda e, d=kdst: e.dma_start(
                out=d, in_=ap3(kstg, 0, [[NPR * TT, 128], [TT, NPR], [1, TT]])),
              reads=[kstgB], writes=[kh_subs[s * 4 + t0 // 512]], tag=f"p3ks{ti}")
        vdst = vh_h.ap()[s].rearrange("j p c -> p j c")[:, 4 * tt:4 * tt + 4, :]
        P.dma("gpsimd", lambda e, d=vdst: e.dma_start(
                out=d, in_=ap3(vstg, 0, [[4 * H * 65, 128], [1040, 4], [1, 1040]])),
              reads=[vstgB], writes=[vh_subs[s * 4 + t0 // 512]], tag=f"p3vs{ti}")
    A.reset(base)



def emit_phase4(nc, P, A, dram, pb):
    QB = 512
    NQ = T // QB
    x2_h, x2_b = dram["x2"]
    x2_subs = dram["x2_subs"]
    qh_h, qh_b = dram["qh"]
    qh_subs = dram["qh_subs"]
    kh_h, kh_b = dram["kh"]
    kh_subs = dram["kh_subs"]
    vh_h, vh_b = dram["vh"]
    vh_subs = dram["vh_subs"]
    out_h, out_b = dram["out"]
    wo_h, wo_b = dram["woT"]
    bw_h, bw_b = dram["bwT"]
    bb_h, bb_b = dram["bb"]
    la_h, la_b = dram["loraA"]
    lb_h, lb_b = dram["loraB"]
    rw_h, rw_b = dram["rwT"]
    rb_h, rb_b = dram["rb"]

    base = A.mark()
    wob, wobB = A.alloc("p4_wo", 128, EC * D, BF16)
    bwb, bwbB = A.alloc("p4_bw", 128, EC * D, BF16)
    bbt, bbtB = A.alloc("p4_bb", 128, EC, F32)
    lab, labB = A.alloc("p4_la", 128, EC * 8, BF16)
    lbb, lbbB = A.alloc("p4_lb", 64, D, BF16)
    rwt, rwtB = A.alloc("p4_rw", 128, EC * 8, BF16)
    rbt, rbtB = A.alloc("p4_rb", 8, 1, F32)
    vse, vseB = A.alloc("p4_vse", 128, (T // 128) * H * 65, BF16)
    khw = [A.alloc(f"p4_khw{i}", 65, T, BF16) for i in range(3)]
    qa, qaB_whole = A.alloc("p4_qa", 65, H * QB, BF16)
    qaB = [P.buf("sbuf", qaB_whole.start + h * QB * 2,
                 qaB_whole.start + (h + 1) * QB * 2, f"p4_qa_h{h}") for h in range(H)]
    x2t, x2tB = A.alloc("p4_x2t", 128, EC * QB, F32)
    ptb = [A.alloc(f"p4_ptb{i}", 128, QB, BF16) for i in range(8)]
    otb, otbB = A.alloc("p4_otb", 128, EC * QB, BF16)
    m4 = A.mark()
    aout, aoutB = A.alloc("p4_aout", 128, EC * QB, F32)
    A.reset(m4)
    fout, foutB = A.alloc("p4_fout", 128, EC * QB, F32)
    x3ln, x3lnB = A.alloc("p4_x3ln", 128, EC * QB, BF16)
    xb, xbB = A.alloc("p4_xb", 128, EC * QB, BF16)
    sqx, sqxB = A.alloc("p4_sqx", 128, EC * QB, BF16)
    s1, s1B = A.alloc("p4_s1", 128, EC * QB // 2, F32)
    muB, muBB = A.alloc("p4_muB", 128, QB, F32)
    rstdB, rstdBB = A.alloc("p4_rstdB", 128, QB, F32)
    stat, statB = A.alloc("p4_stat", 1, 3 * QB, F32)
    onesr, onesrB = A.alloc("p4_onesr", 1, 128, F32)
    onesrb, onesrbB = A.alloc("p4_onesrb", 1, 64, BF16)
    onesc, onescB = A.alloc("p4_onesc", 128, 1, BF16)
    lnc, lncB = A.alloc("p4_lnc", 1, 2, F32)
    rbc = [A.alloc(f"p4_rbc{i}", 64, QB, F32) for i in range(2)]
    srow = [A.alloc(f"p4_srow{i}", 1, QB, BF16) for i in range(2)]
    _kq = A.alloc("p4_ksqb", 64, 512, BF16)
    ksqb = [_kq, _kq]
    qsq4 = A.alloc("p4_qsq4", 64, 4 * QB, BF16)
    mkrow = [A.alloc(f"p4_mkrow{i}", 1, 8, F32) for i in range(2)]
    mks, mksB = A.alloc("p4_mks", 1, 2 * H, F32)
    ones64, ones64B = A.alloc("p4_ones64", 64, 1, BF16)
    i128, i128B = A.alloc("p4_i128", 128, 128, BF16)
    selE, selEB = A.alloc("p4_selE", 8, 64, BF16)
    selR, selRB = A.alloc("p4_selR", 8, 64, BF16)
    it1, it1B = A.alloc("p4_it1", 128, 64, F32)
    itp, itpB = A.alloc("p4_itp", 128, 1, F32)
    gb, gbB = A.alloc("p4_gb", 128, 4 * 8, BF16)
    gsum, gsumB = A.alloc("p4_gsum", 128, 2, F32)
    gex, gexB = A.alloc("p4_gex", 128, 8, F32)
    rlb, rlbB = A.alloc("p4_rlb", 8, QB, BF16)
    gTt, gTtB = A.alloc("p4_gT", 8, QB, BF16)
    mb, mbB = A.alloc("p4_mb", 8, QB, BF16)
    geb, gebB = A.alloc("p4_geb", 64, QB, BF16)
    meb, mebB = A.alloc("p4_meb", 64, QB, BF16)
    p64, p64B = A.alloc("p4_p64", 64, QB, BF16)


    P.op("vector", lambda e: e.memset(onesr[:, :], 1.0), writes=[onesrB], tag="p4or")
    P.op("vector", lambda e: e.memset(onesrb[:, :], 1.0), writes=[onesrbB], tag="p4orb")
    P.op("vector", lambda e: e.memset(ones64[:, :], 1.0), writes=[ones64B], tag="p4o64")
    P.op("vector", lambda e: e.memset(onesc[:, :], 1.0), writes=[onescB], tag="p4oc")
    P.op("vector", lambda e: e.memset(lnc[0:1, 0:1], LN_EPS), writes=[lncB], tag="p4lne")
    P.op("vector", lambda e: e.memset(lnc[0:1, 1:2], -0.5), writes=[lncB], tag="p4lnh")
    P.op("vector", lambda e: e.memset(i128[:, :], 1.0), writes=[i128B], tag="p4i1")
    P.op("gpsimd", lambda e: e.affine_select(
            out=i128[:, :], in_=i128[:, :], pattern=[[-1, 128]],
            compare_op=ALU.is_equal, fill=0.0, base=0, channel_multiplier=1),
         reads=[i128B], writes=[i128B], tag="p4i2")
    P.op("gpsimd", lambda e: e.iota(it1[0:8, :], [[1, 8], [0, 8]],
                                    channel_multiplier=0,
                                    allow_small_or_imprecise_dtypes=True),
         writes=[it1B], tag="p4it1")
    P.op("gpsimd", lambda e: e.iota(itp[0:8, :], [[0, 1]], channel_multiplier=1,
                                    allow_small_or_imprecise_dtypes=True),
         writes=[itpB], tag="p4itp")
    P.op("vector", lambda e: e.tensor_scalar(out=selE[:, :], in0=it1[0:8, :],
                                             scalar1=itp[0:8, :], scalar2=None,
                                             op0=ALU.is_equal),
         reads=[it1B, itpB], writes=[selEB], tag="p4se")
    P.op("gpsimd", lambda e: e.iota(it1[0:8, :], [[0, 8], [1, 8]],
                                    channel_multiplier=0,
                                    allow_small_or_imprecise_dtypes=True),
         writes=[it1B], tag="p4it2")
    P.op("vector", lambda e: e.tensor_scalar(out=selR[:, :], in0=it1[0:8, :],
                                             scalar1=itp[0:8, :], scalar2=None,
                                             op0=ALU.is_equal),
         reads=[it1B, itpB], writes=[selRB], tag="p4sr")
    for (hb_, hbuf), (sb, sbuf) in (((wo_h, wo_b), (wob, wobB)),
                                    ((bw_h, bw_b), (bwb, bwbB))):
        src = hb_.ap().rearrange("(c p) d -> p c d", p=128)
        P.dma("gpsimd", lambda e, d=sb, s=src: e.dma_start(
                out=ap3(d, 0, [[EC * D, 128], [D, EC], [1, D]]), in_=s),
              reads=[hbuf], writes=[sbuf], tag="p4wld")
    P.dma("sync", lambda e: e.dma_start(out=bbt[:, :], in_=bb_h.ap()),
          reads=[bb_b], writes=[bbtB], tag="p4bb")
    src = la_h.ap().rearrange("(c p) r -> p c r", p=128)
    P.dma("gpsimd", lambda e, s=src: e.dma_start(
            out=ap3(lab, 0, [[EC * 8, 128], [8, EC], [1, 8]]), in_=s),
          reads=[la_b], writes=[labB], tag="p4la")
    P.dma("gpsimd", lambda e: e.dma_start(out=lbb[:, :], in_=lb_h.ap()),
          reads=[lb_b], writes=[lbbB], tag="p4lb")
    src = rw_h.ap().rearrange("(c p) r -> p c r", p=128)
    P.dma("gpsimd", lambda e, s=src: e.dma_start(
            out=ap3(rwt, 0, [[EC * 8, 128], [8, EC], [1, 8]]), in_=s),
          reads=[rw_b], writes=[rwtB], tag="p4rw")
    P.dma("sync", lambda e: e.dma_start(out=rbt[:, :], in_=rb_h.ap()),
          reads=[rb_b], writes=[rbtB], tag="p4rb")
    for i in range(3):
        P.op("vector", lambda e, i=i: e.memset(khw[i][0][64:65, :], 1.0),
             writes=[khw[i][1]], tag=f"p4ko{i}")

    def emit_ln_stats(src_t, srcB, sq_t, sqB_, tag):
        for c in range(EC):
            P.op("tensor", lambda e, c=c: e.matmul(pb[6][0][0:1, 0:QB], onesc[:, 0:1],
                                                   ap3(src_t, c * QB, [[EC * QB, 128], [1, QB]]),
                                                   start=(c == 0), stop=(c == EC - 1)),
                 reads=[srcB, onescB], writes=[pb[6][1]], tag=f"{tag}_sum")
        for c in range(EC):
            P.op("tensor", lambda e, c=c: e.matmul(pb[7][0][0:1, 0:QB], onesc[:, 0:1],
                                                   ap3(sq_t, c * QB, [[EC * QB, 128], [1, QB]]),
                                                   start=(c == 0), stop=(c == EC - 1)),
                 reads=[sqB_, onescB], writes=[pb[7][1]], tag=f"{tag}_ssq")
        n = EC * 128
        P.op("vector", lambda e: e.tensor_scalar_mul(out=stat[0:1, 0:QB],
                                                     in0=pb[6][0][0:1, 0:QB], scalar1=1.0 / n),
             reads=[pb[6][1]], writes=[statB], tag=f"{tag}_mu")
        P.op("vector", lambda e: e.tensor_scalar_mul(out=stat[0:1, QB:2 * QB],
                                                     in0=pb[7][0][0:1, 0:QB], scalar1=1.0 / n),
             reads=[pb[7][1]], writes=[statB], tag=f"{tag}_ex2")
        P.op("vector", lambda e: e.tensor_mul(out=stat[0:1, 2 * QB:3 * QB],
                                              in0=stat[0:1, 0:QB], in1=stat[0:1, 0:QB]),
             reads=[statB], writes=[statB], tag=f"{tag}_musq")
        P.op("vector", lambda e: e.tensor_sub(out=stat[0:1, QB:2 * QB],
                                              in0=stat[0:1, QB:2 * QB], in1=stat[0:1, 2 * QB:3 * QB]),
             reads=[statB], writes=[statB], tag=f"{tag}_var")
        P.op("scalar", lambda e: e.activation(out=stat[0:1, QB:2 * QB],
                                              in_=stat[0:1, QB:2 * QB], func=AF.Ln,
                                              bias=lnc[0:1, 0:1]),
             reads=[statB, lncB], writes=[statB], tag=f"{tag}_lnv")
        P.op("scalar", lambda e: e.activation(out=stat[0:1, 2 * QB:3 * QB],
                                              in_=stat[0:1, QB:2 * QB], func=AF.Exp,
                                              scale=lnc[0:1, 1:2]),
             reads=[statB, lncB], writes=[statB], tag=f"{tag}_rstd")
        P.op("tensor", lambda e: e.matmul(pb[6][0][:, 0:QB], onesr[0:1, :],
                                          stat[0:1, 0:QB], start=True, stop=True),
             reads=[statB, onesrB], writes=[pb[6][1]], tag=f"{tag}_bmu")
        P.op("tensor", lambda e: e.matmul(pb[7][0][:, 0:QB], onesr[0:1, :],
                                          stat[0:1, 2 * QB:3 * QB], start=True, stop=True),
             reads=[statB, onesrB], writes=[pb[7][1]], tag=f"{tag}_brs")
        P.op("scalar", lambda e: e.copy(out=muB[:, :], in_=pb[6][0][:, 0:QB]),
             reads=[pb[6][1]], writes=[muBB], tag=f"{tag}_emu")
        P.op("scalar", lambda e: e.copy(out=rstdB[:, :], in_=pb[7][0][:, 0:QB]),
             reads=[pb[7][1]], writes=[rstdBB], tag=f"{tag}_ers")

    def ln_apply(src_t, srcB, dst_t, dstB, tag):
        for g in range(2):
            off = g * 4 * QB
            P.op("vector", lambda e, off=off, src_t=src_t: e.tensor_tensor(
                    out=ap3(s1, 0, [[4 * QB, 128], [QB, 4], [1, QB]]),
                    in0=ap3(src_t, off, [[EC * QB, 128], [QB, 4], [1, QB]]),
                    in1=ap3(muB, 0, [[QB, 128], [0, 4], [1, QB]]), op=ALU.subtract),
                 reads=[srcB, muBB], writes=[s1B], tag=f"{tag}_s{g}")
            P.op("vector", lambda e, off=off, dst_t=dst_t: e.tensor_tensor(
                    out=ap3(dst_t, off, [[EC * QB, 128], [QB, 4], [1, QB]]),
                    in0=ap3(s1, 0, [[4 * QB, 128], [QB, 4], [1, QB]]),
                    in1=ap3(rstdB, 0, [[QB, 128], [0, 4], [1, QB]]), op=ALU.mult),
                 reads=[s1B, rstdBB], writes=[dstB], tag=f"{tag}_m{g}")

    for s in range(BL):
        vsrc = vh_h.ap()[s].rearrange("j p c -> p j c")
        P.dma("sync", lambda e, sr=vsrc: e.dma_start(
                out=ap3(vse, 0, [[(T // 128) * H * 65, 128], [1040, T // 128], [1, 1040]]), in_=sr),
              reads=[vh_subs[s * 4 + i] for i in range(4)], writes=[vseB], tag=f"p4vld{s}")
        for qb in range(NQ):
            blk = f"b{s}_{qb}"
            qsrc = qh_h.ap()[s].rearrange("h d t -> d h t")[:, :, qb * QB:(qb + 1) * QB]
            P.dma("sync", lambda e, sr=qsrc: e.dma_start(
                    out=ap3(qa, 0, [[H * QB, 64], [QB, H], [1, QB]]), in_=sr),
                  reads=[qh_subs[s * 4 + qb]], writes=[qaB_whole], tag=f"{blk}_qld")
            xsrc = x2_h.ap()[s].rearrange("(c p) t -> p c t", p=128)[:, :, qb * QB:(qb + 1) * QB]
            P.dma("sync", lambda e, sr=xsrc: e.dma_start(
                    out=ap3(x2t, 0, [[EC * QB, 128], [QB, EC], [1, QB]]), in_=sr),
                  reads=[x2_subs[s * 8 + qb * 2], x2_subs[s * 8 + qb * 2 + 1]],
                  writes=[x2tB], tag=f"{blk}_xld")
            mext = (qb + 1) * 512
            nk128 = mext // 128

            def kload(h):
                kb, kbB = khw[h % 3]
                ksrc = kh_h.ap()[s][h][:, 0:T]
                P.dma("sync", lambda e, sr=ksrc, kb=kb: e.dma_start(
                        out=kb[0:64, 0:T], in_=sr),
                      reads=[kh_subs[s * 4 + i] for i in range(4)],
                      writes=[kbB], tag=f"{blk}_kld{h}")

            def knorm(h):
                kb, kbB = khw[h % 3]
                ks, ksB = ksqb[h % 2]
                mk, mkB = mkrow[h % 2]
                for j in range(T // 512):
                    P.op("scalar", lambda e, j=j, kb=kb, ks=ks: e.activation(
                            out=ks[:, :], in_=kb[0:64, j * 512:(j + 1) * 512],
                            func=AF.Square),
                         reads=[kbB], writes=[ksB], tag=f"{blk}_ksq{h}_{j}")
                    P.op("tensor", lambda e, ks=ks: e.matmul(
                            pb[1][0][0:1, 0:512], ones64[:, 0:1], ks[:, :],
                            start=True, stop=True),
                         reads=[ksB, ones64B], writes=[pb[1][1]], tag=f"{blk}_kmm{h}_{j}")
                    P.op("vector", lambda e, j=j, mk=mk: e.reduce_max(
                            out=mk[0:1, j:j + 1], in_=pb[1][0][0:1, 0:512], axis=AX.X),
                         reads=[pb[1][1]], writes=[mkB], tag=f"{blk}_kmx{h}_{j}")
                P.op("vector", lambda e, mk=mk, h=h: e.tensor_reduce(
                        out=mks[0:1, s * H + h: s * H + h + 1], in_=mk[0:1, 0:4],
                        axis=AX.X, op=ALU.max),
                     reads=[mkB], writes=[mksB], tag=f"{blk}_kmc{h}")

            def stHq(h4):
                qs, qsB = qsq4
                P.op("scalar", lambda e, h4=h4, qs=qs: e.activation(
                        out=qs[:, :], in_=qa[0:64, h4 * QB: (h4 + 4) * QB], func=AF.Square),
                     reads=[qaB[h4], qaB[h4 + 1], qaB[h4 + 2], qaB[h4 + 3]],
                     writes=[qsB], tag=f"{blk}_qsq{h4}")
                for h in range(h4, h4 + 4):
                    bank = h % 2
                    P.op("tensor", lambda e, qs=qs, h=h, h4=h4, bank=bank: e.matmul(
                            pb[bank][0][0:1, 0:QB], ones64[:, 0:1],
                            qs[:, (h - h4) * QB: (h - h4 + 1) * QB],
                            start=True, stop=True),
                         reads=[qsB, ones64B], writes=[pb[bank][1]], tag=f"{blk}_qmm{h}")
                    P.op("vector", lambda e, h=h, bank=bank: e.tensor_scalar_mul(
                            out=qa[64:65, h * QB: h * QB + QB],
                            in0=pb[bank][0][0:1, 0:QB],
                            scalar1=mks[0:1, s * H + h: s * H + h + 1]),
                         reads=[pb[bank][1], mksB], writes=[qaB[h]], tag=f"{blk}_qm2{h}")
                P.op("scalar", lambda e, h4=h4: e.activation(
                        out=qa[64:65, h4 * QB: (h4 + 4) * QB],
                        in_=qa[64:65, h4 * QB: (h4 + 4) * QB], func=AF.Sqrt),
                     reads=[qaB[h4], qaB[h4 + 1], qaB[h4 + 2], qaB[h4 + 3]],
                     writes=[qaB[h4], qaB[h4 + 1], qaB[h4 + 2], qaB[h4 + 3]],
                     tag=f"{blk}_qsr{h4}")
                P.op("vector", lambda e, h4=h4: e.tensor_scalar_mul(
                        out=qa[64:65, h4 * QB: (h4 + 4) * QB],
                        in0=qa[64:65, h4 * QB: (h4 + 4) * QB], scalar1=-1.0),
                     reads=[qaB[h4], qaB[h4 + 1], qaB[h4 + 2], qaB[h4 + 3]],
                     writes=[qaB[h4], qaB[h4 + 1], qaB[h4 + 2], qaB[h4 + 3]],
                     tag=f"{blk}_qng{h4}")

            def stB_st(h, k0, k1):
                kb, kbB = khw[h % 3]
                for km in range(k0, k1):
                    bank = 2 + km % 2
                    P.op("tensor", lambda e, h=h, km=km, kb=kb, bank=bank: e.matmul(
                            pb[bank][0][:, 0:QB],
                            kb[0:65, km * 128: km * 128 + 128],
                            qa[0:65, h * QB: h * QB + QB],
                            start=True, stop=True),
                         reads=[qaB[h], kbB], writes=[pb[bank][1]], tag=f"{blk}_st{h}_{km}")
                    pt, ptB = ptb[km % 8]
                    P.op("scalar", lambda e, pt=pt, bank=bank: e.activation(
                            out=pt[:, :], in_=pb[bank][0][:, 0:QB], func=AF.Exp),
                         reads=[pb[bank][1]], writes=[ptB], tag=f"{blk}_pe{h}_{km}")
                    if km >= 4 * qb:
                        P.op("gpsimd", lambda e, pt=pt, km=km, qb=qb: e.affine_select(
                                out=pt[:, :], in_=pt[:, :], pattern=[[1, QB]],
                                compare_op=ALU.is_ge, fill=0.0,
                                base=qb * 512 - km * 128, channel_multiplier=-1),
                             reads=[ptB], writes=[ptB], tag=f"{blk}_msk{h}_{km}")

            def stB_pv(h, k0, k1):
                ob = pb[4 + h % 2]
                for km in range(k0, k1):
                    pt, ptB = ptb[km % 8]
                    P.op("tensor", lambda e, h=h, km=km, pt=pt, nk=nk128, ob=ob: e.matmul(
                            ob[0][0:65, 0:QB],
                            vse[:, (km * H + h) * 65: (km * H + h) * 65 + 65],
                            pt[:, :],
                            start=(km == 0), stop=(km == nk - 1)),
                         reads=[vseB, ptB], writes=[ob[1]], tag=f"{blk}_pv{h}_{km}")

            def stB_fin(h):
                ob = pb[4 + h % 2]
                rc, rcB = rbc[h % 2]
                sr_, srB_ = srow[h % 2]
                def _cast(e, ob=ob, sr_=sr_):
                    with nc.allow_low_precision(reason="bf16 softmax normalizer"):
                        return e.copy(out=sr_[0:1, 0:QB], in_=ob[0][64:65, 0:QB])
                P.op("scalar", _cast, reads=[ob[1]], writes=[srB_], tag=f"{blk}_rs{h}")
                P.op("tensor", lambda e, sr_=sr_: e.matmul(
                        pb[7][0][0:64, 0:QB], onesrb[0:1, :], sr_[0:1, 0:QB],
                        start=True, stop=True),
                     reads=[srB_, onesrbB], writes=[pb[7][1]], tag=f"{blk}_rb{h}")
                P.op("scalar", lambda e, rc=rc: e.activation(out=rc[:, :],
                                                             in_=pb[7][0][0:64, 0:QB],
                                                             func=AF.Ln),
                     reads=[pb[7][1]], writes=[rcB], tag=f"{blk}_rbe{h}")
                P.op("vector", lambda e, rc=rc: e.tensor_scalar_mul(
                        out=rc[:, :], in0=rc[:, :], scalar1=-1.0),
                     reads=[rcB], writes=[rcB], tag=f"{blk}_rbn{h}")
                P.op("scalar", lambda e, rc=rc: e.activation(out=rc[:, :], in_=rc[:, :],
                                                             func=AF.Exp),
                     reads=[rcB], writes=[rcB], tag=f"{blk}_rbx{h}")
                P.op("vector", lambda e, h=h, ob=ob, rc=rc: e.tensor_tensor(
                        out=otb[64 * (h % 2):64 * (h % 2) + 64, (h // 2) * QB:(h // 2 + 1) * QB],
                        in0=ob[0][0:64, 0:QB], in1=rc[:, :], op=ALU.mult),
                     reads=[ob[1], rcB], writes=[otbB], tag=f"{blk}_ot{h}")

            kload(0)
            kload(1)
            if qb == 0:
                for h in range(H):
                    if h + 2 < H:
                        kload(h + 2)
                    knorm(h)
                kload(0)
                kload(1)
            for h4 in range(0, H, 4):
                stHq(h4)
            for h in range(H):
                if h + 2 < H:
                    kload(h + 2)
                nh = min(nk128, 8)
                stB_st(h, 0, nh)
                stB_pv(h, 0, nh)
                if nk128 > 8:
                    stB_st(h, 8, nk128)
                    stB_pv(h, 8, nk128)
                if h > 0:
                    stB_fin(h - 1)
            stB_fin(H - 1)
            for ecc in range(EC):
                bank = ecc % 2
                for hdc in range(EC):
                    P.op("tensor", lambda e, ecc=ecc, hdc=hdc, bank=bank: e.matmul(
                            pb[bank][0][:, 0:QB],
                            wob[:, hdc * D + ecc * 128: hdc * D + ecc * 128 + 128],
                            otb[:, hdc * QB:(hdc + 1) * QB],
                            start=(hdc == 0), stop=(hdc == EC - 1)),
                         reads=[wobB, otbB], writes=[pb[bank][1]], tag=f"{blk}_op{ecc}_{hdc}")
                P.op("scalar", lambda e, ecc=ecc, bank=bank: e.copy(
                        out=aout[:, ecc * QB:(ecc + 1) * QB], in_=pb[bank][0][:, 0:QB]),
                     reads=[pb[bank][1]], writes=[aoutB], tag=f"{blk}_oe{ecc}")
            P.op("scalar", lambda e: e.copy(out=xb[:, :], in_=aout[:, :]),
                 reads=[aoutB], writes=[xbB], tag=f"{blk}_ab")
            P.op("scalar", lambda e: e.activation(out=sqx[:, :], in_=xb[:, :], func=AF.Square),
                 reads=[xbB], writes=[sqxB], tag=f"{blk}_asq")
            emit_ln_stats(xb, xbB, sqx, sqxB, f"{blk}_l2")
            ln_apply(aout, aoutB, aout, aoutB, f"{blk}_l2a")
            P.op("vector", lambda e: e.tensor_add(out=x2t[:, :], in0=x2t[:, :], in1=aout[:, :]),
                 reads=[aoutB, x2tB], writes=[x2tB], tag=f"{blk}_rs1")
            P.op("scalar", lambda e: e.copy(out=xb[:, :], in_=x2t[:, :]),
                 reads=[x2tB], writes=[xbB], tag=f"{blk}_fb")
            P.op("scalar", lambda e: e.activation(out=sqx[:, :], in_=xb[:, :], func=AF.Square),
                 reads=[xbB], writes=[sqxB], tag=f"{blk}_fsq")
            emit_ln_stats(xb, xbB, sqx, sqxB, f"{blk}_l3")
            ln_apply(x2t, x2tB, x3ln, x3lnB, f"{blk}_l3a")
            for ec in range(EC):
                P.op("tensor", lambda e, ec=ec: e.matmul(
                        pb[5][0][0:8, 0:QB], rwt[:, ec * 8:(ec + 1) * 8],
                        x3ln[:, ec * QB:(ec + 1) * QB],
                        start=(ec == 0), stop=(ec == EC - 1)),
                     reads=[rwtB, x3lnB], writes=[pb[5][1]], tag=f"{blk}_ro{ec}")
            P.op("vector", lambda e: e.tensor_scalar_add(out=rlb[:, :], in0=pb[5][0][0:8, 0:QB],
                                                         scalar1=rbt[:, 0:1]),
                 reads=[pb[5][1], rbtB], writes=[rlbB], tag=f"{blk}_rle")
            for j in range(4):
                P.op("tensor", lambda e, j=j: e.matmul(
                        pb[6][0][:, j * 8:(j + 1) * 8], rlb[:, j * 128:(j + 1) * 128],
                        i128[0:8, 0:8], start=True, stop=True),
                     reads=[rlbB, i128B], writes=[pb[6][1]], tag=f"{blk}_rt{j}")
            for j in range(4):
                P.op("vector", lambda e, j=j: e.reduce_max(
                        out=gsum[:, 1:2], in_=pb[6][0][:, j * 8:(j + 1) * 8],
                        axis=AX.X, negate=True),
                     reads=[pb[6][1]], writes=[gsumB], tag=f"{blk}_gm{j}")
                P.op("scalar", lambda e, j=j: e.activation(
                        out=gex[:, :], in_=pb[6][0][:, j * 8:(j + 1) * 8],
                        func=AF.Exp, bias=gsum[:, 1:2], accum_out=gsum[:, 0:1]),
                     reads=[pb[6][1], gsumB], writes=[gexB, gsumB], tag=f"{blk}_ge{j}")
                P.op("vector", lambda e: e.reciprocal(out=gsum[:, 0:1], in_=gsum[:, 0:1]),
                     reads=[gsumB], writes=[gsumB], tag=f"{blk}_gr{j}")
                P.op("vector", lambda e, j=j: e.tensor_scalar_mul(
                        out=gb[:, j * 8:(j + 1) * 8], in0=gex[:, :], scalar1=gsum[:, 0:1]),
                     reads=[gexB, gsumB], writes=[gbB], tag=f"{blk}_gn{j}")
                P.op("tensor", lambda e, j=j: e.matmul(
                        pb[7][0][0:8, j * 128:(j + 1) * 128], gb[:, j * 8:(j + 1) * 8],
                        i128[:, :], start=True, stop=True),
                     reads=[gbB, i128B], writes=[pb[7][1]], tag=f"{blk}_gt{j}")
            P.op("vector", lambda e: e.tensor_copy(out=gTt[:, :], in_=pb[7][0][0:8, 0:QB]),
                 reads=[pb[7][1]], writes=[gTtB], tag=f"{blk}_gte")
            for ec in range(EC):
                P.op("tensor", lambda e, ec=ec: e.matmul(
                        pb[5][0][0:8, 0:QB], lab[:, ec * 8:(ec + 1) * 8],
                        x3ln[:, ec * QB:(ec + 1) * QB],
                        start=(ec == 0), stop=(ec == EC - 1)),
                     reads=[labB, x3lnB], writes=[pb[5][1]], tag=f"{blk}_mi{ec}")
            P.op("vector", lambda e: e.tensor_scalar_mul(out=mb[:, :],
                                                         in0=pb[5][0][0:8, 0:QB], scalar1=2.0),
                 reads=[pb[5][1]], writes=[mbB], tag=f"{blk}_me")
            P.op("tensor", lambda e: e.matmul(pb[5][0][0:64, 0:QB], selE[:, :], gTt[:, :],
                                              start=True, stop=True),
                 reads=[selEB, gTtB], writes=[pb[5][1]], tag=f"{blk}_sge")
            P.op("scalar", lambda e: e.copy(out=geb[:, :], in_=pb[5][0][0:64, 0:QB]),
                 reads=[pb[5][1]], writes=[gebB], tag=f"{blk}_sgee")
            P.op("tensor", lambda e: e.matmul(pb[5][0][0:64, 0:QB], selR[:, :], mb[:, :],
                                              start=True, stop=True),
                 reads=[selRB, mbB], writes=[pb[5][1]], tag=f"{blk}_smr")
            P.op("scalar", lambda e: e.copy(out=meb[:, :], in_=pb[5][0][0:64, 0:QB]),
                 reads=[pb[5][1]], writes=[mebB], tag=f"{blk}_smre")
            P.op("vector", lambda e: e.tensor_mul(out=p64[:, :], in0=geb[:, :], in1=meb[:, :]),
                 reads=[gebB, mebB], writes=[p64B], tag=f"{blk}_p64")
            for oc in range(EC):
                bank = 2 + oc % 2
                for ec in range(EC):
                    P.op("tensor", lambda e, oc=oc, ec=ec, bank=bank: e.matmul(
                            pb[bank][0][:, 0:QB],
                            bwb[:, ec * D + oc * 128: ec * D + oc * 128 + 128],
                            x3ln[:, ec * QB:(ec + 1) * QB],
                            start=(ec == 0), stop=(ec == EC - 1)),
                         reads=[bwbB, x3lnB], writes=[pb[bank][1]], tag=f"{blk}_bs{oc}_{ec}")
                lbank = 4 + oc % 2
                P.op("tensor", lambda e, oc=oc, lbank=lbank: e.matmul(
                        pb[lbank][0][:, 0:QB], lbb[:, oc * 128:(oc + 1) * 128], p64[:, :],
                        start=True, stop=True),
                     reads=[lbbB, p64B], writes=[pb[lbank][1]], tag=f"{blk}_lo{oc}")
                P.op("vector", lambda e, oc=oc, bank=bank: e.tensor_scalar_add(
                        out=fout[:, oc * QB:(oc + 1) * QB], in0=pb[bank][0][:, 0:QB],
                        scalar1=bbt[:, oc:oc + 1]),
                     reads=[pb[bank][1], bbtB], writes=[foutB], tag=f"{blk}_be{oc}")
                P.op("vector", lambda e, oc=oc, lbank=lbank: e.tensor_add(
                        out=fout[:, oc * QB:(oc + 1) * QB],
                        in0=fout[:, oc * QB:(oc + 1) * QB], in1=pb[lbank][0][:, 0:QB]),
                     reads=[pb[lbank][1], foutB], writes=[foutB], tag=f"{blk}_ba{oc}")
            P.op("scalar", lambda e: e.copy(out=xb[:, :], in_=fout[:, :]),
                 reads=[foutB], writes=[xbB], tag=f"{blk}_ob")
            P.op("scalar", lambda e: e.activation(out=sqx[:, :], in_=xb[:, :], func=AF.Square),
                 reads=[xbB], writes=[sqxB], tag=f"{blk}_osq")
            emit_ln_stats(xb, xbB, sqx, sqxB, f"{blk}_l4")
            ln_apply(fout, foutB, fout, foutB, f"{blk}_l4a")
            P.op("vector", lambda e: e.tensor_add(out=x2t[:, :], in0=x2t[:, :], in1=fout[:, :]),
                 reads=[foutB, x2tB], writes=[x2tB], tag=f"{blk}_rs2")
            dst = out_h.ap()[s].rearrange("(c p) t -> p c t", p=128)[:, :, qb * QB:(qb + 1) * QB]
            P.dma("gpsimd", lambda e, d=dst: e.dma_start(
                    out=d, in_=ap3(x2t, 0, [[EC * QB, 128], [QB, EC], [1, QB]])),
                  reads=[x2tB], writes=[out_b], tag=f"{blk}_st")
    A.reset(base)


_STREAMS = [
    ("x0", (BL, D, T), F32, None, (1,)),
    ("x1", (BL, D, T), F32, 1, (2,)),
    ("x2", (BL, D, T), F32, 2, (3, 4)),
    ("qh", (BL, H, DH, T), BF16, 3, (4,)),
    ("kh", (BL, H, DH, T), BF16, 3, (4,)),
    ("vh", (BL, T // 128, 128, H * 65), BF16, 3, (4,)),
    ("out", (BL, D, T), F32, 4, ()),
]

_PHASE_WEIGHTS = {
    1: [("WkT", (D, D), F32), ("WvT", (D, D), F32), ("WrT", (D, D), F32),
        ("WoT", (D, D), F32), ("tmk", (128, EC), F32), ("tmv", (128, EC), F32),
        ("tmr", (128, EC), F32), ("tdec", (128, EC), F32), ("tfir", (128, EC), F32)],
    2: [("WfcT", (D, HFF), F32), ("WprT", (HFF, D), F32),
        ("lnw", (128, HFF // 128), F32), ("lnb", (128, HFF // 128), F32)],
    3: [("wqr", (D, D), F32), ("wkr", (D, D), F32), ("wvr", (D, D), F32)],
    4: [("woT", (D, D), F32), ("bwT", (D, D), F32), ("bb", (128, EC), F32),
        ("loraA", (D, 8), F32), ("loraB", (64, D), F32),
        ("rwT", (D, 8), F32), ("rb", (8, 1), F32)],
}

_EMITTERS = {1: emit_phase1, 2: emit_phase2, 3: emit_phase3, 4: emit_phase4}


def build_nc(phases):
    nc = bass.Bass("TRN2", target_bir_lowering=False, debug=False)
    P = Prog()
    A = Arena(nc, P, SBUF_LIMIT)
    dram = {}
    pset = set(phases)
    nbuf = [0]
    def _dbuf(name):
        nbuf[0] += 1
        return P.buf("dram", nbuf[0] - 1, nbuf[0], name)
    for name, shape, dt, prod, cons in _STREAMS:
        needed = (prod in pset) or (pset & set(cons))
        if not needed:
            continue
        if prod in pset and (set(cons) - pset or not cons):
            kind = "ExternalOutput"
        elif prod in pset:
            kind = "Internal"
        else:
            kind = "ExternalInput"
        h = nc.dram_tensor(name, list(shape), dt, kind=kind)
        if name in ("x1", "x2", "qh", "kh", "vh"):
            n = 16 if name in ("x1", "x2") else 8
            b0 = nbuf[0]
            nbuf[0] += n
            parent = P.buf("dram", b0, b0 + n, name)
            dram[name] = (h, parent)
            dram[name + "_subs"] = [
                P.buf("dram", b0 + i, b0 + i + 1, f"{name}_sub{i}") for i in range(n)]
        else:
            dram[name] = (h, _dbuf(name))
    for ph in phases:
        for name, shape, dt in _PHASE_WEIGHTS[ph]:
            h = nc.dram_tensor(name, list(shape), dt, kind="ExternalInput")
            dram[name] = (h, _dbuf(name))
    if 2 in pset:
        for nm, srcn, shape in (("WfcB16", "WfcT", (D, HFF)),
                                ("WprB16", "WprT", (HFF, D))):
            h16 = nc.dram_tensor(nm, list(shape), BF16, kind="Internal")
            b16 = _dbuf(nm)
            dram[nm] = (h16, b16)
            sh, sb = dram[srcn]
            P.dma("gpsimd", lambda e, h16=h16, sh=sh: e.dma_start(
                    out=h16.ap(), in_=sh.ap()),
                  reads=[sb], writes=[b16], tag=f"cast_{nm}")
    pb = []
    for i in range(8):
        t = nc.alloc_psum_tensor(f"pb{i}", [128, 512], F32)
        pb.append((t, P.buf("psum", i * 2048, (i + 1) * 2048, f"pb{i}")))
    for ph in phases:
        _EMITTERS[ph](nc, P, A, dram, pb)
    P.build(nc, None)
    outs = [n for n, _, _, prod, cons in _STREAMS
            if prod in pset and (set(cons) - pset or not cons)]
    ins = [n for n, _, _, prod, cons in _STREAMS
           if prod not in pset and (pset & set(cons))]
    wts = [name for ph in phases for name, _, _ in _PHASE_WEIGHTS[ph]]
    return nc, ins, wts, outs


def _prep_all(inputs):
    f = np.ascontiguousarray
    w = _prep_shared(inputs)
    w["WfcT"] = f(inputs["L0_ffn_W_fc"].T)
    w["WprT"] = f(inputs["L0_ffn_W_proj"].T)
    w["lnw"] = f(inputs["L0_ffn_ln_w"].reshape(HFF // 128, 128).T)
    w["lnb"] = f(inputs["L0_ffn_ln_b"].reshape(HFF // 128, 128).T)
    w["wqr"] = f(inputs["L1_attn_w_q"].reshape(D, D))
    w["wkr"] = f(inputs["L1_attn_w_k"].reshape(D, D))
    w["wvr"] = f(inputs["L1_attn_w_v"].reshape(D, D))
    w["woT"] = f(inputs["L1_attn_w_o"].reshape(D, D).T)
    w["bwT"] = f(inputs["L1_ffn_base_w"].T)
    w["bb"] = f(inputs["L1_ffn_base_b"].reshape(EC, 128).T)
    w["loraA"] = f(inputs["L1_ffn_lora_A"])
    w["loraB"] = f(inputs["L1_ffn_lora_B"].reshape(64, D))
    w["rwT"] = f(inputs["L1_ffn_router_w"].T)
    w["rb"] = f(inputs["L1_ffn_router_b"].reshape(8, 1))
    return w


PHASE_GROUPS = [[1, 2, 3, 4]]


def kernel(**inputs):
    w = _prep_all(inputs)
    x = np.ascontiguousarray(inputs["L0_attn_x"])

    stream_vals = [dict() for _ in range(NCORES)]
    for c in range(NCORES):
        stream_vals[c]["x0"] = np.ascontiguousarray(
            np.transpose(x[2 * c:2 * c + 2], (0, 2, 1)))

    final = None
    for grp in PHASE_GROUPS:
        nc, ins, wts, outs = build_nc(grp)
        in_maps = []
        for c in range(NCORES):
            m = {k: stream_vals[c][k] for k in ins}
            m.update({k: w[k] for k in wts})
            in_maps.append(m)
        res = run_bass_kernel_spmd(nc, in_maps, core_ids=list(range(NCORES)))
        for c in range(NCORES):
            for o in outs:
                stream_vals[c][o] = res.results[c][o]
        final = outs[-1] if outs else final

    key = "out" if any("out" in sv for sv in stream_vals) else final
    out = []
    for c in range(NCORES):
        out.append(np.transpose(stream_vals[c][key], (0, 2, 1)))
    return np.concatenate(out, axis=0).astype(np.float32)


# version 39 (final)
# speedup vs baseline: 1.0255x; 1.0222x over previous
import math
import numpy as np
import concourse.bass as bass
import concourse.mybir as mybir
from concourse.bass_utils import run_bass_kernel_spmd

F32 = mybir.dt.float32
BF16 = mybir.dt.bfloat16
AF = mybir.ActivationFunctionType
ALU = mybir.AluOpType
AX = mybir.AxisListType

NCORES = 8
BL = 2
T = 2048
D = 1024
EC = 8
H = 16
DH = 64
HFF = 4096
LN_EPS = 1e-5


class Buf:
    __slots__ = ("space", "start", "end", "name", "overlaps")

    def __init__(self, space, start, end, name):
        self.space, self.start, self.end, self.name = space, start, end, name
        self.overlaps = []


class _Op:
    __slots__ = ("eng", "emit", "reads", "writes", "seq", "kind", "queue",
                 "inc_needed", "inc_val", "dma_val", "waits", "tag", "skey")

    def __init__(self, eng, emit, reads, writes, seq, kind, queue, tag):
        self.eng, self.emit = eng, emit
        self.reads, self.writes = reads, writes
        self.seq, self.kind, self.queue = seq, kind, queue
        self.inc_needed = False
        self.inc_val = None
        self.dma_val = None
        self.waits = []
        self.tag = tag


ENGINES = ("sync", "vector", "scalar", "tensor", "gpsimd")
DRAIN_ENGINES = ("vector", "scalar", "gpsimd")


class Prog:
    def __init__(self):
        self.ops = []
        self.state = {}
        self._bufs = []

    def buf(self, space, start, end, name):
        b = Buf(space, start, end, name)
        for o in self._bufs:
            if o.space == space and not (end <= o.start or start >= o.end):
                o.overlaps.append(b)
                b.overlaps.append(o)
        self._bufs.append(b)
        return b

    def _touch(self, op):
        def dep(producer):
            if producer is None or producer is op:
                return
            op.waits.append(producer)
        for r in op.reads:
            for b in (r, *r.overlaps):
                st = self.state.get(b)
                if st:
                    dep(st[0])
        for w in op.writes:
            for b in (w, *w.overlaps):
                st = self.state.get(b)
                if st:
                    dep(st[0])
                    for rd in st[1]:
                        dep(rd)
        for r in op.reads:
            st = self.state.get(r)
            if st is None:
                self.state[r] = (None, [op])
            else:
                st[1].append(op)
        for w in op.writes:
            self.state[w] = (op, [])

    def op(self, eng, emit, reads=(), writes=(), tag=""):
        o = _Op(eng, emit, tuple(reads), tuple(writes), len(self.ops), "op", None, tag)
        self._touch(o)
        self.ops.append(o)
        return o

    def dma(self, queue, emit, reads=(), writes=(), tag=""):
        o = _Op(queue, emit, tuple(reads), tuple(writes), len(self.ops), "dma", queue, tag)
        skey = None
        for b in (*o.writes, *o.reads):
            if b.space == "sbuf":
                skey = b
                break
        if skey is None:
            skey = (o.writes + o.reads)[0]
        o.skey = (skey, "sw" if queue == "gpsimd" else "hw")
        self._touch(o)
        self.ops.append(o)
        return o

    def build(self, nc, ctx):
        for o in self.ops:
            for p in o.waits:
                if p.kind == "dma":
                    continue
                if p.eng == o.eng:
                    if p.eng == "tensor" and o.kind == "op":
                        continue
                    continue
                p.inc_needed = True
        eng_cnt = {e: 0 for e in ENGINES}
        skey_cnt = {}
        dma_by_queue = {e: [] for e in ENGINES}
        for o in self.ops:
            if o.kind == "dma":
                skey_cnt[o.skey] = skey_cnt.get(o.skey, 0) + 16
                o.dma_val = skey_cnt[o.skey]
                dma_by_queue[o.queue].append(o)
            elif o.inc_needed:
                eng_cnt[o.eng] += 1
                o.inc_val = eng_cnt[o.eng]
        self.final_dma = dict(skey_cnt)

        sems = {}
        for e in ENGINES:
            if eng_cnt[e]:
                sems[("eng", e)] = nc.alloc_semaphore(f"s_{e}")
        for sk in skey_cnt:
            sems[("dma", sk)] = nc.alloc_semaphore(f"d_{sk[0].name}_{sk[1]}")
        self.sems = sems

        streams = {e: [] for e in ENGINES}
        hw = {e: {} for e in ENGINES}
        drained = {e: -1 for e in ENGINES}
        lastseq = {e: -1 for e in ENGINES}

        for o in self.ops:
            needs_drain = False
            waitlist = {}
            for p in o.waits:
                if p.kind == "dma":
                    key = ("dma", p.skey)
                    waitlist[key] = max(waitlist.get(key, 0), p.dma_val)
                elif p.eng == o.eng:
                    if p.eng == "tensor" and o.kind == "op":
                        continue
                    if p.seq > drained[o.eng]:
                        needs_drain = True
                else:
                    key = ("eng", p.eng)
                    if p.inc_val is None:
                        raise RuntimeError(f"producer without inc: {p.tag} -> {o.tag}")
                    waitlist[key] = max(waitlist.get(key, 0), p.inc_val)
            if needs_drain:
                drained[o.eng] = lastseq[o.eng]
            ww = []
            for key, val in waitlist.items():
                if hw[o.eng].get(key, 0) < val:
                    hw[o.eng][key] = val
                    ww.append((key, val))
            streams[o.eng].append((o, needs_drain, ww))
            lastseq[o.eng] = o.seq

        def replay(ename, eng):
            for (o, needs_drain, ww) in streams[ename]:
                if needs_drain:
                    eng.drain()
                for key, val in ww:
                    eng.wait_ge(sems[key], val)
                ins = o.emit(eng)
                if o.kind == "dma":
                    ins.then_inc(sems[("dma", o.skey)], 16)
                elif o.inc_needed:
                    ins.then_inc(sems[("eng", o.eng)], 1)
            finals = {}
            for o in dma_by_queue[ename]:
                finals[o.skey] = skey_cnt[o.skey]
            for sk, v in finals.items():
                eng.wait_ge(sems[("dma", sk)], v)

        with nc.Block() as block:
            if streams["sync"]:
                block.sync(lambda eng: replay("sync", eng))
            if streams["vector"]:
                block.vector(lambda eng: replay("vector", eng))
            if streams["scalar"]:
                block.scalar(lambda eng: replay("scalar", eng))
            if streams["tensor"]:
                block.tensor(lambda eng: replay("tensor", eng))
            if streams["gpsimd"]:
                block.gpsimd(lambda eng: replay("gpsimd", eng))



SBUF_BASE = 18432


class Arena:
    def __init__(self, nc, prog, limit_bytes):
        self.nc = nc
        self.prog = prog
        self.limit = limit_bytes
        self.off = SBUF_BASE
        self.n = 0

    def alloc(self, name, parts, cols, dt):
        esz = 4 if dt == F32 else 2
        nbytes = cols * esz
        nbytes_al = (nbytes + 31) & ~31
        t = self.nc.alloc_sbuf_tensor_at(name, [parts, cols], dt, offset=self.off)
        b = self.prog.buf("sbuf", self.off, self.off + nbytes_al, name)
        self.off += nbytes_al
        if self.off > self.limit:
            raise RuntimeError(f"SBUF arena overflow at {name}: {self.off} > {self.limit}")
        return t, b

    def mark(self):
        return self.off

    def reset(self, off):
        self.off = off


def ap3(t, offset, pattern):
    return bass.AP(t, offset, pattern)



def emit_phase1(nc, P, A, dram, pb):
    TT = 256
    NT = T // TT
    NTT = NT * BL

    x_h, x_b = dram["x0"]
    x1_h, x1_b = dram["x1"]
    x1_subs = dram["x1_subs"]
    wk_h, wk_b = dram["WkT"]
    wv_h, wv_b = dram["WvT"]
    wr_h, wr_b = dram["WrT"]
    wo_h, wo_b = dram["WoT"]
    tmk_h, tmk_b = dram["tmk"]
    tmv_h, tmv_b = dram["tmv"]
    tmr_h, tmr_b = dram["tmr"]
    tdec_h, tdec_b = dram["tdec"]
    tfir_h, tfir_b = dram["tfir"]

    base = A.mark()
    wkb, wkbB = A.alloc("p1_wk", 128, EC * 1024, BF16)
    wvb, wvbB = A.alloc("p1_wv", 128, EC * 1024, BF16)
    wrb, wrbB = A.alloc("p1_wr", 128, EC * 1024, BF16)
    wob, wobB = A.alloc("p1_wo", 128, EC * 1024, BF16)
    tmk, tmkB = A.alloc("p1_tmk", 128, EC, F32)
    tmv, tmvB = A.alloc("p1_tmv", 128, EC, F32)
    tmr, tmrB = A.alloc("p1_tmr", 128, EC, F32)
    wdec, wdecB = A.alloc("p1_wdec", 128, EC, F32)
    ufir, ufirB = A.alloc("p1_u", 128, EC, F32)
    xin = [A.alloc(f"p1_xin{i}", 128, EC * TT, F32) for i in range(2)]
    xb, xbB = A.alloc("p1_xb", 128, EC * TT, BF16)
    sqx, sqxB = A.alloc("p1_sqx", 128, EC * TT, BF16)
    xln, xlnB = A.alloc("p1_xln", 128, EC * TT, BF16)
    xkb, xkbB = A.alloc("p1_xk", 128, EC * TT, BF16)
    xvb, xvbB = A.alloc("p1_xv", 128, EC * TT, BF16)
    xrb, xrbB = A.alloc("p1_xr", 128, EC * TT, BF16)
    kbuf = [A.alloc(f"p1_k{i}", 128, EC * TT, F32) for i in range(2)]
    vbuf = [A.alloc(f"p1_v{i}", 128, EC * TT, BF16) for i in range(2)]
    srb = [A.alloc(f"p1_sr{i}", 128, EC * TT, BF16) for i in range(2)]
    psub = {}
    for lst, esz in ((kbuf, 4), (vbuf, 2), (srb, 2)):
        for t_, b_ in lst:
            psub[b_] = [P.buf("sbuf", b_.start + g * 2 * TT * esz,
                              b_.start + (g + 1) * 2 * TT * esz,
                              f"{b_.name}_g{g}") for g in range(4)]
    ppx, ppxB = A.alloc("p1_ppx", 128, EC * (TT + 1), F32)
    aax, aaxB = A.alloc("p1_aax", 128, EC * (TT + 1), F32)
    bbx, bbxB = A.alloc("p1_bbx", 128, EC * (TT + 1), F32)
    s1, s1B = A.alloc("p1_s1", 128, EC * TT, F32)
    s2, s2B = A.alloc("p1_s2", 128, EC * TT, F32)
    s3, s3B = A.alloc("p1_s3", 128, EC * TT, F32)
    rwb = [A.alloc(f"p1_rw{i}", 128, EC * TT, BF16) for i in range(2)]
    xb2, xb2B = A.alloc("p1_xb2", 128, EC * TT, BF16)
    xb2C = [P.buf("sbuf", xb2B.start + g * 2 * TT * 2,
                  xb2B.start + (g + 1) * 2 * TT * 2, f"p1_xb2_g{g}") for g in range(4)]
    sqx2, sqx2B = sqx, sqxB
    muB, muBB = A.alloc("p1_muB", 128, TT, F32)
    rstdB, rstdBB = A.alloc("p1_rstdB", 128, TT, F32)
    muB2, muB2B = A.alloc("p1_muB2", 128, TT, F32)
    rstdB2, rstdB2B = A.alloc("p1_rstdB2", 128, TT, F32)
    stat, statB = A.alloc("p1_stat", 1, 4 * TT, F32)
    stat2, stat2B = stat, statB
    onesr, onesrB = A.alloc("p1_onesr", 1, 128, F32)
    onesc, onescB = A.alloc("p1_onesc", 128, 1, BF16)
    lnc, lncB = A.alloc("p1_lnc", 1, 2, F32)
    carry = {}
    carryB = {}
    for s in range(BL):
        for nm in ("pp", "aa", "bb"):
            carry[(nm, s)], carryB[(nm, s)] = A.alloc(f"p1_c{nm}{s}", 128, EC, F32)
        carry[("xc", s)], carryB[("xc", s)] = A.alloc(f"p1_cx{s}", 128, EC, BF16)

    for (hb, hbuf), (sb, sbuf) in (((wk_h, wk_b), (wkb, wkbB)),
                                   ((wv_h, wv_b), (wvb, wvbB)),
                                   ((wr_h, wr_b), (wrb, wrbB)),
                                   ((wo_h, wo_b), (wob, wobB))):
        src = hb.ap().rearrange("(c p) d -> p c d", p=128)
        P.dma("gpsimd", lambda e, d=sb, s=src: e.dma_start(
                out=ap3(d, 0, [[EC * 1024, 128], [1024, EC], [1, 1024]]), in_=s),
              reads=[hbuf], writes=[sbuf], tag="w_ld")
    for (hb, hbuf), (sb, sbuf) in (((tmk_h, tmk_b), (tmk, tmkB)),
                                   ((tmv_h, tmv_b), (tmv, tmvB)),
                                   ((tmr_h, tmr_b), (tmr, tmrB)),
                                   ((tfir_h, tfir_b), (ufir, ufirB))):
        P.dma("sync", lambda e, d=sb, s=hb: e.dma_start(out=d[:, :], in_=s.ap()),
              reads=[hbuf], writes=[sbuf], tag="v_ld")
    P.dma("sync", lambda e: e.dma_start(out=wdec[:, :], in_=tdec_h.ap()),
          reads=[tdec_b], writes=[wdecB], tag="tdec_ld")
    P.op("scalar", lambda e: e.activation(out=wdec[:, :], in_=wdec[:, :], func=AF.Exp),
         reads=[wdecB], writes=[wdecB], tag="wdec_exp")
    P.op("vector", lambda e: e.tensor_scalar_mul(out=wdec[:, :], in0=wdec[:, :], scalar1=-1.0),
         reads=[wdecB], writes=[wdecB], tag="wdec_neg")
    P.op("vector", lambda e: e.memset(onesr[:, :], 1.0), writes=[onesrB], tag="ones_r")
    P.op("vector", lambda e: e.memset(onesc[:, :], 1.0), writes=[onescB], tag="ones_c")
    P.op("vector", lambda e: e.memset(lnc[0:1, 0:1], LN_EPS), writes=[lncB], tag="p1lne")
    P.op("vector", lambda e: e.memset(lnc[0:1, 1:2], -0.5), writes=[lncB], tag="p1lnh")
    for s in range(BL):
        P.op("vector", lambda e, s=s: e.memset(carry[("pp", s)][:, :], -1e38),
             writes=[carryB[("pp", s)]], tag="c_pp")
        P.op("vector", lambda e, s=s: e.memset(carry[("aa", s)][:, :], 0.0),
             writes=[carryB[("aa", s)]], tag="c_aa")
        P.op("vector", lambda e, s=s: e.memset(carry[("bb", s)][:, :], 0.0),
             writes=[carryB[("bb", s)]], tag="c_bb")
        P.op("vector", lambda e, s=s: e.memset(carry[("xc", s)][:, :], 0.0),
             writes=[carryB[("xc", s)]], tag="c_x")

    def v3(t, off=0, n=EC):
        return ap3(t, off, [[EC * TT, 128], [TT, n], [1, TT]])

    def b3(t, n=EC):
        return ap3(t, 0, [[TT, 128], [0, n], [1, TT]])

    def pv3(t, a=0, n=EC):
        return ap3(t, a, [[EC * (TT + 1), 128], [TT + 1, n], [1, TT]])

    def cb3(t, c0=0, n=EC):
        return ap3(t, c0, [[EC, 128], [1, n], [0, TT]])

    def emit_ln_stats(src_t, srcB, sq_t, sqB_, st_, stB_, mu_t, muB_, rs_t, rsB_,
                      bsum, bsq, tag, delta=None):
        if delta is not None:
            for c in range(EC):
                P.op("tensor", lambda e, c=c: e.matmul(
                        pb[bsum][0][0:1, 0:2 * TT], onesc[:, 0:1],
                        ap3(src_t, c * TT, [[EC * TT, 128], [delta, 2], [1, TT]]),
                        start=(c == 0), stop=(c == EC - 1)),
                     reads=[srcB, sqB_, onescB], writes=[pb[bsum][1]], tag=f"{tag}_sum")
        else:
            for c in range(EC):
                P.op("tensor", lambda e, c=c: e.matmul(pb[bsum][0][0:1, 0:TT], onesc[:, 0:1],
                                                       ap3(src_t, c * TT, [[EC * TT, 128], [1, TT]]),
                                                       start=(c == 0), stop=(c == EC - 1)),
                     reads=[srcB, onescB], writes=[pb[bsum][1]], tag=f"{tag}_sum")
            for c in range(EC):
                P.op("tensor", lambda e, c=c: e.matmul(pb[bsum][0][0:1, TT:2 * TT], onesc[:, 0:1],
                                                       ap3(sq_t, c * TT, [[EC * TT, 128], [1, TT]]),
                                                       start=(c == 0), stop=(c == EC - 1)),
                     reads=[sqB_, onescB], writes=[pb[bsum][1]], tag=f"{tag}_ssq")
        n = EC * 128
        P.op("vector", lambda e: e.tensor_scalar_mul(out=st_[0:1, 0:2 * TT],
                                                     in0=pb[bsum][0][0:1, 0:2 * TT], scalar1=1.0 / n),
             reads=[pb[bsum][1]], writes=[stB_], tag=f"{tag}_mu")
        P.op("vector", lambda e: e.tensor_mul(out=st_[0:1, 2 * TT:3 * TT],
                                              in0=st_[0:1, 0:TT], in1=st_[0:1, 0:TT]),
             reads=[stB_], writes=[stB_], tag=f"{tag}_musq")
        P.op("vector", lambda e: e.tensor_sub(out=st_[0:1, 2 * TT:3 * TT],
                                              in0=st_[0:1, TT:2 * TT], in1=st_[0:1, 2 * TT:3 * TT]),
             reads=[stB_], writes=[stB_], tag=f"{tag}_var")
        P.op("scalar", lambda e: e.activation(out=st_[0:1, 2 * TT:3 * TT],
                                              in_=st_[0:1, 2 * TT:3 * TT], func=AF.Ln,
                                              bias=lnc[0:1, 0:1]),
             reads=[stB_, lncB], writes=[stB_], tag=f"{tag}_lnv")
        P.op("scalar", lambda e: e.activation(out=st_[0:1, TT:2 * TT],
                                              in_=st_[0:1, 2 * TT:3 * TT], func=AF.Exp,
                                              scale=lnc[0:1, 1:2]),
             reads=[stB_, lncB], writes=[stB_], tag=f"{tag}_rstd")
        P.op("tensor", lambda e: e.matmul(pb[bsum][0][:, 0:2 * TT], onesr[0:1, :],
                                          st_[0:1, 0:2 * TT], start=True, stop=True),
             reads=[stB_, onesrB], writes=[pb[bsum][1]], tag=f"{tag}_bmu")
        P.op("scalar", lambda e: e.copy(out=mu_t[:, :], in_=pb[bsum][0][:, 0:TT]),
             reads=[pb[bsum][1]], writes=[muB_], tag=f"{tag}_emu")
        P.op("scalar", lambda e: e.copy(out=rs_t[:, :], in_=pb[bsum][0][:, TT:2 * TT]),
             reads=[pb[bsum][1]], writes=[rsB_], tag=f"{tag}_ers")

    def stA(ti):
        tt, s = divmod(ti, BL)
        t0 = tt * TT
        xt, xtB = xin[ti % 2]
        src = x_h.ap()[s].rearrange("(c p) t -> p c t", p=128)[:, :, t0:t0 + TT]
        P.dma("sync", lambda e, d=xt, sr=src: e.dma_start(
                out=ap3(d, 0, [[EC * TT, 128], [TT, EC], [1, TT]]), in_=sr),
              reads=[x_b], writes=[xtB], tag=f"x_ld{ti}")
        xdst = x1_h.ap()[s].rearrange("(c p) t -> p c t", p=128)[:, :, t0:t0 + TT]
        P.dma("gpsimd", lambda e, xt=xt, d=xdst: e.dma_start(
                out=d, in_=ap3(xt, 0, [[EC * TT, 128], [TT, EC], [1, TT]])),
              reads=[xtB], writes=[x1_subs[s * 8 + t0 // 256]], tag=f"x_cp{ti}")
        P.op("scalar", lambda e, xt=xt: e.copy(out=xb[:, :], in_=xt[:, :]),
             reads=[xtB], writes=[xbB], tag=f"xb{ti}")
        P.op("scalar", lambda e: e.activation(out=sqx[:, :], in_=xb[:, :], func=AF.Square),
             reads=[xbB], writes=[sqxB], tag=f"sq{ti}")
        emit_ln_stats(xb, xbB, sqx, sqxB, stat, statB, muB, muBB, rstdB, rstdBB,
                      4, 5, f"ln1_{ti}")
        P.op("vector", lambda e, xt=xt: e.tensor_tensor(
                out=v3(s1), in0=v3(xt), in1=b3(muB), op=ALU.subtract),
             reads=[xtB, muBB], writes=[s1B], tag=f"lnsub{ti}")
        P.op("vector", lambda e: e.tensor_tensor(
                out=v3(xln), in0=v3(s1), in1=b3(rstdB), op=ALU.mult),
             reads=[s1B, rstdBB], writes=[xlnB], tag=f"lnmul{ti}")
        P.op("scalar", lambda e, s=s: e.copy(
                out=ap3(sqx, 0, [[EC * TT, 128], [TT, EC], [1, 1]]),
                in_=ap3(carry[("xc", s)], 0, [[EC, 128], [1, EC], [0, 1]])),
             reads=[carryB[("xc", s)]], writes=[sqxB], tag=f"xx0_{ti}")
        P.op("scalar", lambda e: e.copy(
                out=ap3(sqx, 1, [[EC * TT, 128], [TT, EC], [1, TT - 1]]),
                in_=ap3(xln, 0, [[EC * TT, 128], [TT, EC], [1, TT - 1]])),
             reads=[xlnB], writes=[sqxB], tag=f"xxs_{ti}")
        P.op("vector", lambda e, s=s: e.tensor_copy(
                out=carry[("xc", s)][:, :],
                in_=ap3(xln, TT - 1, [[EC * TT, 128], [TT, EC]])),
             reads=[xlnB], writes=[carryB[("xc", s)]], tag=f"xcu_{ti}")
        P.op("vector", lambda e: e.tensor_sub(out=s1[:, :], in0=xln[:, :], in1=sqx[:, :]),
             reads=[xlnB, sqxB], writes=[s1B], tag=f"diff{ti}")
        for (tm, tmB, dst, dstB, nm, eng, sc) in (
                (tmk, tmkB, xkb, xkbB, "k", "vector", s2),
                (tmv, tmvB, xvb, xvbB, "v", "gpsimd", s3),
                (tmr, tmrB, xrb, xrbB, "r", "vector", s2)):
            scB = s2B if sc is s2 else s3B
            P.op(eng, lambda e, tm=tm, sc=sc: e.tensor_tensor(
                    out=v3(sc), in0=v3(s1), in1=cb3(tm), op=ALU.mult),
                 reads=[s1B, tmB], writes=[scB], tag=f"xm{nm}{ti}")
            P.op(eng, lambda e, dst=dst, sc=sc: e.tensor_add(
                    out=dst[:, :], in0=sc[:, :], in1=sqx[:, :]),
                 reads=[scB, sqxB], writes=[dstB], tag=f"xa{nm}{ti}")

    def stB(ti):
        kb_, kbB_ = kbuf[ti % 2]
        vb_, vbB_ = vbuf[ti % 2]
        sb_, sbB_ = srb[ti % 2]
        for (wt, wtB, src_t, srcB, banks, dst, dstB, nm, sig) in (
                (wkb, wkbB, xkb, xkbB, (0, 5), kb_, kbB_, "k", False),
                (wvb, wvbB, xvb, xvbB, (1, 7), vb_, vbB_, "v", False),
                (wrb, wrbB, xrb, xrbB, (2, 5), sb_, sbB_, "r", True)):
            for dc in range(EC):
                bank = banks[(dc // 2) % 2]
                col = (dc % 2) * TT
                for ec in range(EC):
                    P.op("tensor", lambda e, wt=wt, src_t=src_t, dc=dc, ec=ec, bank=bank, col=col:
                         e.matmul(pb[bank][0][:, col:col + TT],
                                  wt[:, ec * 1024 + dc * 128: ec * 1024 + dc * 128 + 128],
                                  src_t[:, ec * TT:(ec + 1) * TT],
                                  start=(ec == 0), stop=(ec == EC - 1)),
                         reads=[wtB, srcB], writes=[pb[bank][1]], tag=f"mm{nm}{ti}_{dc}_{ec}")
                if dc % 2 == 0:
                    continue
                if sig:
                    P.op("scalar", lambda e, dc=dc, dst=dst, bank=bank: e.activation(
                            out=dst[:, (dc - 1) * TT:(dc + 1) * TT], in_=pb[bank][0][:, 0:2 * TT],
                            func=AF.Sigmoid),
                         reads=[pb[bank][1]], writes=[psub[dstB][dc // 2]], tag=f"ev{nm}{ti}_{dc}")
                else:
                    P.op("scalar", lambda e, dc=dc, dst=dst, bank=bank: e.copy(
                            out=dst[:, (dc - 1) * TT:(dc + 1) * TT], in_=pb[bank][0][:, 0:2 * TT]),
                         reads=[pb[bank][1]], writes=[psub[dstB][dc // 2]], tag=f"ev{nm}{ti}_{dc}")

    def _cenv(ti):
        tt, s = divmod(ti, BL)
        kb_, kbB_ = kbuf[ti % 2]
        vb_, vbB_ = vbuf[ti % 2]
        sb_, sbB_ = srb[ti % 2]
        rw_, rwB_ = rwb[ti % 2]
        cpp, cppB = carry[("pp", s)], carryB[("pp", s)]
        caa, caaB = carry[("aa", s)], carryB[("aa", s)]
        cbb, cbbB = carry[("bb", s)], carryB[("bb", s)]
        return (kb_, kbB_, vb_, vbB_, sb_, sbB_, rw_, rwB_,
                cpp, cppB, caa, caaB, cbb, cbbB)

    def stCa(ti):
        (kb_, kbB_, vb_, vbB_, sb_, sbB_, rw_, rwB_,
         cpp, cppB, caa, caaB, cbb, cbbB) = _cenv(ti)
        P.op("vector", lambda e: e.tensor_copy(
                out=ap3(ppx, 0, [[EC * (TT + 1), 128], [TT + 1, EC], [1, 1]]),
                in_=ap3(cpp, 0, [[EC, 128], [1, EC], [0, 1]])),
             reads=[cppB], writes=[ppxB], tag=f"pp0_{ti}")
        for c in range(EC):
            P.op("vector", lambda e, c=c: e.tensor_tensor_scan(
                    out=ppx[:, c * (TT + 1) + 1: c * (TT + 1) + TT + 1],
                    data0=ap3(wdec, c, [[EC, 128], [0, TT]]),
                    data1=kb_[:, c * TT:(c + 1) * TT],
                    initial=cpp[:, c:c + 1],
                    op0=ALU.add, op1=ALU.max),
                 reads=[wdecB, kbB_, cppB], writes=[ppxB], tag=f"ppscan{ti}_{c}")
        P.op("vector", lambda e: e.tensor_tensor(
                out=v3(s1), in0=pv3(ppx, 0), in1=cb3(wdec), op=ALU.add),
             reads=[ppxB, wdecB], writes=[s1B], tag=f"ga{ti}")
        P.op("vector", lambda e: e.tensor_tensor(
                out=v3(s1), in0=v3(s1), in1=pv3(ppx, 1), op=ALU.subtract),
             reads=[s1B, ppxB], writes=[s1B], tag=f"gs{ti}")
        P.op("vector", lambda e: e.tensor_tensor(
                out=v3(s2), in0=v3(kb_), in1=pv3(ppx, 1), op=ALU.subtract),
             reads=[kbB_, ppxB], writes=[s2B], tag=f"hs{ti}")
        P.op("scalar", lambda e: e.activation(out=s1[:, :], in_=s1[:, :], func=AF.Exp),
             reads=[s1B], writes=[s1B], tag=f"gact{ti}")
        P.op("scalar", lambda e: e.activation(out=s2[:, :], in_=s2[:, :], func=AF.Exp),
             reads=[s2B], writes=[s2B], tag=f"hact{ti}")
    def stCb(ti):
        (kb_, kbB_, vb_, vbB_, sb_, sbB_, rw_, rwB_,
         cpp, cppB, caa, caaB, cbb, cbbB) = _cenv(ti)
        P.op("gpsimd", lambda e: e.tensor_mul(out=s3[:, :], in0=s2[:, :], in1=vb_[:, :]),
             reads=[s2B, vbB_], writes=[s3B], tag=f"hv{ti}")
        P.op("vector", lambda e: e.tensor_copy(
                out=ap3(aax, 0, [[EC * (TT + 1), 128], [TT + 1, EC], [1, 1]]),
                in_=ap3(caa, 0, [[EC, 128], [1, EC], [0, 1]])),
             reads=[caaB], writes=[aaxB], tag=f"aa0_{ti}")
        P.op("vector", lambda e: e.tensor_copy(
                out=ap3(bbx, 0, [[EC * (TT + 1), 128], [TT + 1, EC], [1, 1]]),
                in_=ap3(cbb, 0, [[EC, 128], [1, EC], [0, 1]])),
             reads=[cbbB], writes=[bbxB], tag=f"bb0_{ti}")
        for c in range(EC):
            P.op("vector", lambda e, c=c: e.tensor_tensor_scan(
                    out=aax[:, c * (TT + 1) + 1: c * (TT + 1) + TT + 1],
                    data0=s1[:, c * TT:(c + 1) * TT], data1=s3[:, c * TT:(c + 1) * TT],
                    initial=caa[:, c:c + 1], op0=ALU.mult, op1=ALU.add),
                 reads=[s1B, s3B, caaB], writes=[aaxB], tag=f"aascan{ti}_{c}")
            P.op("vector", lambda e, c=c: e.tensor_tensor_scan(
                    out=bbx[:, c * (TT + 1) + 1: c * (TT + 1) + TT + 1],
                    data0=s1[:, c * TT:(c + 1) * TT], data1=s2[:, c * TT:(c + 1) * TT],
                    initial=cbb[:, c:c + 1], op0=ALU.mult, op1=ALU.add),
                 reads=[s1B, s2B, cbbB], writes=[bbxB], tag=f"bbscan{ti}_{c}")
        for nm, t_, tB_, cb_, cbB2 in (("pp", ppx, ppxB, cpp, cppB),
                                       ("aa", aax, aaxB, caa, caaB),
                                       ("bb", bbx, bbxB, cbb, cbbB)):
            P.op("vector", lambda e, t_=t_, cb_=cb_: e.tensor_copy(
                    out=cb_[:, :],
                    in_=ap3(t_, TT, [[EC * (TT + 1), 128], [TT + 1, EC]])),
                 reads=[tB_], writes=[cbB2], tag=f"cu{nm}_{ti}")
        P.op("gpsimd", lambda e: e.tensor_tensor(
                out=v3(s2), in0=v3(kb_), in1=cb3(ufir), op=ALU.add),
             reads=[kbB_, ufirB], writes=[s2B], tag=f"uk{ti}")
        P.op("vector", lambda e: e.tensor_tensor(
                out=v3(s3), in0=v3(s2), in1=pv3(ppx, 0), op=ALU.max),
             reads=[s2B, ppxB], writes=[s3B], tag=f"qm{ti}")
        P.op("vector", lambda e: e.tensor_tensor(
                out=v3(s1), in0=pv3(ppx, 0), in1=v3(s3), op=ALU.subtract),
             reads=[ppxB, s3B], writes=[s1B], tag=f"asub{ti}")
        P.op("vector", lambda e: e.tensor_sub(out=s2[:, :], in0=s2[:, :], in1=s3[:, :]),
             reads=[s2B, s3B], writes=[s2B], tag=f"bsub{ti}")
        P.op("scalar", lambda e: e.activation(out=s1[:, :], in_=s1[:, :], func=AF.Exp),
             reads=[s1B], writes=[s1B], tag=f"aact{ti}")
        P.op("scalar", lambda e: e.activation(out=s2[:, :], in_=s2[:, :], func=AF.Exp),
             reads=[s2B], writes=[s2B], tag=f"bact{ti}")
    def stCc(ti):
        (kb_, kbB_, vb_, vbB_, sb_, sbB_, rw_, rwB_,
         cpp, cppB, caa, caaB, cbb, cbbB) = _cenv(ti)
        P.op("gpsimd", lambda e: e.tensor_tensor(
                out=v3(s3), in0=v3(s1), in1=pv3(bbx, 0), op=ALU.mult),
             reads=[s1B, bbxB], writes=[s3B], tag=f"den1{ti}")
        P.op("vector", lambda e: e.tensor_add(out=s3[:, :], in0=s3[:, :], in1=s2[:, :]),
             reads=[s3B, s2B], writes=[s3B], tag=f"den2{ti}")
        P.op("vector", lambda e: e.tensor_scalar_max(out=s3[:, :], in0=s3[:, :], scalar1=1e-30),
             reads=[s3B], writes=[s3B], tag=f"denc{ti}")
        P.op("scalar", lambda e: e.activation(out=s3[:, :], in_=s3[:, :], func=AF.Ln),
             reads=[s3B], writes=[s3B], tag=f"rdl{ti}")
        P.op("vector", lambda e: e.tensor_scalar_mul(out=s3[:, :], in0=s3[:, :], scalar1=-1.0),
             reads=[s3B], writes=[s3B], tag=f"rdn{ti}")
        P.op("scalar", lambda e: e.activation(out=s3[:, :], in_=s3[:, :], func=AF.Exp),
             reads=[s3B], writes=[s3B], tag=f"rde{ti}")
        P.op("vector", lambda e: e.tensor_tensor(
                out=v3(s1), in0=v3(s1), in1=pv3(aax, 0), op=ALU.mult),
             reads=[s1B, aaxB], writes=[s1B], tag=f"num1{ti}")
        P.op("gpsimd", lambda e: e.tensor_mul(out=s2[:, :], in0=s2[:, :], in1=vb_[:, :]),
             reads=[s2B, vbB_], writes=[s2B], tag=f"num2{ti}")
        P.op("gpsimd", lambda e: e.tensor_add(out=s1[:, :], in0=s1[:, :], in1=s2[:, :]),
             reads=[s1B, s2B], writes=[s1B], tag=f"num3{ti}")
        P.op("vector", lambda e: e.tensor_mul(out=s1[:, :], in0=s1[:, :], in1=s3[:, :]),
             reads=[s1B, s3B], writes=[s1B], tag=f"y{ti}")
        P.op("gpsimd", lambda e, rw_=rw_: e.tensor_mul(out=rw_[:, :], in0=s1[:, :], in1=sb_[:, :]),
             reads=[s1B, sbB_], writes=[rwB_], tag=f"rw{ti}")

    def stD(ti):
        tt, s = divmod(ti, BL)
        t0 = tt * TT
        xt, xtB = xin[ti % 2]
        rw_, rwB_ = rwb[ti % 2]
        for cc in range(EC):
            bank = 3
            col = (cc % 2) * TT
            for dc in range(EC):
                P.op("tensor", lambda e, cc=cc, dc=dc, bank=bank, col=col, rw_=rw_: e.matmul(
                        pb[bank][0][:, col:col + TT],
                        wob[:, dc * 1024 + cc * 128: dc * 1024 + cc * 128 + 128],
                        rw_[:, dc * TT:(dc + 1) * TT],
                        start=(dc == 0), stop=(dc == EC - 1)),
                     reads=[wobB, rwB_], writes=[pb[bank][1]], tag=f"mmo{ti}_{cc}_{dc}")
            if cc % 2 == 1:
                P.op("scalar", lambda e, cc=cc, bank=bank: e.copy(
                        out=xb2[:, (cc - 1) * TT:(cc + 1) * TT], in_=pb[bank][0][:, 0:2 * TT]),
                     reads=[pb[bank][1]], writes=[xb2C[cc // 2]], tag=f"evo{ti}_{cc}")
        P.op("scalar", lambda e: e.activation(out=sqx2[:, :], in_=xb2[:, :], func=AF.Square),
             reads=[xb2B], writes=[sqx2B], tag=f"aosq{ti}")
        emit_ln_stats(xb2, xb2B, sqx2, sqx2B, stat2, stat2B, muB2, muB2B,
                      rstdB2, rstdB2B, 6, 7, f"ln2_{ti}")
        P.op("vector", lambda e: e.tensor_tensor(
                out=v3(s1), in0=v3(xb2), in1=b3(muB2), op=ALU.subtract),
             reads=[xb2B, muB2B], writes=[s1B], tag=f"posub{ti}")
        P.op("vector", lambda e: e.tensor_tensor(
                out=v3(s1), in0=v3(s1), in1=b3(rstdB2), op=ALU.mult),
             reads=[s1B, rstdB2B], writes=[s1B], tag=f"pomul{ti}")
        dst = x1_h.ap()[s].rearrange("(c p) t -> p c t", p=128)[:, :, t0:t0 + TT]
        P.dma("gpsimd", lambda e, d=dst: e.dma_start(
                out=d, in_=ap3(s1, 0, [[EC * TT, 128], [TT, EC], [1, TT]]),
                accum_op=ALU.add),
              reads=[s1B], writes=[x1_subs[s * 8 + t0 // 256]], tag=f"x1_st{ti}")

    order = [tt * BL + s for tt in range(NT) for s in range(BL)]
    stA(order[0])
    stB(order[0])
    stA(order[1])
    for i, ti in enumerate(order):
        if i > 0:
            stD(order[i - 1])
        if i + 1 < NTT:
            stB(order[i + 1])
        stCa(ti)
        stCb(ti)
        stCc(ti)
        if i + 2 < NTT:
            stA(order[i + 2])
    stD(order[NTT - 1])
    A.reset(base)



SBUF_LIMIT = 229376


def _prep_shared(inputs):
    f = np.ascontiguousarray
    w = {}
    w["WkT"] = f(inputs["L0_attn_Wk"].T)
    w["WvT"] = f(inputs["L0_attn_Wv"].T)
    w["WrT"] = f(inputs["L0_attn_Wr"].T)
    w["WoT"] = f(inputs["L0_attn_Wo"].T)
    for nm, key in (("tmk", "L0_attn_time_mix_k"), ("tmv", "L0_attn_time_mix_v"),
                    ("tmr", "L0_attn_time_mix_r"), ("tdec", "L0_attn_time_decay"),
                    ("tfir", "L0_attn_time_first")):
        w[nm] = f(inputs[key].reshape(EC, 128).T)
    return w



def emit_phase2(nc, P, A, dram, pb):
    TT = 256
    NT = (BL * T) // TT
    HC = HFF // 128

    x1_h, x1_b = dram["x1"]
    x1_subs = dram["x1_subs"]
    x2_h, x2_b = dram["x2"]
    x2_subs = dram["x2_subs"]
    wfc_h, wfc_b = dram["WfcT"]
    wpr_h, wpr_b = dram["WprT"]
    lnw_h, lnw_b = dram["lnw"]
    lnb_h, lnb_b = dram["lnb"]

    base = A.mark()
    wfq, wfqB = A.alloc("p2_wfq", 128, EC * HFF, BF16)
    wpq, wpqB = A.alloc("p2_wpq", 128, HC * D, BF16)
    lnw, lnwB = A.alloc("p2_lnw", 128, HC, F32)
    lnb, lnbB = A.alloc("p2_lnb", 128, HC, F32)
    stat, statB = A.alloc("p2_stat", 1, 6 * TT, F32)
    onesr, onesrB = A.alloc("p2_onesr", 1, 128, F32)
    onesc, onescB = A.alloc("p2_onesc", 128, 1, BF16)
    lnc, lncB = A.alloc("p2_lnc", 1, 2, F32)
    thrB_t, thrB_B = A.alloc("p2_thr", 128, 2, F32)
    muB, muBB = A.alloc("p2_muB", 128, TT, F32)
    rstdB, rstdBB = A.alloc("p2_rstdB", 128, TT, F32)

    tile_base = A.mark()
    qstg = [A.alloc(f"p2_qstg{i}", 128, 4096, F32) for i in range(2)]
    sgn, sgnB = A.alloc("p2_sgn", 128, 4096, BF16)
    bstg = [A.alloc(f"p2_bstg{i}", 128, 4096, BF16) for i in range(2)]
    A.reset(tile_base)
    xin = [A.alloc(f"p2_xin{i}", 128, EC * TT, F32) for i in range(2)]
    xb, xbB = A.alloc("p2_xb", 128, EC * TT, BF16)
    sqx, sqxB = A.alloc("p2_sqx", 128, EC * TT, BF16)
    xln, xlnB = A.alloc("p2_xln", 128, EC * TT, BF16)
    hb, hbB_whole = A.alloc("p2_hb", 128, HC * TT, BF16)
    hbC = [P.buf("sbuf", hbB_whole.start + c * TT * 2,
                 hbB_whole.start + (c + 1) * TT * 2, f"p2_hb_c{c}") for c in range(HC)]
    m2 = A.mark()
    h2b, h2bB = A.alloc("p2_h2b", 128, HC * TT, BF16)
    A.reset(m2)
    xout, xoutB = A.alloc("p2_xout", 128, EC * TT, F32)
    xoutC = [P.buf("sbuf", xoutB.start + g * 2 * TT * 4,
                   xoutB.start + (g + 1) * 2 * TT * 4, f"p2_xout_g{g}") for g in range(4)]
    A.reset(max(A.mark(), m2 + HC * TT * 2))
    s1, s1B = A.alloc("p2_s1", 128, EC * TT, F32)


    P.op("vector", lambda e: e.memset(onesr[:, :], 1.0), writes=[onesrB], tag="p2ones_r")
    P.op("vector", lambda e: e.memset(onesc[:, :], 1.0), writes=[onescB], tag="p2ones_c")
    P.op("vector", lambda e: e.memset(lnc[0:1, 0:1], LN_EPS), writes=[lncB], tag="p2lne")
    P.op("vector", lambda e: e.memset(lnc[0:1, 1:2], -0.5), writes=[lncB], tag="p2lnh")
    P.dma("sync", lambda e: e.dma_start(out=lnw[:, :], in_=lnw_h.ap()),
          reads=[lnw_b], writes=[lnwB], tag="lnw_ld")
    P.dma("sync", lambda e: e.dma_start(out=lnb[:, :], in_=lnb_h.ap()),
          reads=[lnb_b], writes=[lnbB], tag="lnb_ld")

    def quant(w_h, w_b, w16_h, w16_b, ntiles, cols, dest, destB, pfx):
        for i in range(ntiles):
            bs, bsB = bstg[i % 2]
            src = w16_h.ap()[128 * i:128 * (i + 1), :]
            P.dma("sync", lambda e, bs=bs, src=src, cols=cols: e.dma_start(
                    out=bs[:, 0:cols], in_=src),
                  reads=[w16_b], writes=[bsB], tag=f"{pfx}qld{i}")
            P.op("scalar", lambda e, bs=bs, cols=cols: e.activation(
                    out=bs[:, 0:cols], in_=bs[:, 0:cols], func=AF.Abs),
                 reads=[bsB], writes=[bsB], tag=f"{pfx}qabs{i}")
            nch = cols // 512
            for j in range(nch):
                first = (i == 0 and j == 0)
                last = (i == ntiles - 1 and j == nch - 1)
                P.op("tensor", lambda e, bs=bs, j=j, first=first, last=last: e.matmul(
                        pb[4][0][0:1, 0:512], onesc[:, 0:1], bs[:, 512 * j:512 * (j + 1)],
                        start=first, stop=last),
                     reads=[bsB, onescB], writes=[pb[4][1]], tag=f"{pfx}qmm{i}_{j}")
        n = ntiles * 128 * cols
        P.op("vector", lambda e: e.tensor_reduce(out=stat[0:1, 0:1], in_=pb[4][0][0:1, 0:512],
                                                 axis=AX.X, op=ALU.add),
             reads=[pb[4][1]], writes=[statB], tag=f"{pfx}qsum")
        P.op("vector", lambda e, n=n: e.tensor_scalar(out=stat[0:1, 1:2], in0=stat[0:1, 0:1],
                                                      scalar1=0.5 / n, scalar2=0.5e-6,
                                                      op0=ALU.mult, op1=ALU.add),
             reads=[statB], writes=[statB], tag=f"{pfx}qthr")
        P.op("tensor", lambda e: e.matmul(pb[5][0][:, 0:1], onesr[0:1, :], stat[0:1, 1:2],
                                          start=True, stop=True),
             reads=[statB, onesrB], writes=[pb[5][1]], tag=f"{pfx}qbct")
        P.op("scalar", lambda e: e.copy(out=thrB_t[:, 0:1], in_=pb[5][0][:, 0:1]),
             reads=[pb[5][1]], writes=[thrB_B], tag=f"{pfx}qbce")
        for i in range(ntiles):
            qs, qsB = qstg[i % 2]
            src = w_h.ap()[128 * i:128 * (i + 1), :]
            P.dma("sync", lambda e, qs=qs, src=src, cols=cols: e.dma_start(
                    out=qs[:, 0:cols], in_=src),
                  reads=[w_b], writes=[qsB], tag=f"{pfx}tld{i}")
            P.op("scalar", lambda e, qs=qs, cols=cols: e.activation(
                    out=sgn[:, 0:cols], in_=qs[:, 0:cols], func=AF.Sign),
                 reads=[qsB], writes=[sgnB], tag=f"{pfx}tsg{i}")
            P.op("scalar", lambda e, qs=qs, cols=cols: e.activation(
                    out=qs[:, 0:cols], in_=qs[:, 0:cols], func=AF.Abs),
                 reads=[qsB], writes=[qsB], tag=f"{pfx}tab{i}")
            P.op("vector", lambda e, qs=qs, cols=cols: e.tensor_scalar(
                    out=qs[:, 0:cols], in0=qs[:, 0:cols], scalar1=thrB_t[:, 0:1],
                    scalar2=None, op0=ALU.is_ge),
                 reads=[qsB, thrB_B], writes=[qsB], tag=f"{pfx}tge{i}")
            P.op("vector", lambda e, qs=qs, i=i, cols=cols: e.tensor_mul(
                    out=dest[:, i * cols:(i + 1) * cols], in0=qs[:, 0:cols], in1=sgn[:, 0:cols]),
                 reads=[qsB, sgnB], writes=[destB], tag=f"{pfx}tmq{i}")

    wfc16_h, wfc16_b = dram["WfcB16"]
    wpr16_h, wpr16_b = dram["WprB16"]
    quant(wfc_h, wfc_b, wfc16_h, wfc16_b, EC, HFF, wfq, wfqB, "f")
    quant(wpr_h, wpr_b, wpr16_h, wpr16_b, HC, D, wpq, wpqB, "p")

    def emit_ln_stats(src_t, srcB, sq_t, sqB_, nchunks, tag, delta=None):
        srcBl = srcB if isinstance(srcB, list) else [srcB]
        if delta is not None:
            for c in range(nchunks):
                rb_ = srcBl[min(c * len(srcBl) // nchunks, len(srcBl) - 1)]
                P.op("tensor",
                     lambda e, c=c: e.matmul(
                         pb[4][0][0:1, 0:2 * TT], onesc[:, 0:1],
                         ap3(src_t, c * TT, [[nchunks * TT, 128], [delta, 2], [1, TT]]),
                         start=(c == 0), stop=(c == nchunks - 1)),
                     reads=[rb_, sqB_, onescB], writes=[pb[4][1]], tag=f"{tag}_sum")
        else:
            for c in range(nchunks):
                rb_ = srcBl[min(c * len(srcBl) // nchunks, len(srcBl) - 1)]
                P.op("tensor",
                     lambda e, c=c: e.matmul(pb[4][0][0:1, 0:TT], onesc[:, 0:1],
                                             ap3(src_t, c * TT, [[nchunks * TT, 128], [1, TT]]),
                                             start=(c == 0), stop=(c == nchunks - 1)),
                     reads=[rb_, onescB], writes=[pb[4][1]], tag=f"{tag}_sum")
            for c in range(nchunks):
                P.op("tensor",
                     lambda e, c=c: e.matmul(pb[4][0][0:1, TT:2 * TT], onesc[:, 0:1],
                                             ap3(sq_t, c * TT, [[nchunks * TT, 128], [1, TT]]),
                                             start=(c == 0), stop=(c == nchunks - 1)),
                     reads=[sqB_, onescB], writes=[pb[4][1]], tag=f"{tag}_sumsq")
        n = nchunks * 128
        P.op("vector", lambda e, n=n: e.tensor_scalar_mul(out=stat[0:1, 0:2 * TT],
                                                          in0=pb[4][0][0:1, 0:2 * TT], scalar1=1.0 / n),
             reads=[pb[4][1]], writes=[statB], tag=f"{tag}_mu")
        P.op("vector", lambda e: e.tensor_mul(out=stat[0:1, 2 * TT:3 * TT],
                                              in0=stat[0:1, 0:TT], in1=stat[0:1, 0:TT]),
             reads=[statB], writes=[statB], tag=f"{tag}_musq")
        P.op("vector", lambda e: e.tensor_sub(out=stat[0:1, 2 * TT:3 * TT],
                                              in0=stat[0:1, TT:2 * TT], in1=stat[0:1, 2 * TT:3 * TT]),
             reads=[statB], writes=[statB], tag=f"{tag}_var")
        P.op("scalar", lambda e: e.activation(out=stat[0:1, 2 * TT:3 * TT],
                                              in_=stat[0:1, 2 * TT:3 * TT], func=AF.Ln,
                                              bias=lnc[0:1, 0:1]),
             reads=[statB, lncB], writes=[statB], tag=f"{tag}_lnv")
        P.op("scalar", lambda e: e.activation(out=stat[0:1, TT:2 * TT],
                                              in_=stat[0:1, 2 * TT:3 * TT], func=AF.Exp,
                                              scale=lnc[0:1, 1:2]),
             reads=[statB, lncB], writes=[statB], tag=f"{tag}_rstd")
        P.op("tensor", lambda e: e.matmul(pb[5][0][:, 0:2 * TT], onesr[0:1, :],
                                          stat[0:1, 0:2 * TT], start=True, stop=True),
             reads=[statB, onesrB], writes=[pb[5][1]], tag=f"{tag}_bmu")
        P.op("scalar", lambda e: e.copy(out=muB[:, :], in_=pb[5][0][:, 0:TT]),
             reads=[pb[5][1]], writes=[muBB], tag=f"{tag}_emu")
        P.op("scalar", lambda e: e.copy(out=rstdB[:, :], in_=pb[5][0][:, TT:2 * TT]),
             reads=[pb[5][1]], writes=[rstdBB], tag=f"{tag}_ers")

    def b3(t, nch, off=0):
        return ap3(t, off, [[TT, 128], [0, nch], [1, TT]])

    def v3(t, nch, off=0):
        return ap3(t, off, [[nch * TT, 128], [TT, nch], [1, TT]])

    def stA2(ti):
        s, tt = divmod(ti, NT // BL)
        t0 = tt * TT
        xt, xtB = xin[ti % 2]
        src = x1_h.ap()[s].rearrange("(c p) t -> p c t", p=128)[:, :, t0:t0 + TT]
        P.dma("sync", lambda e, d=xt, sr=src: e.dma_start(
                out=ap3(d, 0, [[EC * TT, 128], [TT, EC], [1, TT]]), in_=sr),
              reads=[x1_subs[s * 8 + t0 // 256]], writes=[xtB], tag=f"p2xld{ti}")
        P.op("scalar", lambda e, xt=xt: e.copy(out=xb[:, :], in_=xt[:, :]),
             reads=[xtB], writes=[xbB], tag=f"p2xb{ti}")
        P.op("scalar", lambda e: e.activation(out=sqx[:, :], in_=xb[:, :], func=AF.Square),
             reads=[xbB], writes=[sqxB], tag=f"p2sq{ti}")
        emit_ln_stats(xb, xbB, sqx, sqxB, EC, f"p2l1_{ti}")
        P.op("vector", lambda e, xt=xt: e.tensor_tensor(out=v3(s1, EC), in0=v3(xt, EC),
                                                        in1=b3(muB, EC), op=ALU.subtract),
             reads=[xtB, muBB], writes=[s1B], tag=f"p2ls{ti}")
        P.op("vector", lambda e: e.tensor_tensor(out=v3(xln, EC), in0=v3(s1, EC),
                                                 in1=b3(rstdB, EC), op=ALU.mult),
             reads=[s1B, rstdBB], writes=[xlnB], tag=f"p2lm{ti}")

    def stF2(ti):
        for hc in range(HC):
            bank = (hc // 2) % 2
            col = (hc % 2) * TT
            for ec in range(EC):
                P.op("tensor", lambda e, hc=hc, ec=ec, bank=bank, col=col: e.matmul(
                        pb[bank][0][:, col:col + TT],
                        wfq[:, ec * HFF + hc * 128: ec * HFF + hc * 128 + 128],
                        xln[:, ec * TT:(ec + 1) * TT],
                        start=(ec == 0), stop=(ec == EC - 1)),
                     reads=[wfqB, xlnB], writes=[pb[bank][1]], tag=f"p2fc{ti}_{hc}_{ec}")
            if hc % 2 == 1:
                P.op("scalar", lambda e, hc=hc, bank=bank: e.activation(
                        out=hb[:, (hc - 1) * TT:(hc + 1) * TT], in_=pb[bank][0][:, 0:2 * TT],
                        func=AF.Gelu),
                     reads=[pb[bank][1]], writes=[hbC[hc - 1], hbC[hc]], tag=f"p2ge{ti}_{hc}")

    def stH2(ti):
        P.op("gpsimd", lambda e: e.tensor_mul(out=h2b[:, :], in0=hb[:, :], in1=hb[:, :]),
             reads=list(hbC), writes=[h2bB], tag=f"p2hsq{ti}")
        emit_ln_stats(hb, hbC, h2b, h2bB, HC, f"p2l2_{ti}")
        for g in range(4):
            off = g * 8 * TT
            P.op("vector", lambda e, off=off: e.tensor_tensor(
                    out=v3(s1, EC),
                    in0=ap3(hb, off, [[HC * TT, 128], [TT, 8], [1, TT]]),
                    in1=b3(muB, EC), op=ALU.subtract),
                 reads=hbC[8 * g:8 * g + 8] + [muBB], writes=[s1B], tag=f"p2hs{ti}_{g}")
            P.op("vector", lambda e: e.tensor_tensor(
                    out=v3(s1, EC), in0=v3(s1, EC), in1=b3(rstdB, EC), op=ALU.mult),
                 reads=[s1B, rstdBB], writes=[s1B], tag=f"p2hm{ti}_{g}")
            for j in range(8):
                hc = g * 8 + j
                P.op("vector", lambda e, hc=hc, j=j: e.tensor_scalar(
                        out=hb[:, hc * TT:(hc + 1) * TT], in0=s1[:, j * TT:(j + 1) * TT],
                        scalar1=lnw[:, hc:hc + 1], scalar2=lnb[:, hc:hc + 1],
                        op0=ALU.mult, op1=ALU.add),
                     reads=[s1B, lnwB, lnbB], writes=[hbC[hc]], tag=f"p2ha{ti}_{hc}")

    def stC2(ti):
        s, tt = divmod(ti, NT // BL)
        t0 = tt * TT
        xt, xtB = xin[ti % 2]
        for dc in range(EC):
            bank = 2 + (dc // 2) % 2
            col = (dc % 2) * TT
            for hc in range(HC):
                P.op("tensor", lambda e, dc=dc, hc=hc, bank=bank, col=col: e.matmul(
                        pb[bank][0][:, col:col + TT],
                        wpq[:, hc * D + dc * 128: hc * D + dc * 128 + 128],
                        hb[:, hc * TT:(hc + 1) * TT],
                        start=(hc == 0), stop=(hc == HC - 1)),
                     reads=[wpqB, hbC[hc]], writes=[pb[bank][1]], tag=f"p2pj{ti}_{dc}_{hc}")
            if dc % 2 == 1:
                P.op("scalar", lambda e, dc=dc, bank=bank: e.copy(
                        out=xout[:, (dc - 1) * TT:(dc + 1) * TT], in_=pb[bank][0][:, 0:2 * TT]),
                     reads=[pb[bank][1]], writes=[xoutC[dc // 2]], tag=f"p2pe{ti}_{dc}")
        P.op("scalar", lambda e: e.copy(out=xb[:, :], in_=xout[:, :]),
             reads=list(xoutC), writes=[xbB], tag=f"p2ob{ti}")
        P.op("scalar", lambda e: e.activation(out=sqx[:, :], in_=xb[:, :], func=AF.Square),
             reads=[xbB], writes=[sqxB], tag=f"p2osq{ti}")
        emit_ln_stats(xb, xbB, sqx, sqxB, EC, f"p2l3_{ti}")
        P.op("vector", lambda e: e.tensor_tensor(out=v3(s1, EC), in0=v3(xout, EC),
                                                 in1=b3(muB, EC), op=ALU.subtract),
             reads=xoutC + [muBB], writes=[s1B], tag=f"p2os{ti}")
        P.op("vector", lambda e: e.tensor_tensor(out=v3(s1, EC), in0=v3(s1, EC),
                                                 in1=b3(rstdB, EC), op=ALU.mult),
             reads=[s1B, rstdBB], writes=[s1B], tag=f"p2om{ti}")
        P.op("vector", lambda e, xt=xt: e.tensor_add(out=xt[:, :], in0=xt[:, :], in1=s1[:, :]),
             reads=[s1B, xtB], writes=[xtB], tag=f"p2rs{ti}")
        dst = x2_h.ap()[s].rearrange("(c p) t -> p c t", p=128)[:, :, t0:t0 + TT]
        P.dma("gpsimd", lambda e, xt=xt, d=dst: e.dma_start(
                out=d, in_=ap3(xt, 0, [[EC * TT, 128], [TT, EC], [1, TT]])),
              reads=[xtB], writes=[x2_subs[s * 8 + t0 // 256]], tag=f"p2st{ti}")

    stA2(0)
    stF2(0)
    stA2(1)
    for ti in range(NT):
        stH2(ti)
        stC2(ti)
        if ti + 1 < NT:
            stF2(ti + 1)
        if ti + 2 < NT:
            stA2(ti + 2)
    A.reset(base)



def emit_phase3(nc, P, A, dram, pb):
    TT = 512
    NT = (BL * T) // TT
    NPR = H // 2

    x2_h, x2_b = dram["x2"]
    x2_subs = dram["x2_subs"]
    qh_h, qh_b = dram["qh"]
    qh_subs = dram["qh_subs"]
    kh_h, kh_b = dram["kh"]
    kh_subs = dram["kh_subs"]
    vh_h, vh_b = dram["vh"]
    vh_subs = dram["vh_subs"]
    wq_h, wq_b = dram["wqr"]
    wk_h, wk_b = dram["wkr"]
    wv_h, wv_b = dram["wvr"]

    base = A.mark()
    wqb, wqbB = A.alloc("p3_wq", 128, EC * D, BF16)
    wkb, wkbB = A.alloc("p3_wk", 128, EC * D, BF16)
    wvb, wvbB = A.alloc("p3_wv", 128, EC * D, BF16)
    xin = [A.alloc(f"p3_xin{i}", 128, EC * TT, F32) for i in range(2)]
    xb, xbB = A.alloc("p3_xb", 128, EC * TT, BF16)
    sqx, sqxB = A.alloc("p3_sqx", 128, EC * TT, BF16)
    xln, xlnB = A.alloc("p3_xln", 128, EC * TT, BF16)
    s1, s1B = A.alloc("p3_s1", 128, EC * TT, F32)
    qstg, qstgB = A.alloc("p3_qstg", 128, NPR * TT, BF16)
    kstg, kstgB = A.alloc("p3_kstg", 128, NPR * TT, BF16)
    vstg, vstgB = A.alloc("p3_vstg", 128, 4 * H * 65, BF16)
    qstgC = [P.buf("sbuf", qstgB.start + p * TT * 2, qstgB.start + (p + 1) * TT * 2,
                   f"p3_qstg_c{p}") for p in range(NPR)]
    kstgC = [P.buf("sbuf", kstgB.start + p * TT * 2, kstgB.start + (p + 1) * TT * 2,
                   f"p3_kstg_c{p}") for p in range(NPR)]
    vstgC = [P.buf("sbuf", vstgB.start + w * 520 * 2, vstgB.start + (w + 1) * 520 * 2,
                   f"p3_vstg_c{w}") for w in range(8)]
    muB, muBB = A.alloc("p3_muB", 128, TT, F32)
    rstdB, rstdBB = A.alloc("p3_rstdB", 128, TT, F32)
    stat, statB = A.alloc("p3_stat", 1, 6 * TT, F32)
    onesr, onesrB = A.alloc("p3_onesr", 1, 128, F32)
    onesc, onescB = A.alloc("p3_onesc", 128, 1, BF16)
    lnc, lncB = A.alloc("p3_lnc", 1, 2, F32)


    P.op("vector", lambda e: e.memset(onesr[:, :], 1.0), writes=[onesrB], tag="p3or")
    P.op("vector", lambda e: e.memset(onesc[:, :], 1.0), writes=[onescB], tag="p3oc")
    P.op("vector", lambda e: e.memset(lnc[0:1, 0:1], LN_EPS), writes=[lncB], tag="p3lne")
    P.op("vector", lambda e: e.memset(lnc[0:1, 1:2], -0.5), writes=[lncB], tag="p3lnh")
    P.op("vector", lambda e: e.memset(
            ap3(vstg, 64, [[4 * H * 65, 128], [1040, 4], [65, H], [1, 1]]), 1.0),
         writes=[vstgB], tag="p3vones")
    for (hb_, hbuf), (sb, sbuf) in (((wq_h, wq_b), (wqb, wqbB)),
                                    ((wk_h, wk_b), (wkb, wkbB)),
                                    ((wv_h, wv_b), (wvb, wvbB))):
        src = hb_.ap().rearrange("(c p) d -> p c d", p=128)
        P.dma("gpsimd", lambda e, d=sb, s=src: e.dma_start(
                out=ap3(d, 0, [[EC * D, 128], [D, EC], [1, D]]), in_=s),
              reads=[hbuf], writes=[sbuf], tag="p3wld")

    def emit_ln_stats(src_t, srcB, sq_t, sqB_, tag):
        for c in range(EC):
            P.op("tensor", lambda e, c=c: e.matmul(pb[4][0][0:1, 0:TT], onesc[:, 0:1],
                                                   ap3(src_t, c * TT, [[EC * TT, 128], [1, TT]]),
                                                   start=(c == 0), stop=(c == EC - 1)),
                 reads=[srcB, onescB], writes=[pb[4][1]], tag=f"{tag}_sum")
        for c in range(EC):
            P.op("tensor", lambda e, c=c: e.matmul(pb[5][0][0:1, 0:TT], onesc[:, 0:1],
                                                   ap3(sq_t, c * TT, [[EC * TT, 128], [1, TT]]),
                                                   start=(c == 0), stop=(c == EC - 1)),
                 reads=[sqB_, onescB], writes=[pb[5][1]], tag=f"{tag}_ssq")
        n = EC * 128
        P.op("vector", lambda e: e.tensor_scalar_mul(out=stat[0:1, 0:TT],
                                                     in0=pb[4][0][0:1, 0:TT], scalar1=1.0 / n),
             reads=[pb[4][1]], writes=[statB], tag=f"{tag}_mu")
        P.op("vector", lambda e: e.tensor_scalar_mul(out=stat[0:1, TT:2 * TT],
                                                     in0=pb[5][0][0:1, 0:TT], scalar1=1.0 / n),
             reads=[pb[5][1]], writes=[statB], tag=f"{tag}_ex2")
        P.op("vector", lambda e: e.tensor_mul(out=stat[0:1, 2 * TT:3 * TT],
                                              in0=stat[0:1, 0:TT], in1=stat[0:1, 0:TT]),
             reads=[statB], writes=[statB], tag=f"{tag}_musq")
        P.op("vector", lambda e: e.tensor_sub(out=stat[0:1, 3 * TT:4 * TT],
                                              in0=stat[0:1, TT:2 * TT], in1=stat[0:1, 2 * TT:3 * TT]),
             reads=[statB], writes=[statB], tag=f"{tag}_var")
        P.op("scalar", lambda e: e.activation(out=stat[0:1, 3 * TT:4 * TT],
                                              in_=stat[0:1, 3 * TT:4 * TT], func=AF.Ln,
                                              bias=lnc[0:1, 0:1]),
             reads=[statB, lncB], writes=[statB], tag=f"{tag}_lnv")
        P.op("scalar", lambda e: e.activation(out=stat[0:1, 5 * TT:6 * TT],
                                              in_=stat[0:1, 3 * TT:4 * TT], func=AF.Exp,
                                              scale=lnc[0:1, 1:2]),
             reads=[statB, lncB], writes=[statB], tag=f"{tag}_rstd")
        P.op("tensor", lambda e: e.matmul(pb[6][0][:, 0:TT], onesr[0:1, :],
                                          stat[0:1, 0:TT], start=True, stop=True),
             reads=[statB, onesrB], writes=[pb[6][1]], tag=f"{tag}_bmu")
        P.op("tensor", lambda e: e.matmul(pb[7][0][:, 0:TT], onesr[0:1, :],
                                          stat[0:1, 5 * TT:6 * TT], start=True, stop=True),
             reads=[statB, onesrB], writes=[pb[7][1]], tag=f"{tag}_brs")
        P.op("scalar", lambda e: e.copy(out=muB[:, :], in_=pb[6][0][:, 0:TT]),
             reads=[pb[6][1]], writes=[muBB], tag=f"{tag}_emu")
        P.op("scalar", lambda e: e.copy(out=rstdB[:, :], in_=pb[7][0][:, 0:TT]),
             reads=[pb[7][1]], writes=[rstdBB], tag=f"{tag}_ers")

    for ti in range(NT):
        s, tt = divmod(ti, NT // BL)
        t0 = tt * TT
        xt, xtB = xin[ti % 2]
        src = x2_h.ap()[s].rearrange("(c p) t -> p c t", p=128)[:, :, t0:t0 + TT]
        P.dma("sync", lambda e, d=xt, sr=src: e.dma_start(
                out=ap3(d, 0, [[EC * TT, 128], [TT, EC], [1, TT]]), in_=sr),
              reads=[x2_subs[s * 8 + t0 // 256], x2_subs[s * 8 + t0 // 256 + 1]],
              writes=[xtB], tag=f"p3xld{ti}")
        P.op("scalar", lambda e, xt=xt: e.copy(out=xb[:, :], in_=xt[:, :]),
             reads=[xtB], writes=[xbB], tag=f"p3xb{ti}")
        P.op("scalar", lambda e: e.activation(out=sqx[:, :], in_=xb[:, :], func=AF.Square),
             reads=[xbB], writes=[sqxB], tag=f"p3sq{ti}")
        emit_ln_stats(xb, xbB, sqx, sqxB, f"p3l_{ti}")
        P.op("vector", lambda e, xt=xt: e.tensor_tensor(
                out=ap3(s1, 0, [[EC * TT, 128], [TT, EC], [1, TT]]),
                in0=ap3(xt, 0, [[EC * TT, 128], [TT, EC], [1, TT]]),
                in1=ap3(muB, 0, [[TT, 128], [0, EC], [1, TT]]), op=ALU.subtract),
             reads=[xtB, muBB], writes=[s1B], tag=f"p3ls{ti}")
        P.op("vector", lambda e: e.tensor_tensor(
                out=ap3(xln, 0, [[EC * TT, 128], [TT, EC], [1, TT]]),
                in0=ap3(s1, 0, [[EC * TT, 128], [TT, EC], [1, TT]]),
                in1=ap3(rstdB, 0, [[TT, 128], [0, EC], [1, TT]]), op=ALU.mult),
             reads=[s1B, rstdBB], writes=[xlnB], tag=f"p3lm{ti}")
        for pr in range(NPR):
            bank = pr % 2
            for ec in range(EC):
                P.op("tensor", lambda e, pr=pr, ec=ec, bank=bank: e.matmul(
                        pb[bank][0][:, 0:TT],
                        wqb[:, ec * D + pr * 128: ec * D + pr * 128 + 128],
                        xln[:, ec * TT:(ec + 1) * TT],
                        start=(ec == 0), stop=(ec == EC - 1)),
                     reads=[wqbB, xlnB], writes=[pb[bank][1]], tag=f"p3q{ti}_{pr}_{ec}")
            P.op("vector", lambda e, pr=pr, bank=bank: e.tensor_scalar_mul(
                    out=qstg[:, pr * TT:(pr + 1) * TT], in0=pb[bank][0][:, 0:TT],
                    scalar1=0.125),
                 reads=[pb[bank][1]], writes=[qstgC[pr]], tag=f"p3qe{ti}_{pr}")
            bank = 2 + pr % 2
            for ec in range(EC):
                P.op("tensor", lambda e, pr=pr, ec=ec, bank=bank: e.matmul(
                        pb[bank][0][:, 0:TT],
                        wkb[:, ec * D + pr * 128: ec * D + pr * 128 + 128],
                        xln[:, ec * TT:(ec + 1) * TT],
                        start=(ec == 0), stop=(ec == EC - 1)),
                     reads=[wkbB, xlnB], writes=[pb[bank][1]], tag=f"p3k{ti}_{pr}_{ec}")
            P.op("scalar", lambda e, pr=pr, bank=bank: e.copy(
                    out=kstg[:, pr * TT:(pr + 1) * TT], in_=pb[bank][0][:, 0:TT]),
                 reads=[pb[bank][1]], writes=[kstgC[pr]], tag=f"p3ke{ti}_{pr}")
        for j in range(TT // 128):
            for half in range(2):
                bank = 2 * half
                for ec in range(EC):
                    P.op("tensor", lambda e, j=j, half=half, ec=ec, bank=bank: e.matmul(
                            pb[bank][0][:, 0:512],
                            xln[:, ec * TT + j * 128: ec * TT + j * 128 + 128],
                            wvb[:, ec * D + half * 512: ec * D + half * 512 + 512],
                            start=(ec == 0), stop=(ec == EC - 1)),
                         reads=[wvbB, xlnB], writes=[pb[bank][1]], tag=f"p3v{ti}_{j}_{half}_{ec}")
                P.op("vector", lambda e, j=j, half=half, bank=bank: e.tensor_copy(
                        out=ap3(vstg, j * 1040 + half * 8 * 65, [[4 * H * 65, 128], [65, 8], [1, 64]]),
                        in_=ap3(pb[bank][0], 0, [[512, 128], [64, 8], [1, 64]])),
                     reads=[pb[bank][1]], writes=[vstgC[2 * j + half]], tag=f"p3ve{ti}_{j}_{half}")
        qdst = qh_h.ap()[s].rearrange("(h2 hh) d t -> (hh d) h2 t", hh=2)[:, :, t0:t0 + TT]
        P.dma("gpsimd", lambda e, d=qdst: e.dma_start(
                out=d, in_=ap3(qstg, 0, [[NPR * TT, 128], [TT, NPR], [1, TT]])),
              reads=[qstgB], writes=[qh_subs[s * 4 + t0 // 512]], tag=f"p3qs{ti}")
        kdst = kh_h.ap()[s].rearrange("(h2 hh) d t -> (hh d) h2 t", hh=2)[:, :, t0:t0 + TT]
        P.dma("gpsimd", lambda e, d=kdst: e.dma_start(
                out=d, in_=ap3(kstg, 0, [[NPR * TT, 128], [TT, NPR], [1, TT]])),
              reads=[kstgB], writes=[kh_subs[s * 4 + t0 // 512]], tag=f"p3ks{ti}")
        vdst = vh_h.ap()[s].rearrange("j p c -> p j c")[:, 4 * tt:4 * tt + 4, :]
        P.dma("gpsimd", lambda e, d=vdst: e.dma_start(
                out=d, in_=ap3(vstg, 0, [[4 * H * 65, 128], [1040, 4], [1, 1040]])),
              reads=[vstgB], writes=[vh_subs[s * 4 + t0 // 512]], tag=f"p3vs{ti}")
    A.reset(base)



def emit_phase4(nc, P, A, dram, pb):
    QB = 512
    NQ = T // QB
    x2_h, x2_b = dram["x2"]
    x2_subs = dram["x2_subs"]
    qh_h, qh_b = dram["qh"]
    qh_subs = dram["qh_subs"]
    kh_h, kh_b = dram["kh"]
    kh_subs = dram["kh_subs"]
    vh_h, vh_b = dram["vh"]
    vh_subs = dram["vh_subs"]
    out_h, out_b = dram["out"]
    wo_h, wo_b = dram["woT"]
    bw_h, bw_b = dram["bwT"]
    bb_h, bb_b = dram["bb"]
    la_h, la_b = dram["loraA"]
    lb_h, lb_b = dram["loraB"]
    rw_h, rw_b = dram["rwT"]
    rb_h, rb_b = dram["rb"]

    base = A.mark()
    wob, wobB = A.alloc("p4_wo", 128, EC * D, BF16)
    bwb, bwbB = A.alloc("p4_bw", 128, EC * D, BF16)
    bbt, bbtB = A.alloc("p4_bb", 128, EC, F32)
    lab, labB = A.alloc("p4_la", 128, EC * 8, BF16)
    lbb, lbbB = A.alloc("p4_lb", 64, D, BF16)
    rwt, rwtB = A.alloc("p4_rw", 128, EC * 8, BF16)
    rbt, rbtB = A.alloc("p4_rb", 8, 1, F32)
    vse, vseB = A.alloc("p4_vse", 128, (T // 128) * H * 65, BF16)
    khw = [A.alloc(f"p4_khw{i}", 65, T, BF16) for i in range(3)]
    qa, qaB_whole = A.alloc("p4_qa", 65, H * QB, BF16)
    qaB = [P.buf("sbuf", qaB_whole.start + h * QB * 2,
                 qaB_whole.start + (h + 1) * QB * 2, f"p4_qa_h{h}") for h in range(H)]
    x2t, x2tB = A.alloc("p4_x2t", 128, EC * QB, F32)
    ptb = [A.alloc(f"p4_ptb{i}", 128, QB, BF16) for i in range(8)]
    otb, otbB = A.alloc("p4_otb", 128, EC * QB, BF16)
    m4 = A.mark()
    aout, aoutB = A.alloc("p4_aout", 128, EC * QB, F32)
    A.reset(m4)
    fout, foutB = A.alloc("p4_fout", 128, EC * QB, F32)
    x3ln, x3lnB = A.alloc("p4_x3ln", 128, EC * QB, BF16)
    xb, xbB = A.alloc("p4_xb", 128, EC * QB, BF16)
    sqx, sqxB = A.alloc("p4_sqx", 128, EC * QB, BF16)
    s1, s1B = A.alloc("p4_s1", 128, EC * QB // 2, F32)
    muB, muBB = A.alloc("p4_muB", 128, QB, F32)
    rstdB, rstdBB = A.alloc("p4_rstdB", 128, QB, F32)
    stat, statB = A.alloc("p4_stat", 1, 3 * QB, F32)
    onesr, onesrB = A.alloc("p4_onesr", 1, 128, F32)
    onesrb, onesrbB = A.alloc("p4_onesrb", 1, 64, BF16)
    onesc, onescB = A.alloc("p4_onesc", 128, 1, BF16)
    lnc, lncB = A.alloc("p4_lnc", 1, 2, F32)
    rbc = [A.alloc(f"p4_rbc{i}", 64, QB, F32) for i in range(2)]
    srow = [A.alloc(f"p4_srow{i}", 1, QB, BF16) for i in range(2)]
    _kq = A.alloc("p4_ksqb", 64, 512, BF16)
    ksqb = [_kq, _kq]
    qsq4 = A.alloc("p4_qsq4", 64, 4 * QB, BF16)
    mkrow = [A.alloc(f"p4_mkrow{i}", 1, 8, F32) for i in range(2)]
    mks, mksB = A.alloc("p4_mks", 1, 2 * H, F32)
    ones64, ones64B = A.alloc("p4_ones64", 64, 1, BF16)
    i128, i128B = A.alloc("p4_i128", 128, 128, BF16)
    selE, selEB = A.alloc("p4_selE", 8, 64, BF16)
    selR, selRB = A.alloc("p4_selR", 8, 64, BF16)
    it1, it1B = A.alloc("p4_it1", 128, 64, F32)
    itp, itpB = A.alloc("p4_itp", 128, 1, F32)
    gb, gbB = A.alloc("p4_gb", 128, 4 * 8, BF16)
    gsum, gsumB = A.alloc("p4_gsum", 128, 2, F32)
    gex, gexB = A.alloc("p4_gex", 128, 8, F32)
    rlb, rlbB = A.alloc("p4_rlb", 8, QB, BF16)
    gTt, gTtB = A.alloc("p4_gT", 8, QB, BF16)
    mb, mbB = A.alloc("p4_mb", 8, QB, BF16)
    geb, gebB = A.alloc("p4_geb", 64, QB, BF16)
    meb, mebB = A.alloc("p4_meb", 64, QB, BF16)
    p64, p64B = A.alloc("p4_p64", 64, QB, BF16)


    P.op("vector", lambda e: e.memset(onesr[:, :], 1.0), writes=[onesrB], tag="p4or")
    P.op("vector", lambda e: e.memset(onesrb[:, :], 1.0), writes=[onesrbB], tag="p4orb")
    P.op("vector", lambda e: e.memset(ones64[:, :], 1.0), writes=[ones64B], tag="p4o64")
    P.op("vector", lambda e: e.memset(onesc[:, :], 1.0), writes=[onescB], tag="p4oc")
    P.op("vector", lambda e: e.memset(lnc[0:1, 0:1], LN_EPS), writes=[lncB], tag="p4lne")
    P.op("vector", lambda e: e.memset(lnc[0:1, 1:2], -0.5), writes=[lncB], tag="p4lnh")
    P.op("vector", lambda e: e.memset(i128[:, :], 1.0), writes=[i128B], tag="p4i1")
    P.op("gpsimd", lambda e: e.affine_select(
            out=i128[:, :], in_=i128[:, :], pattern=[[-1, 128]],
            compare_op=ALU.is_equal, fill=0.0, base=0, channel_multiplier=1),
         reads=[i128B], writes=[i128B], tag="p4i2")
    P.op("gpsimd", lambda e: e.iota(it1[0:8, :], [[1, 8], [0, 8]],
                                    channel_multiplier=0,
                                    allow_small_or_imprecise_dtypes=True),
         writes=[it1B], tag="p4it1")
    P.op("gpsimd", lambda e: e.iota(itp[0:8, :], [[0, 1]], channel_multiplier=1,
                                    allow_small_or_imprecise_dtypes=True),
         writes=[itpB], tag="p4itp")
    P.op("vector", lambda e: e.tensor_scalar(out=selE[:, :], in0=it1[0:8, :],
                                             scalar1=itp[0:8, :], scalar2=None,
                                             op0=ALU.is_equal),
         reads=[it1B, itpB], writes=[selEB], tag="p4se")
    P.op("gpsimd", lambda e: e.iota(it1[0:8, :], [[0, 8], [1, 8]],
                                    channel_multiplier=0,
                                    allow_small_or_imprecise_dtypes=True),
         writes=[it1B], tag="p4it2")
    P.op("vector", lambda e: e.tensor_scalar(out=selR[:, :], in0=it1[0:8, :],
                                             scalar1=itp[0:8, :], scalar2=None,
                                             op0=ALU.is_equal),
         reads=[it1B, itpB], writes=[selRB], tag="p4sr")
    for (hb_, hbuf), (sb, sbuf) in (((wo_h, wo_b), (wob, wobB)),
                                    ((bw_h, bw_b), (bwb, bwbB))):
        src = hb_.ap().rearrange("(c p) d -> p c d", p=128)
        P.dma("gpsimd", lambda e, d=sb, s=src: e.dma_start(
                out=ap3(d, 0, [[EC * D, 128], [D, EC], [1, D]]), in_=s),
              reads=[hbuf], writes=[sbuf], tag="p4wld")
    P.dma("sync", lambda e: e.dma_start(out=bbt[:, :], in_=bb_h.ap()),
          reads=[bb_b], writes=[bbtB], tag="p4bb")
    src = la_h.ap().rearrange("(c p) r -> p c r", p=128)
    P.dma("gpsimd", lambda e, s=src: e.dma_start(
            out=ap3(lab, 0, [[EC * 8, 128], [8, EC], [1, 8]]), in_=s),
          reads=[la_b], writes=[labB], tag="p4la")
    P.dma("gpsimd", lambda e: e.dma_start(out=lbb[:, :], in_=lb_h.ap()),
          reads=[lb_b], writes=[lbbB], tag="p4lb")
    src = rw_h.ap().rearrange("(c p) r -> p c r", p=128)
    P.dma("gpsimd", lambda e, s=src: e.dma_start(
            out=ap3(rwt, 0, [[EC * 8, 128], [8, EC], [1, 8]]), in_=s),
          reads=[rw_b], writes=[rwtB], tag="p4rw")
    P.dma("sync", lambda e: e.dma_start(out=rbt[:, :], in_=rb_h.ap()),
          reads=[rb_b], writes=[rbtB], tag="p4rb")
    for i in range(3):
        P.op("vector", lambda e, i=i: e.memset(khw[i][0][64:65, :], 1.0),
             writes=[khw[i][1]], tag=f"p4ko{i}")

    def emit_ln_stats(src_t, srcB, sq_t, sqB_, tag):
        for c in range(EC):
            P.op("tensor", lambda e, c=c: e.matmul(pb[6][0][0:1, 0:QB], onesc[:, 0:1],
                                                   ap3(src_t, c * QB, [[EC * QB, 128], [1, QB]]),
                                                   start=(c == 0), stop=(c == EC - 1)),
                 reads=[srcB, onescB], writes=[pb[6][1]], tag=f"{tag}_sum")
        for c in range(EC):
            P.op("tensor", lambda e, c=c: e.matmul(pb[7][0][0:1, 0:QB], onesc[:, 0:1],
                                                   ap3(sq_t, c * QB, [[EC * QB, 128], [1, QB]]),
                                                   start=(c == 0), stop=(c == EC - 1)),
                 reads=[sqB_, onescB], writes=[pb[7][1]], tag=f"{tag}_ssq")
        n = EC * 128
        P.op("vector", lambda e: e.tensor_scalar_mul(out=stat[0:1, 0:QB],
                                                     in0=pb[6][0][0:1, 0:QB], scalar1=1.0 / n),
             reads=[pb[6][1]], writes=[statB], tag=f"{tag}_mu")
        P.op("vector", lambda e: e.tensor_scalar_mul(out=stat[0:1, QB:2 * QB],
                                                     in0=pb[7][0][0:1, 0:QB], scalar1=1.0 / n),
             reads=[pb[7][1]], writes=[statB], tag=f"{tag}_ex2")
        P.op("vector", lambda e: e.tensor_mul(out=stat[0:1, 2 * QB:3 * QB],
                                              in0=stat[0:1, 0:QB], in1=stat[0:1, 0:QB]),
             reads=[statB], writes=[statB], tag=f"{tag}_musq")
        P.op("vector", lambda e: e.tensor_sub(out=stat[0:1, QB:2 * QB],
                                              in0=stat[0:1, QB:2 * QB], in1=stat[0:1, 2 * QB:3 * QB]),
             reads=[statB], writes=[statB], tag=f"{tag}_var")
        P.op("scalar", lambda e: e.activation(out=stat[0:1, QB:2 * QB],
                                              in_=stat[0:1, QB:2 * QB], func=AF.Ln,
                                              bias=lnc[0:1, 0:1]),
             reads=[statB, lncB], writes=[statB], tag=f"{tag}_lnv")
        P.op("scalar", lambda e: e.activation(out=stat[0:1, 2 * QB:3 * QB],
                                              in_=stat[0:1, QB:2 * QB], func=AF.Exp,
                                              scale=lnc[0:1, 1:2]),
             reads=[statB, lncB], writes=[statB], tag=f"{tag}_rstd")
        P.op("tensor", lambda e: e.matmul(pb[6][0][:, 0:QB], onesr[0:1, :],
                                          stat[0:1, 0:QB], start=True, stop=True),
             reads=[statB, onesrB], writes=[pb[6][1]], tag=f"{tag}_bmu")
        P.op("tensor", lambda e: e.matmul(pb[7][0][:, 0:QB], onesr[0:1, :],
                                          stat[0:1, 2 * QB:3 * QB], start=True, stop=True),
             reads=[statB, onesrB], writes=[pb[7][1]], tag=f"{tag}_brs")
        P.op("scalar", lambda e: e.copy(out=muB[:, :], in_=pb[6][0][:, 0:QB]),
             reads=[pb[6][1]], writes=[muBB], tag=f"{tag}_emu")
        P.op("scalar", lambda e: e.copy(out=rstdB[:, :], in_=pb[7][0][:, 0:QB]),
             reads=[pb[7][1]], writes=[rstdBB], tag=f"{tag}_ers")

    def ln_apply(src_t, srcB, dst_t, dstB, tag):
        for g in range(2):
            off = g * 4 * QB
            P.op("vector", lambda e, off=off, src_t=src_t: e.tensor_tensor(
                    out=ap3(s1, 0, [[4 * QB, 128], [QB, 4], [1, QB]]),
                    in0=ap3(src_t, off, [[EC * QB, 128], [QB, 4], [1, QB]]),
                    in1=ap3(muB, 0, [[QB, 128], [0, 4], [1, QB]]), op=ALU.subtract),
                 reads=[srcB, muBB], writes=[s1B], tag=f"{tag}_s{g}")
            P.op("vector", lambda e, off=off, dst_t=dst_t: e.tensor_tensor(
                    out=ap3(dst_t, off, [[EC * QB, 128], [QB, 4], [1, QB]]),
                    in0=ap3(s1, 0, [[4 * QB, 128], [QB, 4], [1, QB]]),
                    in1=ap3(rstdB, 0, [[QB, 128], [0, 4], [1, QB]]), op=ALU.mult),
                 reads=[s1B, rstdBB], writes=[dstB], tag=f"{tag}_m{g}")

    for s in range(BL):
        vsrc = vh_h.ap()[s].rearrange("j p c -> p j c")
        P.dma("sync", lambda e, sr=vsrc: e.dma_start(
                out=ap3(vse, 0, [[(T // 128) * H * 65, 128], [1040, T // 128], [1, 1040]]), in_=sr),
              reads=[vh_subs[s * 4 + i] for i in range(4)], writes=[vseB], tag=f"p4vld{s}")
        for qb in range(NQ):
            blk = f"b{s}_{qb}"
            qsrc = qh_h.ap()[s].rearrange("h d t -> d h t")[:, :, qb * QB:(qb + 1) * QB]
            P.dma("sync", lambda e, sr=qsrc: e.dma_start(
                    out=ap3(qa, 0, [[H * QB, 64], [QB, H], [1, QB]]), in_=sr),
                  reads=[qh_subs[s * 4 + qb]], writes=[qaB_whole], tag=f"{blk}_qld")
            xsrc = x2_h.ap()[s].rearrange("(c p) t -> p c t", p=128)[:, :, qb * QB:(qb + 1) * QB]
            P.dma("sync", lambda e, sr=xsrc: e.dma_start(
                    out=ap3(x2t, 0, [[EC * QB, 128], [QB, EC], [1, QB]]), in_=sr),
                  reads=[x2_subs[s * 8 + qb * 2], x2_subs[s * 8 + qb * 2 + 1]],
                  writes=[x2tB], tag=f"{blk}_xld")
            mext = (qb + 1) * 512
            nk128 = mext // 128

            def kload(h):
                kb, kbB = khw[h % 3]
                ksrc = kh_h.ap()[s][h][:, 0:T]
                P.dma("sync", lambda e, sr=ksrc, kb=kb: e.dma_start(
                        out=kb[0:64, 0:T], in_=sr),
                      reads=[kh_subs[s * 4 + i] for i in range(4)],
                      writes=[kbB], tag=f"{blk}_kld{h}")

            def knorm(h):
                kb, kbB = khw[h % 3]
                ks, ksB = ksqb[h % 2]
                mk, mkB = mkrow[h % 2]
                for j in range(T // 512):
                    P.op("scalar", lambda e, j=j, kb=kb, ks=ks: e.activation(
                            out=ks[:, :], in_=kb[0:64, j * 512:(j + 1) * 512],
                            func=AF.Square),
                         reads=[kbB], writes=[ksB], tag=f"{blk}_ksq{h}_{j}")
                    P.op("tensor", lambda e, ks=ks: e.matmul(
                            pb[1][0][0:1, 0:512], ones64[:, 0:1], ks[:, :],
                            start=True, stop=True),
                         reads=[ksB, ones64B], writes=[pb[1][1]], tag=f"{blk}_kmm{h}_{j}")
                    P.op("vector", lambda e, j=j, mk=mk: e.reduce_max(
                            out=mk[0:1, j:j + 1], in_=pb[1][0][0:1, 0:512], axis=AX.X),
                         reads=[pb[1][1]], writes=[mkB], tag=f"{blk}_kmx{h}_{j}")
                P.op("vector", lambda e, mk=mk, h=h: e.tensor_reduce(
                        out=mks[0:1, s * H + h: s * H + h + 1], in_=mk[0:1, 0:4],
                        axis=AX.X, op=ALU.max),
                     reads=[mkB], writes=[mksB], tag=f"{blk}_kmc{h}")

            def stHq(h4):
                qs, qsB = qsq4
                P.op("scalar", lambda e, h4=h4, qs=qs: e.activation(
                        out=qs[:, :], in_=qa[0:64, h4 * QB: (h4 + 4) * QB], func=AF.Square),
                     reads=[qaB[h4], qaB[h4 + 1], qaB[h4 + 2], qaB[h4 + 3]],
                     writes=[qsB], tag=f"{blk}_qsq{h4}")
                for h in range(h4, h4 + 4):
                    bank = h % 2
                    P.op("tensor", lambda e, qs=qs, h=h, h4=h4, bank=bank: e.matmul(
                            pb[bank][0][0:1, 0:QB], ones64[:, 0:1],
                            qs[:, (h - h4) * QB: (h - h4 + 1) * QB],
                            start=True, stop=True),
                         reads=[qsB, ones64B], writes=[pb[bank][1]], tag=f"{blk}_qmm{h}")
                    P.op("vector", lambda e, h=h, bank=bank: e.tensor_scalar_mul(
                            out=qa[64:65, h * QB: h * QB + QB],
                            in0=pb[bank][0][0:1, 0:QB],
                            scalar1=mks[0:1, s * H + h: s * H + h + 1]),
                         reads=[pb[bank][1], mksB], writes=[qaB[h]], tag=f"{blk}_qm2{h}")
                P.op("scalar", lambda e, h4=h4: e.activation(
                        out=qa[64:65, h4 * QB: (h4 + 4) * QB],
                        in_=qa[64:65, h4 * QB: (h4 + 4) * QB], func=AF.Sqrt),
                     reads=[qaB[h4], qaB[h4 + 1], qaB[h4 + 2], qaB[h4 + 3]],
                     writes=[qaB[h4], qaB[h4 + 1], qaB[h4 + 2], qaB[h4 + 3]],
                     tag=f"{blk}_qsr{h4}")
                P.op("vector", lambda e, h4=h4: e.tensor_scalar_mul(
                        out=qa[64:65, h4 * QB: (h4 + 4) * QB],
                        in0=qa[64:65, h4 * QB: (h4 + 4) * QB], scalar1=-1.0),
                     reads=[qaB[h4], qaB[h4 + 1], qaB[h4 + 2], qaB[h4 + 3]],
                     writes=[qaB[h4], qaB[h4 + 1], qaB[h4 + 2], qaB[h4 + 3]],
                     tag=f"{blk}_qng{h4}")

            def stB_st(h, k0, k1):
                kb, kbB = khw[h % 3]
                for km in range(k0, k1):
                    bank = 2 + km % 2
                    P.op("tensor", lambda e, h=h, km=km, kb=kb, bank=bank: e.matmul(
                            pb[bank][0][:, 0:QB],
                            kb[0:65, km * 128: km * 128 + 128],
                            qa[0:65, h * QB: h * QB + QB],
                            start=True, stop=True),
                         reads=[qaB[h], kbB], writes=[pb[bank][1]], tag=f"{blk}_st{h}_{km}")
                    pt, ptB = ptb[km % 8]
                    P.op("scalar", lambda e, pt=pt, bank=bank: e.activation(
                            out=pt[:, :], in_=pb[bank][0][:, 0:QB], func=AF.Exp),
                         reads=[pb[bank][1]], writes=[ptB], tag=f"{blk}_pe{h}_{km}")
                    if km >= 4 * qb:
                        P.op("gpsimd", lambda e, pt=pt, km=km, qb=qb: e.affine_select(
                                out=pt[:, :], in_=pt[:, :], pattern=[[1, QB]],
                                compare_op=ALU.is_ge, fill=0.0,
                                base=qb * 512 - km * 128, channel_multiplier=-1),
                             reads=[ptB], writes=[ptB], tag=f"{blk}_msk{h}_{km}")

            def stB_pv(h, k0, k1):
                ob = pb[4 + h % 2]
                for km in range(k0, k1):
                    pt, ptB = ptb[km % 8]
                    P.op("tensor", lambda e, h=h, km=km, pt=pt, nk=nk128, ob=ob: e.matmul(
                            ob[0][0:65, 0:QB],
                            vse[:, (km * H + h) * 65: (km * H + h) * 65 + 65],
                            pt[:, :],
                            start=(km == 0), stop=(km == nk - 1)),
                         reads=[vseB, ptB], writes=[ob[1]], tag=f"{blk}_pv{h}_{km}")

            def stB_fin(h):
                ob = pb[4 + h % 2]
                rc, rcB = rbc[h % 2]
                sr_, srB_ = srow[h % 2]
                def _cast(e, ob=ob, sr_=sr_):
                    with nc.allow_low_precision(reason="bf16 softmax normalizer"):
                        return e.copy(out=sr_[0:1, 0:QB], in_=ob[0][64:65, 0:QB])
                P.op("scalar", _cast, reads=[ob[1]], writes=[srB_], tag=f"{blk}_rs{h}")
                P.op("tensor", lambda e, sr_=sr_: e.matmul(
                        pb[7][0][0:64, 0:QB], onesrb[0:1, :], sr_[0:1, 0:QB],
                        start=True, stop=True),
                     reads=[srB_, onesrbB], writes=[pb[7][1]], tag=f"{blk}_rb{h}")
                P.op("scalar", lambda e, rc=rc: e.activation(out=rc[:, :],
                                                             in_=pb[7][0][0:64, 0:QB],
                                                             func=AF.Ln),
                     reads=[pb[7][1]], writes=[rcB], tag=f"{blk}_rbe{h}")
                P.op("vector", lambda e, rc=rc: e.tensor_scalar_mul(
                        out=rc[:, :], in0=rc[:, :], scalar1=-1.0),
                     reads=[rcB], writes=[rcB], tag=f"{blk}_rbn{h}")
                P.op("scalar", lambda e, rc=rc: e.activation(out=rc[:, :], in_=rc[:, :],
                                                             func=AF.Exp),
                     reads=[rcB], writes=[rcB], tag=f"{blk}_rbx{h}")
                P.op("vector", lambda e, h=h, ob=ob, rc=rc: e.tensor_tensor(
                        out=otb[64 * (h % 2):64 * (h % 2) + 64, (h // 2) * QB:(h // 2 + 1) * QB],
                        in0=ob[0][0:64, 0:QB], in1=rc[:, :], op=ALU.mult),
                     reads=[ob[1], rcB], writes=[otbB], tag=f"{blk}_ot{h}")

            kload(0)
            kload(1)
            if qb == 0:
                for h in range(H):
                    if h + 2 < H:
                        kload(h + 2)
                    knorm(h)
                kload(0)
                kload(1)
            for h4 in range(0, H, 4):
                stHq(h4)
            for h in range(H):
                if h + 2 < H:
                    kload(h + 2)
                nh = min(nk128, 8)
                stB_st(h, 0, nh)
                stB_pv(h, 0, nh)
                if nk128 > 8:
                    stB_st(h, 8, nk128)
                    stB_pv(h, 8, nk128)
                if h > 0:
                    stB_fin(h - 1)
            stB_fin(H - 1)
            for ecc in range(EC):
                bank = ecc % 2
                for hdc in range(EC):
                    P.op("tensor", lambda e, ecc=ecc, hdc=hdc, bank=bank: e.matmul(
                            pb[bank][0][:, 0:QB],
                            wob[:, hdc * D + ecc * 128: hdc * D + ecc * 128 + 128],
                            otb[:, hdc * QB:(hdc + 1) * QB],
                            start=(hdc == 0), stop=(hdc == EC - 1)),
                         reads=[wobB, otbB], writes=[pb[bank][1]], tag=f"{blk}_op{ecc}_{hdc}")
                P.op("scalar", lambda e, ecc=ecc, bank=bank: e.copy(
                        out=aout[:, ecc * QB:(ecc + 1) * QB], in_=pb[bank][0][:, 0:QB]),
                     reads=[pb[bank][1]], writes=[aoutB], tag=f"{blk}_oe{ecc}")
            P.op("scalar", lambda e: e.copy(out=xb[:, :], in_=aout[:, :]),
                 reads=[aoutB], writes=[xbB], tag=f"{blk}_ab")
            P.op("scalar", lambda e: e.activation(out=sqx[:, :], in_=xb[:, :], func=AF.Square),
                 reads=[xbB], writes=[sqxB], tag=f"{blk}_asq")
            emit_ln_stats(xb, xbB, sqx, sqxB, f"{blk}_l2")
            ln_apply(aout, aoutB, aout, aoutB, f"{blk}_l2a")
            P.op("vector", lambda e: e.tensor_add(out=x2t[:, :], in0=x2t[:, :], in1=aout[:, :]),
                 reads=[aoutB, x2tB], writes=[x2tB], tag=f"{blk}_rs1")
            P.op("scalar", lambda e: e.copy(out=xb[:, :], in_=x2t[:, :]),
                 reads=[x2tB], writes=[xbB], tag=f"{blk}_fb")
            P.op("scalar", lambda e: e.activation(out=sqx[:, :], in_=xb[:, :], func=AF.Square),
                 reads=[xbB], writes=[sqxB], tag=f"{blk}_fsq")
            emit_ln_stats(xb, xbB, sqx, sqxB, f"{blk}_l3")
            ln_apply(x2t, x2tB, x3ln, x3lnB, f"{blk}_l3a")
            for ec in range(EC):
                P.op("tensor", lambda e, ec=ec: e.matmul(
                        pb[5][0][0:8, 0:QB], rwt[:, ec * 8:(ec + 1) * 8],
                        x3ln[:, ec * QB:(ec + 1) * QB],
                        start=(ec == 0), stop=(ec == EC - 1)),
                     reads=[rwtB, x3lnB], writes=[pb[5][1]], tag=f"{blk}_ro{ec}")
            P.op("vector", lambda e: e.tensor_scalar_add(out=rlb[:, :], in0=pb[5][0][0:8, 0:QB],
                                                         scalar1=rbt[:, 0:1]),
                 reads=[pb[5][1], rbtB], writes=[rlbB], tag=f"{blk}_rle")
            for j in range(4):
                P.op("tensor", lambda e, j=j: e.matmul(
                        pb[6][0][:, j * 8:(j + 1) * 8], rlb[:, j * 128:(j + 1) * 128],
                        i128[0:8, 0:8], start=True, stop=True),
                     reads=[rlbB, i128B], writes=[pb[6][1]], tag=f"{blk}_rt{j}")
            for j in range(4):
                P.op("vector", lambda e, j=j: e.reduce_max(
                        out=gsum[:, 1:2], in_=pb[6][0][:, j * 8:(j + 1) * 8],
                        axis=AX.X, negate=True),
                     reads=[pb[6][1]], writes=[gsumB], tag=f"{blk}_gm{j}")
                P.op("scalar", lambda e, j=j: e.activation(
                        out=gex[:, :], in_=pb[6][0][:, j * 8:(j + 1) * 8],
                        func=AF.Exp, bias=gsum[:, 1:2], accum_out=gsum[:, 0:1]),
                     reads=[pb[6][1], gsumB], writes=[gexB, gsumB], tag=f"{blk}_ge{j}")
                P.op("vector", lambda e: e.reciprocal(out=gsum[:, 0:1], in_=gsum[:, 0:1]),
                     reads=[gsumB], writes=[gsumB], tag=f"{blk}_gr{j}")
                P.op("vector", lambda e, j=j: e.tensor_scalar_mul(
                        out=gb[:, j * 8:(j + 1) * 8], in0=gex[:, :], scalar1=gsum[:, 0:1]),
                     reads=[gexB, gsumB], writes=[gbB], tag=f"{blk}_gn{j}")
                P.op("tensor", lambda e, j=j: e.matmul(
                        pb[7][0][0:8, j * 128:(j + 1) * 128], gb[:, j * 8:(j + 1) * 8],
                        i128[:, :], start=True, stop=True),
                     reads=[gbB, i128B], writes=[pb[7][1]], tag=f"{blk}_gt{j}")
            P.op("vector", lambda e: e.tensor_copy(out=gTt[:, :], in_=pb[7][0][0:8, 0:QB]),
                 reads=[pb[7][1]], writes=[gTtB], tag=f"{blk}_gte")
            for ec in range(EC):
                P.op("tensor", lambda e, ec=ec: e.matmul(
                        pb[5][0][0:8, 0:QB], lab[:, ec * 8:(ec + 1) * 8],
                        x3ln[:, ec * QB:(ec + 1) * QB],
                        start=(ec == 0), stop=(ec == EC - 1)),
                     reads=[labB, x3lnB], writes=[pb[5][1]], tag=f"{blk}_mi{ec}")
            P.op("vector", lambda e: e.tensor_scalar_mul(out=mb[:, :],
                                                         in0=pb[5][0][0:8, 0:QB], scalar1=2.0),
                 reads=[pb[5][1]], writes=[mbB], tag=f"{blk}_me")
            P.op("tensor", lambda e: e.matmul(pb[5][0][0:64, 0:QB], selE[:, :], gTt[:, :],
                                              start=True, stop=True),
                 reads=[selEB, gTtB], writes=[pb[5][1]], tag=f"{blk}_sge")
            P.op("scalar", lambda e: e.copy(out=geb[:, :], in_=pb[5][0][0:64, 0:QB]),
                 reads=[pb[5][1]], writes=[gebB], tag=f"{blk}_sgee")
            P.op("tensor", lambda e: e.matmul(pb[5][0][0:64, 0:QB], selR[:, :], mb[:, :],
                                              start=True, stop=True),
                 reads=[selRB, mbB], writes=[pb[5][1]], tag=f"{blk}_smr")
            P.op("scalar", lambda e: e.copy(out=meb[:, :], in_=pb[5][0][0:64, 0:QB]),
                 reads=[pb[5][1]], writes=[mebB], tag=f"{blk}_smre")
            P.op("vector", lambda e: e.tensor_mul(out=p64[:, :], in0=geb[:, :], in1=meb[:, :]),
                 reads=[gebB, mebB], writes=[p64B], tag=f"{blk}_p64")
            for oc in range(EC):
                bank = 2 + oc % 2
                for ec in range(EC):
                    P.op("tensor", lambda e, oc=oc, ec=ec, bank=bank: e.matmul(
                            pb[bank][0][:, 0:QB],
                            bwb[:, ec * D + oc * 128: ec * D + oc * 128 + 128],
                            x3ln[:, ec * QB:(ec + 1) * QB],
                            start=(ec == 0), stop=(ec == EC - 1)),
                         reads=[bwbB, x3lnB], writes=[pb[bank][1]], tag=f"{blk}_bs{oc}_{ec}")
                lbank = 4 + oc % 2
                P.op("tensor", lambda e, oc=oc, lbank=lbank: e.matmul(
                        pb[lbank][0][:, 0:QB], lbb[:, oc * 128:(oc + 1) * 128], p64[:, :],
                        start=True, stop=True),
                     reads=[lbbB, p64B], writes=[pb[lbank][1]], tag=f"{blk}_lo{oc}")
                P.op("vector", lambda e, oc=oc, bank=bank: e.tensor_scalar_add(
                        out=fout[:, oc * QB:(oc + 1) * QB], in0=pb[bank][0][:, 0:QB],
                        scalar1=bbt[:, oc:oc + 1]),
                     reads=[pb[bank][1], bbtB], writes=[foutB], tag=f"{blk}_be{oc}")
                P.op("vector", lambda e, oc=oc, lbank=lbank: e.tensor_add(
                        out=fout[:, oc * QB:(oc + 1) * QB],
                        in0=fout[:, oc * QB:(oc + 1) * QB], in1=pb[lbank][0][:, 0:QB]),
                     reads=[pb[lbank][1], foutB], writes=[foutB], tag=f"{blk}_ba{oc}")
            P.op("scalar", lambda e: e.copy(out=xb[:, :], in_=fout[:, :]),
                 reads=[foutB], writes=[xbB], tag=f"{blk}_ob")
            P.op("scalar", lambda e: e.activation(out=sqx[:, :], in_=xb[:, :], func=AF.Square),
                 reads=[xbB], writes=[sqxB], tag=f"{blk}_osq")
            emit_ln_stats(xb, xbB, sqx, sqxB, f"{blk}_l4")
            ln_apply(fout, foutB, fout, foutB, f"{blk}_l4a")
            P.op("vector", lambda e: e.tensor_add(out=x2t[:, :], in0=x2t[:, :], in1=fout[:, :]),
                 reads=[foutB, x2tB], writes=[x2tB], tag=f"{blk}_rs2")
            dst = out_h.ap()[s].rearrange("(c p) t -> p c t", p=128)[:, :, qb * QB:(qb + 1) * QB]
            P.dma("gpsimd", lambda e, d=dst: e.dma_start(
                    out=d, in_=ap3(x2t, 0, [[EC * QB, 128], [QB, EC], [1, QB]])),
                  reads=[x2tB], writes=[out_b], tag=f"{blk}_st")
    A.reset(base)


_STREAMS = [
    ("x0", (BL, D, T), F32, None, (1,)),
    ("x1", (BL, D, T), F32, 1, (2,)),
    ("x2", (BL, D, T), F32, 2, (3, 4)),
    ("qh", (BL, H, DH, T), BF16, 3, (4,)),
    ("kh", (BL, H, DH, T), BF16, 3, (4,)),
    ("vh", (BL, T // 128, 128, H * 65), BF16, 3, (4,)),
    ("out", (BL, D, T), F32, 4, ()),
]

_PHASE_WEIGHTS = {
    1: [("WkT", (D, D), F32), ("WvT", (D, D), F32), ("WrT", (D, D), F32),
        ("WoT", (D, D), F32), ("tmk", (128, EC), F32), ("tmv", (128, EC), F32),
        ("tmr", (128, EC), F32), ("tdec", (128, EC), F32), ("tfir", (128, EC), F32)],
    2: [("WfcT", (D, HFF), F32), ("WprT", (HFF, D), F32),
        ("lnw", (128, HFF // 128), F32), ("lnb", (128, HFF // 128), F32)],
    3: [("wqr", (D, D), F32), ("wkr", (D, D), F32), ("wvr", (D, D), F32)],
    4: [("woT", (D, D), F32), ("bwT", (D, D), F32), ("bb", (128, EC), F32),
        ("loraA", (D, 8), F32), ("loraB", (64, D), F32),
        ("rwT", (D, 8), F32), ("rb", (8, 1), F32)],
}

_EMITTERS = {1: emit_phase1, 2: emit_phase2, 3: emit_phase3, 4: emit_phase4}


def build_nc(phases):
    nc = bass.Bass("TRN2", target_bir_lowering=False, debug=False)
    P = Prog()
    A = Arena(nc, P, SBUF_LIMIT)
    dram = {}
    pset = set(phases)
    nbuf = [0]
    def _dbuf(name):
        nbuf[0] += 1
        return P.buf("dram", nbuf[0] - 1, nbuf[0], name)
    for name, shape, dt, prod, cons in _STREAMS:
        needed = (prod in pset) or (pset & set(cons))
        if not needed:
            continue
        if prod in pset and (set(cons) - pset or not cons):
            kind = "ExternalOutput"
        elif prod in pset:
            kind = "Internal"
        else:
            kind = "ExternalInput"
        h = nc.dram_tensor(name, list(shape), dt, kind=kind)
        if name in ("x1", "x2", "qh", "kh", "vh"):
            n = 16 if name in ("x1", "x2") else 8
            b0 = nbuf[0]
            nbuf[0] += n
            parent = P.buf("dram", b0, b0 + n, name)
            dram[name] = (h, parent)
            dram[name + "_subs"] = [
                P.buf("dram", b0 + i, b0 + i + 1, f"{name}_sub{i}") for i in range(n)]
        else:
            dram[name] = (h, _dbuf(name))
    for ph in phases:
        for name, shape, dt in _PHASE_WEIGHTS[ph]:
            h = nc.dram_tensor(name, list(shape), dt, kind="ExternalInput")
            dram[name] = (h, _dbuf(name))
    if 2 in pset:
        for nm, srcn, shape in (("WfcB16", "WfcT", (D, HFF)),
                                ("WprB16", "WprT", (HFF, D))):
            h16 = nc.dram_tensor(nm, list(shape), BF16, kind="Internal")
            b16 = _dbuf(nm)
            dram[nm] = (h16, b16)
            sh, sb = dram[srcn]
            P.dma("gpsimd", lambda e, h16=h16, sh=sh: e.dma_start(
                    out=h16.ap(), in_=sh.ap()),
                  reads=[sb], writes=[b16], tag=f"cast_{nm}")
    pb = []
    for i in range(8):
        t = nc.alloc_psum_tensor(f"pb{i}", [128, 512], F32)
        pb.append((t, P.buf("psum", i * 2048, (i + 1) * 2048, f"pb{i}")))
    for ph in phases:
        _EMITTERS[ph](nc, P, A, dram, pb)
    P.build(nc, None)
    outs = [n for n, _, _, prod, cons in _STREAMS
            if prod in pset and (set(cons) - pset or not cons)]
    ins = [n for n, _, _, prod, cons in _STREAMS
           if prod not in pset and (pset & set(cons))]
    wts = [name for ph in phases for name, _, _ in _PHASE_WEIGHTS[ph]]
    return nc, ins, wts, outs


def _prep_all(inputs):
    f = np.ascontiguousarray
    w = _prep_shared(inputs)
    w["WfcT"] = f(inputs["L0_ffn_W_fc"].T)
    w["WprT"] = f(inputs["L0_ffn_W_proj"].T)
    w["lnw"] = f(inputs["L0_ffn_ln_w"].reshape(HFF // 128, 128).T)
    w["lnb"] = f(inputs["L0_ffn_ln_b"].reshape(HFF // 128, 128).T)
    w["wqr"] = f(inputs["L1_attn_w_q"].reshape(D, D))
    w["wkr"] = f(inputs["L1_attn_w_k"].reshape(D, D))
    w["wvr"] = f(inputs["L1_attn_w_v"].reshape(D, D))
    w["woT"] = f(inputs["L1_attn_w_o"].reshape(D, D).T)
    w["bwT"] = f(inputs["L1_ffn_base_w"].T)
    w["bb"] = f(inputs["L1_ffn_base_b"].reshape(EC, 128).T)
    w["loraA"] = f(inputs["L1_ffn_lora_A"])
    w["loraB"] = f(inputs["L1_ffn_lora_B"].reshape(64, D))
    w["rwT"] = f(inputs["L1_ffn_router_w"].T)
    w["rb"] = f(inputs["L1_ffn_router_b"].reshape(8, 1))
    return w


PHASE_GROUPS = [[1, 2, 3, 4]]


def kernel(**inputs):
    w = _prep_all(inputs)
    x = np.ascontiguousarray(inputs["L0_attn_x"])

    stream_vals = [dict() for _ in range(NCORES)]
    for c in range(NCORES):
        stream_vals[c]["x0"] = np.ascontiguousarray(
            np.transpose(x[2 * c:2 * c + 2], (0, 2, 1)))

    final = None
    for grp in PHASE_GROUPS:
        nc, ins, wts, outs = build_nc(grp)
        in_maps = []
        for c in range(NCORES):
            m = {k: stream_vals[c][k] for k in ins}
            m.update({k: w[k] for k in wts})
            in_maps.append(m)
        res = run_bass_kernel_spmd(nc, in_maps, core_ids=list(range(NCORES)))
        for c in range(NCORES):
            for o in outs:
                stream_vals[c][o] = res.results[c][o]
        final = outs[-1] if outs else final

    key = "out" if any("out" in sv for sv in stream_vals) else final
    out = []
    for c in range(NCORES):
        out.append(np.transpose(stream_vals[c][key], (0, 2, 1)))
    return np.concatenate(out, axis=0).astype(np.float32)
